# version 08
# baseline (speedup 1.0000x reference)
import numpy as np
from contextlib import ExitStack

import concourse.bass as bass
import concourse.mybir as mybir
from concourse.bass_utils import run_bass_kernel_spmd

F32 = mybir.dt.float32
BF16 = mybir.dt.bfloat16
I32 = mybir.dt.int32
U32 = mybir.dt.uint32
AF = mybir.ActivationFunctionType
ALU = mybir.AluOpType
AX = mybir.AxisListType

NCORES = 8
TCORE = 2048
NTT = TCORE // 128
D = 1024
NH = 16
HD = 64

_nc_cache = {}


def core_row_slices(c):
    b, j = divmod(c, 4)
    return [(8192 * b + 1024 * j, 1024), (8192 * b + 1024 * (7 - j), 1024)]

def slice_tokens(flat, c):
    (r1, n1), (r2, n2) = core_row_slices(c)
    return np.ascontiguousarray(
        np.concatenate([flat[r1:r1 + n1], flat[r2:r2 + n2]], axis=0))

def unslice_tokens(shards):
    flat = np.empty((16384,) + shards[0].shape[1:], dtype=shards[0].dtype)
    for c in range(NCORES):
        (r1, n1), (r2, n2) = core_row_slices(c)
        flat[r1:r1 + n1] = shards[c][:n1]
        flat[r2:r2 + n2] = shards[c][n1:]
    return flat


class Buf:
    __slots__ = ("name", "ap_", "w_ev", "r_evs")
    def __init__(self, name, ap_):
        self.name = name
        self.ap_ = ap_
        self.w_ev = None
        self.r_evs = []
    def ap(self):
        return self.ap_
    def __getitem__(self, key):
        return self.ap_[key]

ENGS = ("pe", "act", "dve", "pool", "sp")

class P:
    def __init__(self):
        nc = bass.Bass("TRN2", target_bir_lowering=False, debug=False)
        self.nc = nc
        self.root_es = ExitStack()
        self.es = self.root_es
        self.eng = {"pe": nc.tensor, "act": nc.scalar, "dve": nc.vector,
                    "pool": nc.gpsimd, "sp": nc.sync}
        self.sem = {}
        self.cnt = {}
        for e in ENGS:
            self.sem[e] = self.root_es.enter_context(nc.semaphore(f"S_{e}"))
            self.cnt[e] = 0
        self.hiwater = {}
        self.dma_sems = []
        self.stage_idx = 0
        self.sem_pool = []
        self._stage_sems = []
        self._in_stage = False

    from contextlib import contextmanager as _cm

    @_cm
    def stage(self):
        saved = self.es
        self.es = ExitStack()
        saved_sems = self._stage_sems
        self._stage_sems = []
        saved_in = self._in_stage
        self._in_stage = True
        try:
            yield
        finally:
            self.es.close()
            self.es = saved
            self.stage_idx += 1
            self.barrier()
            for skey in self._stage_sems:
                self.sem_pool.append((self.sem[skey], self.cnt[skey]))
                del self.sem[skey]
                del self.cnt[skey]
                self.dma_sems.remove(skey)
            self._stage_sems = saved_sems
            self._in_stage = saved_in

    def barrier(self):
        for e in ENGS:
            for s, c in list(self.cnt.items()):
                if c > 0 and s != e:
                    self._wait(e, (s, c))

    def cc(self, kind, op, groups, in_buf, in_ap, out_buf, out_ap):
        for ev in self._deps([in_buf], [out_buf]):
            self._wait("pool", ev)
        inst = self.eng["pool"].collective_compute(
            kind, op, replica_groups=groups, ins=[in_ap.opt()],
            outs=[out_ap.opt()])
        self.cnt["pool"] += 1
        inst.then_inc(self.sem["pool"], 1)
        ev = ("pool", self.cnt["pool"])
        in_buf.r_evs.append(ev)
        out_buf.w_ev = ev
        out_buf.r_evs = []
        return ev

    def sval(self, eng_name, buf, ap):
        for ev in self._deps([buf], []):
            self._wait(eng_name, ev)
        return self.eng[eng_name].value_load(ap)

    def sb(self, name, shape, dtype):
        h = self.es.enter_context(self.nc.sbuf_tensor(
            f"{name}_s{self.stage_idx}", list(shape), dtype))
        return Buf(name, h[:])

    def psum(self, name, shape, dtype=F32):
        h = self.es.enter_context(self.nc.psum_tensor(
            f"{name}_s{self.stage_idx}", list(shape), dtype))
        return Buf(name, h[:])

    def dram_in(self, name, shape, dtype):
        if not hasattr(self, "_dins"):
            self._dins = {}
        if name in self._dins:
            return self._dins[name]
        h = self.nc.dram_tensor(name, list(shape), dtype, kind="ExternalInput")
        b = Buf(name, h.ap())
        self._dins[name] = b
        return b

    def dram_out(self, name, shape, dtype):
        h = self.nc.dram_tensor(name, list(shape), dtype, kind="ExternalOutput")
        return Buf(name, h.ap())

    def dram_tmp(self, name, shape, dtype):
        h = self.nc.dram_tensor(name, list(shape), dtype, kind="Internal")
        return Buf(name, h.ap())

    def _wait(self, eng_name, ev):
        if ev is None:
            return
        sname, count = ev
        if sname == eng_name == "pe":
            return
        key = (eng_name, sname)
        if self.hiwater.get(key, -1) >= count:
            return
        self.eng[eng_name].wait_ge(self.sem[sname], count)
        self.hiwater[key] = count

    def _deps(self, reads, writes):
        evs = []
        for b in reads:
            if b.w_ev is not None:
                evs.append(b.w_ev)
        for b in writes:
            if b.w_ev is not None:
                evs.append(b.w_ev)
            evs.extend(b.r_evs)
        return evs

    def op(self, eng_name, fn, reads=(), writes=(), sig=True, dma16=False):
        for ev in self._deps(reads, writes):
            self._wait(eng_name, ev)
        inst = fn(self.eng[eng_name])
        ev = None
        if dma16:
            skey = "q_" + eng_name
            if skey not in self.sem:
                self.sem[skey] = self.root_es.enter_context(self.nc.semaphore(skey))
                self.cnt[skey] = 0
                self.dma_sems.append(skey)
            self.cnt[skey] += 16
            inst.then_inc(self.sem[skey], 16)
            ev = (skey, self.cnt[skey])
        elif sig:
            self.cnt[eng_name] += 1
            inst.then_inc(self.sem[eng_name], 1)
            ev = (eng_name, self.cnt[eng_name])
        for b in reads:
            if ev is not None:
                b.r_evs.append(ev)
        for b in writes:
            b.w_ev = ev
            b.r_evs = []
        return ev

    def dma(self, qname, dst_buf, dst_ap, src_buf, src_ap, unordered_dst=False,
            **kw):
        eng_name = {"dsp": "sp", "dact": "act", "dpool": "pool"}[qname]
        sem_buf = kw.pop("sem_buf", None) or dst_buf
        skey = "b_" + sem_buf.name
        if self._in_stage:
            skey = f"b{self.stage_idx}_" + sem_buf.name
        if skey not in self.sem:
            if self._in_stage and self.sem_pool:
                h, c0 = self.sem_pool.pop()
                self.sem[skey] = h
                self.cnt[skey] = c0
            else:
                self.sem[skey] = self.root_es.enter_context(
                    self.nc.semaphore(skey))
                self.cnt[skey] = 0
            self.dma_sems.append(skey)
            if self._in_stage:
                self._stage_sems.append(skey)
        deps = self._deps([src_buf], [dst_buf] if not unordered_dst else [])
        for ev in deps:
            self._wait(eng_name, ev)
        inst = self.eng[eng_name].dma_start(out=dst_ap, in_=src_ap, **kw)
        self.cnt[skey] += 16
        inst.then_inc(self.sem[skey], 16)
        ev = (skey, self.cnt[skey])
        src_buf.r_evs.append(ev)
        if not unordered_dst:
            dst_buf.w_ev = ev
            dst_buf.r_evs = []
        return ev

    def finish(self):
        for skey in self.dma_sems:
            if self.cnt[skey] > 0:
                self.eng["sp"].wait_ge(self.sem[skey], self.cnt[skey])
        self.root_es.close()
        return self.nc



def emit_ln_stats(p, xb, x_ap, scr, st, col, n=D):
    c0, c1, c2, c3 = col, col + 1, col + 2, col + 3
    p.op("act", lambda e: e.activation(scr.ap(), x_ap, AF.Square,
                                       accum_out=st[:, c1:c1 + 1]),
         reads=[xb], writes=[scr, st])
    p.op("dve", lambda e: e.reduce_sum(st[:, c0:c0 + 1], x_ap, axis=AX.X),
         reads=[xb, st], writes=[st])
    p.op("dve", lambda e: e.tensor_scalar_mul(st[:, c0:c0 + 1], st[:, c0:c0 + 1],
                                              1.0 / n),
         reads=[st], writes=[st])
    p.op("dve", lambda e: e.tensor_scalar_mul(st[:, c1:c1 + 1], st[:, c1:c1 + 1],
                                              1.0 / n),
         reads=[st], writes=[st])
    p.op("dve", lambda e: e.scalar_tensor_tensor(
            st[:, c2:c2 + 1], st[:, c0:c0 + 1], st[:, c0:c0 + 1],
            st[:, c1:c1 + 1], op0=ALU.mult, op1=ALU.subtract),
         reads=[st], writes=[st])
    p.op("dve", lambda e: e.tensor_scalar(st[:, c2:c2 + 1], st[:, c2:c2 + 1],
                                          -1.0, 1e-5, op0=ALU.mult, op1=ALU.add),
         reads=[st], writes=[st])
    p.op("dve", lambda e: e.reciprocal(st[:, c2:c2 + 1], st[:, c2:c2 + 1]),
         reads=[st], writes=[st])
    p.op("act", lambda e: e.activation(st[:, c3:c3 + 1], st[:, c2:c2 + 1], AF.Sqrt),
         reads=[st], writes=[st])
    return st[:, c0:c0 + 1], st[:, c3:c3 + 1]


def emit_ln_apply(p, dst, dst_ap, xb, x_ap, st, mean_ap, rstd_ap):
    p.op("dve", lambda e: e.tensor_scalar(dst_ap, x_ap, mean_ap, rstd_ap,
                                          op0=ALU.subtract, op1=ALU.mult),
         reads=[xb, st], writes=[dst])



def build_l0attn(debug=False, p=None, xsrc=None, xdst=None, pref=""):
    standalone = p is None
    if standalone:
        p = P()
        xin = p.dram_in("xin", (TCORE, D), F32)
        xout_t = p.dram_out("xout", (TCORE, D), F32)
        xsrc = lambda t: (xin, xin[t * 128:(t + 1) * 128, :])
        xdst = lambda t: (xout_t, xout_t[t * 128:(t + 1) * 128, :])
    _ctx = p.stage()
    _ctx.__enter__()
    maskin = p.dram_in(pref + "maskin", (TCORE, D), F32)
    wblk = p.dram_in(pref + "wblk", (3, 128, 128), F32)
    brep = p.dram_in(pref + "brep", (3, 128, D), F32)
    identin = p.dram_in(pref + "identin", (128, 128), F32)
    if debug:
        dbg = {n: p.dram_out("dbg_" + n, shp, F32) for n, shp in [
            ("yf", (128, D)), ("ym", (128, D)), ("ymT", (128, D)),
            ("q", (128, D)), ("k", (128, D)), ("v", (128, D)),
            ("s", (128, 256)), ("z", (128, 256)), ("tau", (128, 16)),
            ("pat", (128, 256)), ("oac", (128, D))]}

    ident = p.sb("ident", (128, 128), BF16)
    wsb = p.sb("wsb", (128, 3 * 128), BF16)
    bsb = p.sb("bsb", (128, 3 * D), F32)
    p.dma("dpool", ident, ident.ap(), identin, identin.ap())
    p.dma("dpool", wsb, wsb.ap().rearrange("p (w n) -> p w n", w=3),
          wblk, wblk.ap().rearrange("w p n -> p w n"))
    p.dma("dsp", bsb, bsb.ap().rearrange("p (w n) -> p w n", w=3),
          brep, brep.ap().rearrange("w p n -> p w n"))

    xt = [p.sb(f"xt{i}", (128, D), F32) for i in range(2)]
    mk = [p.sb(f"mk{i}", (128, D), F32) for i in range(2)]
    xo = [p.sb(f"xo{i}", (128, D), F32) for i in range(2)]
    scr = p.sb("scr", (128, D), F32)
    st = p.sb("st", (128, 16), F32)
    yf = p.sb("yf", (128, D), F32)
    ym = p.sb("ym", (128, D), BF16)
    ymT = p.sb("ymT", (128, D), BF16)
    qsb = p.sb("qsb", (128, D), BF16)
    ksb = p.sb("ksb", (128, D), BF16)
    vsb = p.sb("vsb", (128, D), BF16)
    prod = p.sb("prod", (128, D), F32)
    sS = p.sb("sS", (128, NH * NH), F32)
    zz = p.sb("zz", (128, NH * NH), F32)
    rr = p.sb("rr", (128, NH * NH), F32)
    rp = p.sb("rp", (128, NH * NH), F32)
    sm = p.sb("sm", (128, NH * 8), F32)
    pat = p.sb("pat", (128, NH * NH), BF16)
    oac = p.sb("oac", (128, D), F32)

    tr_ps = p.psum("tr_ps", (128, 1024), BF16)
    q_ps = p.psum("q_ps", (128, D), F32)
    k_ps = p.psum("k_ps", (128, D), F32)
    v_ps = p.psum("v_ps", (128, D), F32)

    NIT = 10

    for t in range(NTT):
        xb = xt[t % 2]
        mb = mk[t % 2]
        ob = xo[t % 2]
        p.dma("dsp", xb, xb.ap(), *xsrc(t))
        p.dma("dsp", mb, mb.ap(), maskin, maskin[t * 128:(t + 1) * 128, :])

        mean, rstd = emit_ln_stats(p, xb, xb.ap(), scr, st, 0)
        emit_ln_apply(p, yf, yf.ap(), xb, xb.ap(), st, mean, rstd)
        p.op("dve", lambda e: e.tensor_mul(ym.ap(), yf.ap(), mb.ap()),
             reads=[yf, mb], writes=[ym])

        for dc in range(8):
            sl = slice(dc * 128, (dc + 1) * 128)
            p.op("pe", lambda e, sl=sl: e.transpose(tr_ps[:, sl], ym[:, sl],
                                                    ident.ap()),
                 reads=[ym, ident], writes=[tr_ps])
        p.op("act", lambda e: e.copy(ymT.ap(), tr_ps.ap()),
             reads=[tr_ps], writes=[ymT])

        for dc in range(8):
            sl = slice(dc * 128, (dc + 1) * 128)
            for wi, ps in enumerate((q_ps, k_ps, v_ps)):
                p.op("pe", lambda e, sl=sl, wi=wi, ps=ps: e.matmul(
                        ps[:, sl], ymT[:, sl], wsb[:, wi * 128:(wi + 1) * 128],
                        start=True, stop=True),
                     reads=[ymT, wsb], writes=[ps])
        for wi, (ps, dst) in enumerate(((q_ps, qsb), (k_ps, ksb), (v_ps, vsb))):
            p.op("dve", lambda e, ps=ps, dst=dst, wi=wi: e.tensor_add(
                    dst.ap(), ps.ap(), bsb[:, wi * D:(wi + 1) * D]),
                 reads=[ps, bsb], writes=[dst])

        q3 = qsb.ap().rearrange("p (h d) -> p h d", h=NH)
        for g in range(NH):
            kg = ksb[:, g * HD:(g + 1) * HD].rearrange("p (o d) -> p o d", o=1)
            p.op("dve", lambda e, kg=kg: e.tensor_tensor(
                    prod.ap().rearrange("p (h d) -> p h d", h=NH), q3,
                    kg.broadcast_to((128, NH, HD)), op=ALU.mult),
                 reads=[qsb, ksb], writes=[prod])
            p.op("dve", lambda e, g=g: e.reduce_sum(
                    sS.ap().rearrange("p (h g) -> p h g", h=NH)[:, :, g:g + 1],
                    prod.ap().rearrange("p (h d) -> p h d", h=NH), axis=AX.X),
                 reads=[prod], writes=[sS])

        z3 = zz.ap().rearrange("p (h g) -> p h g", h=NH)
        s3 = sS.ap().rearrange("p (h g) -> p h g", h=NH)
        r3 = rr.ap().rearrange("p (h g) -> p h g", h=NH)
        rp3 = rp.ap().rearrange("p (h g) -> p h g", h=NH)
        cM, cF, cS1, cTau = 0, NH, 2 * NH, 3 * NH
        p.op("dve", lambda e: e.tensor_scalar_mul(zz.ap(), sS.ap(), 1.0 / 64.0),
             reads=[sS], writes=[zz])
        p.op("dve", lambda e: e.reduce_max(sm[:, cM:cM + NH], z3, axis=AX.X),
             reads=[zz, sm], writes=[sm])
        p.op("dve", lambda e: e.tensor_tensor(
                z3, z3, sm[:, cM:cM + NH].rearrange("p (h o) -> p h o", o=1)
                .broadcast_to((128, NH, NH)), op=ALU.subtract),
             reads=[sm, zz], writes=[zz])
        p.op("dve", lambda e: e.memset(sm[:, cTau:cTau + NH], -1.0),
             reads=[], writes=[sm])
        for it in range(NIT):
            tau_b = sm[:, cTau:cTau + NH].rearrange("p (h o) -> p h o", o=1) \
                .broadcast_to((128, NH, NH))
            p.op("dve", lambda e, tau_b=tau_b: e.tensor_tensor(
                    r3, z3, tau_b, op=ALU.subtract),
                 reads=[zz, sm], writes=[rr])
            p.op("dve", lambda e: e.tensor_scalar_max(rp.ap(), rr.ap(), 0.0),
                 reads=[rr], writes=[rp])
            p.op("act", lambda e: e.activation(rr.ap(), rp.ap(), AF.Square),
                 reads=[rp], writes=[rr])
            p.op("dve", lambda e: e.reduce_sum(sm[:, cF:cF + NH], r3, axis=AX.X),
                 reads=[rr, sm], writes=[sm])
            p.op("dve", lambda e: e.reduce_sum(sm[:, cS1:cS1 + NH], rp3, axis=AX.X),
                 reads=[rp, sm], writes=[sm])
            p.op("dve", lambda e: e.tensor_scalar_add(sm[:, cF:cF + NH],
                                                      sm[:, cF:cF + NH], -1.0),
                 reads=[sm], writes=[sm])
            p.op("dve", lambda e: e.tensor_scalar_mul(sm[:, cS1:cS1 + NH],
                                                      sm[:, cS1:cS1 + NH], 2.0),
                 reads=[sm], writes=[sm])
            p.op("dve", lambda e: e.reciprocal(sm[:, cS1:cS1 + NH],
                                               sm[:, cS1:cS1 + NH]),
                 reads=[sm], writes=[sm])
            p.op("dve", lambda e: e.tensor_tensor(sm[:, cF:cF + NH],
                                                  sm[:, cF:cF + NH],
                                                  sm[:, cS1:cS1 + NH],
                                                  op=ALU.mult),
                 reads=[sm], writes=[sm])
            p.op("dve", lambda e: e.tensor_tensor(sm[:, cTau:cTau + NH],
                                                  sm[:, cTau:cTau + NH],
                                                  sm[:, cF:cF + NH],
                                                  op=ALU.add),
                 reads=[sm], writes=[sm])
        tau_b = sm[:, cTau:cTau + NH].rearrange("p (h o) -> p h o", o=1) \
            .broadcast_to((128, NH, NH))
        p.op("dve", lambda e: e.tensor_tensor(r3, z3, tau_b, op=ALU.subtract),
             reads=[zz, sm], writes=[rr])
        p.op("dve", lambda e: e.tensor_scalar_max(rr.ap(), rr.ap(), 0.0),
             reads=[rr], writes=[rr])
        p.op("act", lambda e: e.activation(pat.ap(), rr.ap(), AF.Square),
             reads=[rr], writes=[pat])

        p4 = pat.ap().rearrange("p (h g) -> p h g", h=NH)
        for g in range(NH):
            pg = p4[:, :, g:g + 1].broadcast_to((128, NH, HD))
            vg = vsb[:, g * HD:(g + 1) * HD].rearrange("p (o d) -> p o d", o=1) \
                .broadcast_to((128, NH, HD))
            o3 = (oac if g == 0 else prod).ap().rearrange("p (h d) -> p h d", h=NH)
            p.op("dve", lambda e, pg=pg, vg=vg, o3=o3: e.tensor_tensor(
                    o3, pg, vg, op=ALU.mult),
                 reads=[pat, vsb], writes=[oac if g == 0 else prod])
            if g > 0:
                p.op("dve", lambda e: e.tensor_add(oac.ap(), oac.ap(), prod.ap()),
                     reads=[prod, oac], writes=[oac])

        if debug and t == 0:
            for nm, b in [("yf", yf), ("ym", ym), ("ymT", ymT), ("q", qsb),
                          ("k", ksb), ("v", vsb), ("s", sS), ("z", zz),
                          ("pat", pat), ("oac", oac)]:
                tmpb = p.sb("dbgtmp_" + nm, b.ap_.shape, F32)
                p.op("dve", lambda e, b=b, tmpb=tmpb: e.tensor_copy(tmpb.ap(), b.ap()),
                     reads=[b], writes=[tmpb])
                p.dma("dsp", dbg[nm], dbg[nm].ap(), tmpb, tmpb.ap())
            tmpb = p.sb("dbgtmp_tau", (128, 16), F32)
            p.op("dve", lambda e, tmpb=tmpb: e.tensor_copy(tmpb.ap(), sm[:, cTau:cTau+16]),
                 reads=[sm], writes=[tmpb])
            p.dma("dsp", dbg["tau"], dbg["tau"].ap(), tmpb, tmpb.ap())

        mean2, rstd2 = emit_ln_stats(p, oac, oac.ap(), scr, st, 8)
        emit_ln_apply(p, yf, yf.ap(), oac, oac.ap(), st, mean2, rstd2)
        p.op("dve", lambda e: e.tensor_add(ob.ap(), xb.ap(), yf.ap()),
             reads=[xb, yf], writes=[ob])
        p.dma("dsp", *(lambda b, a: (b, a))(*xdst(t)), ob, ob.ap(),
              unordered_dst=True, sem_buf=ob)

    _ctx.__exit__(None, None, None)
    if standalone:
        return p.finish()
    return None


def prep_l0attn(inputs, xflat, c, pref=""):
    Wq, bq = inputs["L0_attn_Wq"], inputs["L0_attn_bq"]
    Wk, bk = inputs["L0_attn_Wk"], inputs["L0_attn_bk"]
    Wv, bv = inputs["L0_attn_Wv"], inputs["L0_attn_bv"]
    def blk(W):
        z = np.zeros((128, 128), np.float32)
        z[:64, :64] = W.T
        z[64:, 64:] = W.T
        return z
    wblk = np.stack([blk(Wq), blk(Wk), blk(Wv)])
    def brow(b):
        r = np.concatenate([b, b])
        return np.broadcast_to(np.tile(r, 8)[None, :], (128, D)).copy()
    brep = np.stack([brow(bq), brow(bk), brow(bv)])
    b, j = divmod(c, 4)
    mask = np.ones((TCORE, D), np.float32)
    if j == 0:
        mask[:1024] = np.tril(np.ones((1024, D), np.float32), k=-1)
    d = {"maskin": mask, "wblk": wblk, "brep": brep,
         "identin": np.eye(128, dtype=np.float32)}
    d = {pref + k: v for k, v in d.items()}
    if not pref:
        d["xin"] = slice_tokens(xflat, c)
    return d


def run_l0attn(inputs, xflat):
    if "l0attn" not in _nc_cache:
        _nc_cache["l0attn"] = build_l0attn()
    nc = _nc_cache["l0attn"]
    in_maps = [prep_l0attn(inputs, xflat, c) for c in range(NCORES)]
    res = run_bass_kernel_spmd(nc, in_maps, core_ids=list(range(NCORES)))
    return unslice_tokens([res.results[c]["xout"] for c in range(NCORES)])





def build_l0ffn(p=None, xsrc=None, xdst=None, pref=""):
    TB = 2048
    standalone = p is None
    if standalone:
        p = P()
        xin = p.dram_in("xin", (TCORE, D), F32)
        xout_t = p.dram_out("xout", (TCORE, D), F32)
        xsrc = lambda t: (xin, xin[t * 128:(t + 1) * 128, :])
        xdst = lambda t: (xout_t, xout_t[t * 128:(t + 1) * 128, :])
    _ctx = p.stage()
    _ctx.__enter__()
    rwin = p.dram_in(pref + "rwin", (D, 8), F32)
    w1in = p.dram_in(pref + "w1in", (8, D, D), F32)
    w2in = p.dram_in(pref + "w2in", (8, D, D), F32)
    sguin = p.dram_in(pref + "sguin", (D, 2 * D), F32)
    sdnin = p.dram_in(pref + "sdnin", (D, D), F32)
    sgwin = p.dram_in(pref + "sgwin", (D, 1), F32)
    identin = p.dram_in(pref + "identin", (128, 128), F32)

    ident = p.sb("ident", (128, 128), BF16)
    p.dma("dpool", ident, ident.ap(), identin, identin.ap())
    rw = p.sb("rw", (128, 8 * 8), F32)
    p.dma("dpool", rw, rw.ap().rearrange("p (dc e) -> p dc e", dc=8),
          rwin, rwin.ap().rearrange("(dc p) e -> p dc e", p=128))
    ident32 = p.sb("ident32", (128, 128), F32)
    p.dma("dsp", ident32, ident32.ap(), identin, identin.ap())
    yT32 = p.sb("yT32", (128, 128), F32)
    w1b = [p.sb("w1b0", (128, 64 * 128), BF16)]
    w2b = [p.sb("w2b0", (128, 8 * D), BF16)]
    p.dma("dpool", w1b[0], w1b[0].ap().rearrange("p (dc gc j) -> p dc gc j", dc=4, gc=16),
          sguin, sguin.ap()[0:512, :].rearrange("(dc p) (gc j) -> p dc gc j", p=128, j=128))
    p.dma("dpool", w2b[0], w2b[0].ap().rearrange("p (dc gc j) -> p dc gc j", dc=4, gc=16),
          sguin, sguin.ap()[512:1024, :].rearrange("(dc p) (gc j) -> p dc gc j", p=128, j=128))
    sdn = p.sb("sdn", (128, 8 * D), BF16)
    p.dma("dpool", sdn, sdn.ap().rearrange("p (fc d) -> p fc d", fc=8),
          sdnin, sdnin.ap().rearrange("(fc p) d -> p fc d", p=128))
    sgw = p.sb("sgw", (128, 8), BF16)
    p.dma("dpool", sgw, sgw.ap().rearrange("p (dc o) -> p dc o", dc=8),
          sgwin, sgwin.ap().rearrange("(dc p) o -> p dc o", p=128),
          allow_slow_non_contiguous=True)

    xt = [p.sb(f"xt{i}", (128, D), F32) for i in range(2)]
    scr = p.sb("scr", (128, D), F32)
    st = p.sb("st", (128, 16), F32)
    yb = p.sb("yb", (128, D), BF16)
    xT = p.sb("xT", (128, 8 * TB), BF16)
    acc = p.sb("acc", (128, NTT * D), F32)
    cw = p.sb("cw", (128, NTT * 8), F32)
    rsc = p.sb("rsc", (128, 8), F32)
    rs2 = p.sb("rs2", (128, 8), F32)
    rz = p.sb("rz", (128, 8), F32)
    rzero = p.sb("rzero", (128, 8), F32)
    rst = p.sb("rst", (128, 8), F32)
    gT = p.sb("gT", (128, 8 * 512), BF16)
    uT = p.sb("uT", (128, 8 * 512), BF16)
    hT = gT

    tr_ps = p.psum("tr_ps", (128, 1024), BF16)
    tr32_ps = p.psum("tr32_ps", (128, 128), F32)
    h_ps = p.psum("h_ps", (128, 512), F32)
    o_ps = [p.psum(f"o_ps{i}", (128, D), F32) for i in range(2)]
    r_ps = p.psum("r_ps", (128, 16), F32)

    p.op("dve", lambda e: e.memset(rzero.ap(), 0.0), reads=[], writes=[rzero])
    for t in range(NTT):
        xb = xt[t % 2]
        p.dma("dsp", xb, xb.ap(), *xsrc(t))
        mean, rstd = emit_ln_stats(p, xb, xb.ap(), scr, st, 0)
        yf32 = p.sb(f"yf32", (128, D), F32) if t == 0 else yf32
        emit_ln_apply(p, yf32, yf32.ap(), xb, xb.ap(), st, mean, rstd)
        p.op("dve", lambda e: e.tensor_copy(yb.ap(), yf32.ap()),
             reads=[yf32], writes=[yb])
        for dc in range(8):
            sl = slice(dc * 128, (dc + 1) * 128)
            p.op("pe", lambda e, sl=sl: e.transpose(tr_ps[:, sl], yb[:, sl],
                                                    ident.ap()),
                 reads=[yb, ident], writes=[tr_ps])
        p.op("act", lambda e, t=t: e.copy(
                xT.ap().rearrange("p (dc tb) -> p dc tb", dc=8)[:, :, t * 128:(t + 1) * 128],
                tr_ps.ap().rearrange("p (dc j) -> p dc j", dc=8)),
             reads=[tr_ps], writes=[xT])
        for dc in range(8):
            sl = slice(dc * 128, (dc + 1) * 128)
            p.op("pe", lambda e, sl=sl: e.transpose(tr32_ps.ap(), yf32[:, sl],
                                                    ident32.ap()),
                 reads=[yf32, ident32], writes=[tr32_ps])
            p.op("dve", lambda e: e.tensor_copy(yT32.ap(), tr32_ps.ap()),
                 reads=[tr32_ps], writes=[yT32])
            p.op("pe", lambda e, dc=dc: e.matmul(
                    r_ps[:, 0:8], yT32.ap(),
                    rw.ap().rearrange("p (dc e) -> p dc e", dc=8)[:, dc, :],
                    start=(dc == 0), stop=(dc == 7)),
                 reads=[yT32, rw], writes=[r_ps])
        p.op("dve", lambda e: e.reduce_max(rst[:, 0:1], r_ps[:, 0:8], axis=AX.X),
             reads=[r_ps, rst], writes=[rst])
        p.op("dve", lambda e: e.tensor_scalar_mul(rst[:, 0:1], rst[:, 0:1], -1.0),
             reads=[rst], writes=[rst])
        p.op("act", lambda e: e.activation(rsc.ap(), r_ps[:, 0:8], AF.Exp,
                                           bias=rst[:, 0:1], accum_out=rst[:, 1:2]),
             reads=[r_ps, rst], writes=[rsc, rst])
        p.op("dve", lambda e: e.reciprocal(rst[:, 2:3], rst[:, 1:2]),
             reads=[rst], writes=[rst])
        p.op("dve", lambda e: e.tensor_scalar_mul(rsc.ap(), rsc.ap(), rst[:, 2:3]),
             reads=[rst, rsc], writes=[rsc])
        p.op("dve", lambda e: e.reduce_max(rst[:, 3:4], rsc.ap(), axis=AX.X),
             reads=[rsc, rst], writes=[rst])
        p.op("dve", lambda e: e.tensor_single_scalar(rs2.ap(), rsc.ap(),
                                                     rst[:, 3:4], op=ALU.is_ge),
             reads=[rsc, rst], writes=[rs2])
        p.op("dve", lambda e: e.tensor_tensor_scan(rz.ap(), rs2.ap(), rzero.ap(),
                                                   0.0, op0=ALU.add, op1=ALU.add),
             reads=[rs2, rzero], writes=[rz])
        p.op("dve", lambda e: e.tensor_single_scalar(rz.ap(), rz.ap(), 1.0,
                                                     op=ALU.is_le),
             reads=[rz], writes=[rz])
        p.op("dve", lambda e: e.tensor_mul(rs2.ap(), rs2.ap(), rz.ap()),
             reads=[rz, rs2], writes=[rs2])
        cwv = cw.ap().rearrange("p (t e) -> p t e", t=NTT)
        p.op("dve", lambda e, t=t: e.tensor_scalar_mul(
                cwv[:, t, :], rs2.ap(), rst[:, 3:4]),
             reads=[rs2, rst, cw], writes=[cw])
        p.op("dve", lambda e: e.scalar_tensor_tensor(
                rs2.ap(), rs2.ap(), -1.0, rsc.ap(), op0=ALU.mult, op1=ALU.add),
             reads=[rsc, rs2], writes=[rs2], sig=True)
        p.op("dve", lambda e: e.reduce_max(rst[:, 4:5], rs2.ap(), axis=AX.X),
             reads=[rs2, rst], writes=[rst])
        p.op("dve", lambda e: e.tensor_single_scalar(rs2.ap(), rs2.ap(),
                                                     rst[:, 4:5], op=ALU.is_ge),
             reads=[rs2, rst], writes=[rs2])
        p.op("dve", lambda e: e.tensor_tensor_scan(rz.ap(), rs2.ap(), rzero.ap(),
                                                   0.0, op0=ALU.add, op1=ALU.add),
             reads=[rs2, rzero], writes=[rz])
        p.op("dve", lambda e: e.tensor_single_scalar(rz.ap(), rz.ap(), 1.0,
                                                     op=ALU.is_le),
             reads=[rz], writes=[rz])
        p.op("dve", lambda e: e.tensor_mul(rs2.ap(), rs2.ap(), rz.ap()),
             reads=[rz, rs2], writes=[rs2])
        p.op("dve", lambda e, t=t: e.scalar_tensor_tensor(
                cwv[:, t, :], rs2.ap(), rst[:, 4:5], cwv[:, t, :],
                op0=ALU.mult, op1=ALU.add),
             reads=[rs2, rst, cw], writes=[cw])

    accv = acc.ap().rearrange("p (t d) -> p t d", t=NTT)
    xTv = xT.ap().rearrange("p (dc tb) -> p dc tb", dc=8)
    sguvA = w1b[0].ap().rearrange("p (dc gc j) -> p dc gc j", dc=4, gc=16)
    sguvB = w2b[0].ap().rearrange("p (dc gc j) -> p dc gc j", dc=4, gc=16)
    def sguv_ap(dc, gc):
        return (sguvA if dc < 4 else sguvB)[:, dc % 4, gc, :]
    def sguv_buf(dc):
        return w1b[0] if dc < 4 else w2b[0]
    for tb in range(4):
        tsl = slice(tb * 512, (tb + 1) * 512)
        for gc in range(16):
            for dc in range(8):
                p.op("pe", lambda e, gc=gc, dc=dc, tsl=tsl: e.matmul(
                        h_ps.ap(), sguv_ap(dc, gc), xTv[:, dc, tsl],
                        start=(dc == 0), stop=(dc == 7)),
                     reads=[sguv_buf(dc), xT], writes=[h_ps])
            if gc < 8:
                p.op("act", lambda e, gc=gc: e.activation(
                        gT.ap().rearrange("p (fc tb) -> p fc tb", fc=8)[:, gc, :],
                        h_ps.ap(), AF.Silu),
                     reads=[h_ps], writes=[gT])
            else:
                p.op("dve", lambda e, gc=gc: e.tensor_mul(
                        uT.ap().rearrange("p (fc tb) -> p fc tb", fc=8)[:, gc - 8, :],
                        h_ps.ap(),
                        gT.ap().rearrange("p (fc tb) -> p fc tb", fc=8)[:, gc - 8, :]),
                     reads=[h_ps, gT], writes=[uT])
        uTv = uT.ap().rearrange("p (fc tb) -> p fc tb", fc=8)
        sgwv = sgw.ap().rearrange("p (dc o) -> p dc o", dc=8)
        sdnv = sdn.ap().rearrange("p (fc d) -> p fc d", fc=8)
        for tc in range(4):
            t = tb * 4 + tc
            csl = slice(tc * 128, (tc + 1) * 128)
            for dc in range(8):
                p.op("pe", lambda e, dc=dc, t=t, csl=None: e.matmul(
                        r_ps[:, 8:9], xTv[:, dc, t * 128:(t + 1) * 128],
                        sgwv[:, dc, :], start=(dc == 0), stop=(dc == 7)),
                     reads=[xT, sgw], writes=[r_ps])
            p.op("act", lambda e: e.activation(rst[:, 5:6], r_ps[:, 8:9],
                                               AF.Sigmoid),
                 reads=[r_ps, rst], writes=[rst])
            ops = o_ps[tc % 2]
            for half in range(2):
                hs = slice(half * 512, (half + 1) * 512)
                for fc in range(8):
                    p.op("pe", lambda e, fc=fc, csl=csl, hs=hs: e.matmul(
                            ops[:, hs], uTv[:, fc, csl], sdnv[:, fc, hs],
                            start=(fc == 0), stop=(fc == 7)),
                         reads=[uT, sdn], writes=[ops])
            p.op("dve", lambda e, t=t, ops=ops: e.tensor_scalar_mul(
                    accv[:, t, :], ops.ap(), rst[:, 5:6]),
                 reads=[ops, rst, acc], writes=[acc])

    for e_i in range(8):
        wb1, wb2 = w1b[0], w2b[0]
        p.dma("dpool", wb1, wb1.ap().rearrange("p (dc fc j) -> p dc fc j", dc=8, fc=8),
              w1in, w1in[e_i].rearrange("(dc p) (fc j) -> p dc fc j", p=128, j=128))
        p.dma("dpool", wb2, wb2.ap().rearrange("p (fc d) -> p fc d", fc=8),
              w2in, w2in[e_i].rearrange("(fc p) d -> p fc d", p=128))
        w1v = wb1.ap().rearrange("p (dc fc j) -> p dc fc j", dc=8, fc=8)
        w2v = wb2.ap().rearrange("p (fc d) -> p fc d", fc=8)
        cwv = cw.ap().rearrange("p (t e) -> p t e", t=NTT)
        for tb in range(4):
            tsl = slice(tb * 512, (tb + 1) * 512)
            hTv = hT.ap().rearrange("p (fc tb) -> p fc tb", fc=8)
            for fc in range(8):
                for dc in range(8):
                    p.op("pe", lambda e, fc=fc, dc=dc, tsl=tsl: e.matmul(
                            h_ps.ap(), w1v[:, dc, fc, :], xTv[:, dc, tsl],
                            start=(dc == 0), stop=(dc == 7)),
                         reads=[wb1, xT], writes=[h_ps])
                p.op("act", lambda e, fc=fc: e.activation(
                        hTv[:, fc, :], h_ps.ap(), AF.Silu),
                     reads=[h_ps], writes=[hT])
            for tc in range(4):
                t = tb * 4 + tc
                csl = slice(tc * 128, (tc + 1) * 128)
                ops = o_ps[tc % 2]
                for half in range(2):
                    hs = slice(half * 512, (half + 1) * 512)
                    for fc in range(8):
                        p.op("pe", lambda e, fc=fc, csl=csl, hs=hs: e.matmul(
                                ops[:, hs], hTv[:, fc, csl], w2v[:, fc, hs],
                                start=(fc == 0), stop=(fc == 7)),
                             reads=[hT, wb2], writes=[ops])
                p.op("dve", lambda e, t=t, e_i=e_i, ops=ops: e.scalar_tensor_tensor(
                        accv[:, t, :], ops.ap(), cwv[:, t, e_i:e_i + 1],
                        accv[:, t, :], op0=ALU.mult, op1=ALU.add),
                     reads=[ops, cw, acc], writes=[acc])

    uT32 = uT.ap_.bitcast(F32)
    xo_aps = [uT32[:, 0:D], uT32[:, D:2 * D]]
    for t in range(NTT):
        xb = xt[t % 2]
        ob_ap = xo_aps[t % 2]
        p.dma("dsp", xb, xb.ap(), *xsrc(t))
        a_ap = accv[:, t, :]
        mean2, rstd2 = emit_ln_stats(p, acc, a_ap, scr, st, 8)
        p.op("dve", lambda e, a_ap=a_ap: e.tensor_scalar(
                scr.ap(), a_ap, st[:, 8:9], st[:, 11:12],
                op0=ALU.subtract, op1=ALU.mult),
             reads=[acc, st], writes=[scr])
        p.op("dve", lambda e, ob_ap=ob_ap: e.tensor_add(ob_ap, xb.ap(), scr.ap()),
             reads=[xb, scr], writes=[uT])
        dbuf, dap = xdst(t)
        p.dma("dsp", dbuf, dap, uT, ob_ap, unordered_dst=True, sem_buf=uT)

    _ctx.__exit__(None, None, None)
    if standalone:
        return p.finish()
    return None


def prep_l0ffn(inputs, xflat, c, pref=""):
    d = {"rwin": inputs["L0_ffn_router_w"].astype(np.float32),
         "w1in": inputs["L0_ffn_w1"].astype(np.float32),
         "w2in": inputs["L0_ffn_w2"].astype(np.float32),
         "sguin": inputs["L0_ffn_shared_gate_up"].astype(np.float32),
         "sdnin": inputs["L0_ffn_shared_down"].astype(np.float32),
         "sgwin": inputs["L0_ffn_shared_gate_w"].astype(np.float32),
         "identin": np.eye(128, dtype=np.float32)}
    d = {pref + k: v for k, v in d.items()}
    if not pref:
        d["xin"] = slice_tokens(xflat, c)
    return d


def run_l0ffn(inputs, xflat):
    if "l0ffn" not in _nc_cache:
        _nc_cache["l0ffn"] = build_l0ffn()
    nc = _nc_cache["l0ffn"]
    in_maps = [prep_l0ffn(inputs, xflat, c) for c in range(NCORES)]
    res = run_bass_kernel_spmd(nc, in_maps, core_ids=list(range(NCORES)))
    return unslice_tokens([res.results[c]["xout"] for c in range(NCORES)])





def build_l2ffn(p=None, xsrc=None, xdst=None, pref=""):
    standalone = p is None
    if standalone:
        p = P()
        xin = p.dram_in("xin", (TCORE, D), F32)
        xout_t = p.dram_out("xout", (TCORE, D), F32)
        xsrc = lambda t: (xin, xin[t * 128:(t + 1) * 128, :])
        xdst = lambda t: (xout_t, xout_t[t * 128:(t + 1) * 128, :])
    _ctx = p.stage()
    _ctx.__enter__()
    rwin = p.dram_in(pref + "rwin", (D, 8), F32)
    ebin = p.dram_in(pref + "ebin", (128, 8), F32)
    w1in = p.dram_in(pref + "w1in", (10, D, 2048), F32)
    w3in = p.dram_in(pref + "w3in", (10, D, 2048), F32)
    w2in = p.dram_in(pref + "w2in", (10, 2048, D), F32)
    identin = p.dram_in(pref + "identin", (128, 128), F32)

    ident = p.sb("ident", (128, 128), BF16)
    p.dma("dpool", ident, ident.ap(), identin, identin.ap())
    ident32 = p.sb("ident32", (128, 128), F32)
    p.dma("dsp", ident32, ident32.ap(), identin, identin.ap())
    rw = p.sb("rw", (128, 8 * 8), F32)
    p.dma("dsp", rw, rw.ap().rearrange("p (dc e) -> p dc e", dc=8),
          rwin, rwin.ap().rearrange("(dc p) e -> p dc e", p=128))
    eb = p.sb("eb", (128, 8), F32)
    p.dma("dsp", eb, eb.ap(), ebin, ebin.ap())
    yT32 = p.sb("yT32", (128, 128), F32)

    w1b = p.sb("w1b", (128, 64 * 128), BF16)
    w3b = p.sb("w3b", (128, 64 * 128), BF16)
    w2b = p.sb("w2b", (128, 8 * D), BF16)

    xt = [p.sb(f"xt{i}", (128, D), F32) for i in range(2)]
    scr = p.sb("scr", (128, D), F32)
    st = p.sb("st", (128, 16), F32)
    yf32 = p.sb("yf32", (128, D), F32)
    yb = p.sb("yb", (128, D), BF16)
    xT = p.sb("xT", (128, 8 * TCORE), BF16)
    acc = p.sb("acc", (128, NTT * D), F32)
    cw = p.sb("cw", (128, NTT * 8), F32)
    rsc = p.sb("rsc", (128, 8), F32)
    rs2 = p.sb("rs2", (128, 8), F32)
    rz = p.sb("rz", (128, 8), F32)
    rzero = p.sb("rzero", (128, 8), F32)
    rst = p.sb("rst", (128, 16), F32)
    gT = p.sb("gT", (128, 8 * 512), BF16)

    tr_ps = p.psum("tr_ps", (128, 1024), BF16)
    tr32_ps = p.psum("tr32_ps", (128, 128), F32)
    h_ps = p.psum("h_ps", (128, 512), F32)
    o_ps = [p.psum(f"o_ps{i}", (128, D), F32) for i in range(2)]
    r_ps = p.psum("r_ps", (128, 16), F32)

    accv = acc.ap().rearrange("p (t d) -> p t d", t=NTT)
    cwv = cw.ap().rearrange("p (t e) -> p t e", t=NTT)
    xTv = xT.ap().rearrange("p (dc tb) -> p dc tb", dc=8)

    p.op("dve", lambda e: e.memset(rzero.ap(), 0.0), reads=[], writes=[rzero])
    for t in range(NTT):
        xb = xt[t % 2]
        p.dma("dsp", xb, xb.ap(), *xsrc(t))
        mean, rstd = emit_ln_stats(p, xb, xb.ap(), scr, st, 0)
        emit_ln_apply(p, yf32, yf32.ap(), xb, xb.ap(), st, mean, rstd)
        p.op("dve", lambda e: e.tensor_copy(yb.ap(), yf32.ap()),
             reads=[yf32], writes=[yb])
        for dc in range(8):
            sl = slice(dc * 128, (dc + 1) * 128)
            p.op("pe", lambda e, sl=sl: e.transpose(tr_ps[:, sl], yb[:, sl],
                                                    ident.ap()),
                 reads=[yb, ident], writes=[tr_ps])
        p.op("act", lambda e, t=t: e.copy(
                xTv[:, :, t * 128:(t + 1) * 128],
                tr_ps.ap().rearrange("p (dc j) -> p dc j", dc=8)),
             reads=[tr_ps], writes=[xT])
        for dc in range(8):
            sl = slice(dc * 128, (dc + 1) * 128)
            p.op("pe", lambda e, sl=sl: e.transpose(tr32_ps.ap(), yf32[:, sl],
                                                    ident32.ap()),
                 reads=[yf32, ident32], writes=[tr32_ps])
            p.op("dve", lambda e: e.tensor_copy(yT32.ap(), tr32_ps.ap()),
                 reads=[tr32_ps], writes=[yT32])
            p.op("pe", lambda e, dc=dc: e.matmul(
                    r_ps[:, 0:8], yT32.ap(),
                    rw.ap().rearrange("p (dc e) -> p dc e", dc=8)[:, dc, :],
                    start=(dc == 0), stop=(dc == 7)),
                 reads=[yT32, rw], writes=[r_ps])
        p.op("dve", lambda e: e.tensor_add(rsc.ap(), r_ps[:, 0:8], eb.ap()),
             reads=[r_ps, eb], writes=[rsc])
        p.op("dve", lambda e: e.reduce_max(rst[:, 0:1], rsc.ap(), axis=AX.X),
             reads=[rsc, rst], writes=[rst])
        p.op("dve", lambda e: e.tensor_single_scalar(rs2.ap(), rsc.ap(),
                                                     rst[:, 0:1], op=ALU.is_ge),
             reads=[rsc, rst], writes=[rs2])
        p.op("dve", lambda e: e.tensor_tensor_scan(rz.ap(), rs2.ap(), rzero.ap(),
                                                   0.0, op0=ALU.add, op1=ALU.add),
             reads=[rs2, rzero], writes=[rz])
        p.op("dve", lambda e: e.tensor_single_scalar(rz.ap(), rz.ap(), 1.0,
                                                     op=ALU.is_le),
             reads=[rz], writes=[rz])
        p.op("dve", lambda e: e.tensor_mul(rs2.ap(), rs2.ap(), rz.ap()),
             reads=[rz, rs2], writes=[rs2])
        p.op("dve", lambda e: e.scalar_tensor_tensor(
                rz.ap(), rs2.ap(), -1e9, rsc.ap(), op0=ALU.mult, op1=ALU.add),
             reads=[rs2, rsc], writes=[rz])
        p.op("dve", lambda e: e.reduce_max(rst[:, 1:2], rz.ap(), axis=AX.X),
             reads=[rz, rst], writes=[rst])
        p.op("dve", lambda e: e.tensor_single_scalar(rz.ap(), rz.ap(),
                                                     rst[:, 1:2], op=ALU.is_ge),
             reads=[rz, rst], writes=[rz])
        p.op("dve", lambda e: e.tensor_tensor_scan(rsc.ap(), rz.ap(), rzero.ap(),
                                                   0.0, op0=ALU.add, op1=ALU.add),
             reads=[rz, rzero], writes=[rsc])
        p.op("dve", lambda e: e.tensor_single_scalar(rsc.ap(), rsc.ap(), 1.0,
                                                     op=ALU.is_le),
             reads=[rsc], writes=[rsc])
        p.op("dve", lambda e: e.tensor_mul(rz.ap(), rz.ap(), rsc.ap()),
             reads=[rsc, rz], writes=[rz])
        p.op("dve", lambda e: e.tensor_sub(rst[:, 2:3], rst[:, 0:1], rst[:, 1:2]),
             reads=[rst], writes=[rst])
        p.op("act", lambda e: e.activation(rst[:, 3:4], rst[:, 2:3], AF.Sigmoid),
             reads=[rst], writes=[rst])
        p.op("act", lambda e: e.activation(rst[:, 4:5], rst[:, 2:3], AF.Sigmoid,
                                           scale=-1.0),
             reads=[rst], writes=[rst])
        p.op("dve", lambda e, t=t: e.tensor_scalar_mul(
                cwv[:, t, :], rs2.ap(), rst[:, 3:4]),
             reads=[rs2, rst, cw], writes=[cw])
        p.op("dve", lambda e, t=t: e.scalar_tensor_tensor(
                cwv[:, t, :], rz.ap(), rst[:, 4:5], cwv[:, t, :],
                op0=ALU.mult, op1=ALU.add),
             reads=[rz, rst, cw], writes=[cw])
        p.op("dve", lambda e, t=t: e.memset(accv[:, t, :], 0.0),
             reads=[], writes=[acc])

    w1v = w1b.ap().rearrange("p (dc hc j) -> p dc hc j", dc=8, hc=8)
    w3v = w3b.ap().rearrange("p (dc hc j) -> p dc hc j", dc=8, hc=8)
    w2v = w2b.ap().rearrange("p (hc d) -> p hc d", hc=8)
    gTv = gT.ap().rearrange("p (hc tb) -> p hc tb", hc=8)
    for e_i in range(10):
        routed = e_i < 8
        for half in range(2):
            hsl = slice(half * 1024, (half + 1) * 1024)
            p.dma("dpool", w1b, w1v,
                  w1in, w1in[e_i, :, hsl].rearrange("(dc p) (hc j) -> p dc hc j",
                                                    p=128, j=128))
            p.dma("dpool", w3b, w3v,
                  w3in, w3in[e_i, :, hsl].rearrange("(dc p) (hc j) -> p dc hc j",
                                                    p=128, j=128))
            p.dma("dpool", w2b, w2v,
                  w2in, w2in[e_i, hsl, :].rearrange("(hc p) d -> p hc d", p=128))
            for tb in range(4):
                tsl = slice(tb * 512, (tb + 1) * 512)
                for hc in range(8):
                    for dc in range(8):
                        p.op("pe", lambda e, hc=hc, dc=dc, tsl=tsl: e.matmul(
                                h_ps.ap(), w1v[:, dc, hc, :], xTv[:, dc, tsl],
                                start=(dc == 0), stop=(dc == 7)),
                             reads=[w1b, xT], writes=[h_ps])
                    p.op("act", lambda e, hc=hc: e.activation(
                            gTv[:, hc, :], h_ps.ap(), AF.Silu),
                         reads=[h_ps], writes=[gT])
                    for dc in range(8):
                        p.op("pe", lambda e, hc=hc, dc=dc, tsl=tsl: e.matmul(
                                h_ps.ap(), w3v[:, dc, hc, :], xTv[:, dc, tsl],
                                start=(dc == 0), stop=(dc == 7)),
                             reads=[w3b, xT], writes=[h_ps])
                    p.op("dve", lambda e, hc=hc: e.tensor_mul(
                            gTv[:, hc, :], gTv[:, hc, :], h_ps.ap()),
                         reads=[h_ps, gT], writes=[gT])
                for tc in range(4):
                    t = tb * 4 + tc
                    csl = slice(tc * 128, (tc + 1) * 128)
                    ops = o_ps[tc % 2]
                    for nh in range(2):
                        ns = slice(nh * 512, (nh + 1) * 512)
                        for hc in range(8):
                            p.op("pe", lambda e, hc=hc, csl=csl, ns=ns: e.matmul(
                                    ops[:, ns], gTv[:, hc, csl], w2v[:, hc, ns],
                                    start=(hc == 0), stop=(hc == 7)),
                                 reads=[gT, w2b], writes=[ops])
                    if routed:
                        p.op("dve", lambda e, t=t, e_i=e_i, ops=ops: e.scalar_tensor_tensor(
                                accv[:, t, :], ops.ap(), cwv[:, t, e_i:e_i + 1],
                                accv[:, t, :], op0=ALU.mult, op1=ALU.add),
                             reads=[ops, cw, acc], writes=[acc])
                    else:
                        p.op("dve", lambda e, t=t, ops=ops: e.tensor_add(
                                accv[:, t, :], accv[:, t, :], ops.ap()),
                             reads=[ops, acc], writes=[acc])

    gT32 = gT.ap_.bitcast(F32)
    xo_aps = [gT32[:, 0:D], gT32[:, D:2 * D]]
    for t in range(NTT):
        xb = xt[t % 2]
        ob_ap = xo_aps[t % 2]
        p.dma("dsp", xb, xb.ap(), *xsrc(t))
        a_ap = accv[:, t, :]
        mean2, rstd2 = emit_ln_stats(p, acc, a_ap, scr, st, 8)
        p.op("dve", lambda e, a_ap=a_ap: e.tensor_scalar(
                scr.ap(), a_ap, st[:, 8:9], st[:, 11:12],
                op0=ALU.subtract, op1=ALU.mult),
             reads=[acc, st], writes=[scr])
        p.op("dve", lambda e, ob_ap=ob_ap: e.tensor_add(ob_ap, xb.ap(), scr.ap()),
             reads=[xb, scr], writes=[gT])
        dbuf, dap = xdst(t)
        p.dma("dsp", dbuf, dap, gT, ob_ap, unordered_dst=True, sem_buf=gT)

    _ctx.__exit__(None, None, None)
    if standalone:
        return p.finish()
    return None


def prep_l2ffn(inputs, xflat, c, pref=""):
    w1 = np.concatenate([inputs["L2_ffn_w1"], inputs["L2_ffn_sw1"]], 0)
    w3 = np.concatenate([inputs["L2_ffn_w3"], inputs["L2_ffn_sw3"]], 0)
    w2 = np.concatenate([inputs["L2_ffn_w2"], inputs["L2_ffn_sw2"]], 0)
    d = {"rwin": np.ascontiguousarray(inputs["L2_ffn_router_w"].T.astype(np.float32)),
         "ebin": np.broadcast_to(inputs["L2_ffn_expert_bias"].astype(np.float32),
                                 (128, 8)).copy(),
         "w1in": np.ascontiguousarray(w1.transpose(0, 2, 1).astype(np.float32)),
         "w3in": np.ascontiguousarray(w3.transpose(0, 2, 1).astype(np.float32)),
         "w2in": np.ascontiguousarray(w2.transpose(0, 2, 1).astype(np.float32)),
         "identin": np.eye(128, dtype=np.float32)}
    d = {pref + k: v for k, v in d.items()}
    if not pref:
        d["xin"] = slice_tokens(xflat, c)
    return d


def run_l2ffn(inputs, xflat):
    if "l2ffn" not in _nc_cache:
        _nc_cache["l2ffn"] = build_l2ffn()
    nc = _nc_cache["l2ffn"]
    in_maps = [prep_l2ffn(inputs, xflat, c) for c in range(NCORES)]
    res = run_bass_kernel_spmd(nc, in_maps, core_ids=list(range(NCORES)))
    return unslice_tokens([res.results[c]["xout"] for c in range(NCORES)])





def build_l3ffn_h1(p=None, xsrc=None, pref="", yout=None, qkvout=None,
                   kvout=None):
    standalone = p is None
    if standalone:
        p = P()
        xin = p.dram_in("xin", (TCORE, D), F32)
        xsrc = lambda t: (xin, xin[t * 128:(t + 1) * 128, :])
        yout = p.dram_out("yout", (TCORE, D), F32)
        qkvout = p.dram_out("qkvout", (TCORE, 3 * D), BF16)
    _ctx = p.stage()
    _ctx.__enter__()
    ipwin = p.dram_in(pref + "ipwin", (D, 3 * D), F32)
    ipbin = p.dram_in(pref + "ipbin", (128, 3 * D), F32)
    ln1win = p.dram_in(pref + "ln1win", (128, D), F32)
    ln1bin = p.dram_in(pref + "ln1bin", (128, D), F32)
    identin = p.dram_in(pref + "identin", (128, 128), F32)

    ident = p.sb("ident", (128, 128), BF16)
    p.dma("dpool", ident, ident.ap(), identin, identin.ap())
    ipw = p.sb("ipw", (128, 8 * 3 * D), BF16)
    p.dma("dpool", ipw, ipw.ap().rearrange("p (dc n) -> p dc n", dc=8),
          ipwin, ipwin.ap().rearrange("(dc p) n -> p dc n", p=128))
    ipb = p.sb("ipb", (128, 3 * D), F32)
    p.dma("dsp", ipb, ipb.ap(), ipbin, ipbin.ap())
    ln1w = p.sb("ln1w", (128, D), F32)
    p.dma("dsp", ln1w, ln1w.ap(), ln1win, ln1win.ap())
    ln1b = p.sb("ln1b", (128, D), F32)
    p.dma("dsp", ln1b, ln1b.ap(), ln1bin, ln1bin.ap())

    xt = [p.sb(f"xt{i}", (128, D), F32) for i in range(2)]
    scr = p.sb("scr", (128, D), F32)
    st = p.sb("st", (128, 16), F32)
    yf = [p.sb(f"yf{i}", (128, D), F32) for i in range(2)]
    hnb = p.sb("hnb", (128, D), BF16)
    hnT = p.sb("hnT", (128, D), BF16)
    qkv = [p.sb(f"qkv{i}", (128, 3 * D), BF16) for i in range(2)]
    tr_ps = p.psum("tr_ps", (128, 1024), BF16)
    q_ps = [p.psum(f"q_ps{i}", (128, 512), F32) for i in range(6)]

    for t in range(NTT):
        xb = xt[t % 2]
        yb_ = yf[t % 2]
        qb = qkv[t % 2]
        p.dma("dsp", xb, xb.ap(), *xsrc(t))
        mean, rstd = emit_ln_stats(p, xb, xb.ap(), scr, st, 0)
        emit_ln_apply(p, yb_, yb_.ap(), xb, xb.ap(), st, mean, rstd)
        p.dma("dsp", yout, yout[t * 128:(t + 1) * 128, :], yb_, yb_.ap(),
              unordered_dst=True, sem_buf=yb_)
        mean2, rstd2 = emit_ln_stats(p, yb_, yb_.ap(), scr, st, 8)
        p.op("dve", lambda e, yb_=yb_: e.tensor_scalar(
                scr.ap(), yb_.ap(), st[:, 8:9], st[:, 11:12],
                op0=ALU.subtract, op1=ALU.mult),
             reads=[yb_, st], writes=[scr])
        p.op("dve", lambda e: e.tensor_mul(scr.ap(), scr.ap(), ln1w.ap()),
             reads=[ln1w, scr], writes=[scr])
        p.op("dve", lambda e: e.tensor_add(scr.ap(), scr.ap(), ln1b.ap()),
             reads=[ln1b, scr], writes=[scr])
        p.op("dve", lambda e: e.tensor_copy(hnb.ap(), scr.ap()),
             reads=[scr], writes=[hnb])
        for dc in range(8):
            sl = slice(dc * 128, (dc + 1) * 128)
            p.op("pe", lambda e, sl=sl: e.transpose(tr_ps[:, sl], hnb[:, sl],
                                                    ident.ap()),
                 reads=[hnb, ident], writes=[tr_ps])
        p.op("act", lambda e: e.copy(hnT.ap(), tr_ps.ap()),
             reads=[tr_ps], writes=[hnT])
        ipwv = ipw.ap().rearrange("p (dc n) -> p dc n", dc=8)
        for nh in range(6):
            ns = slice(nh * 512, (nh + 1) * 512)
            ps = q_ps[nh]
            for dc in range(8):
                p.op("pe", lambda e, dc=dc, ns=ns, ps=ps: e.matmul(
                        ps.ap(), hnT[:, dc * 128:(dc + 1) * 128], ipwv[:, dc, ns],
                        start=(dc == 0), stop=(dc == 7)),
                     reads=[hnT, ipw], writes=[ps])
            p.op("dve", lambda e, ns=ns, ps=ps, qb=qb: e.tensor_add(
                    qb[:, ns], ps.ap(), ipb[:, ns]),
                 reads=[ps, ipb, qb], writes=[qb])
        p.dma("dsp", qkvout, qkvout[t * 128:(t + 1) * 128, :], qb, qb.ap(),
              unordered_dst=True, sem_buf=qb)
        if kvout is not None:
            p.dma("dsp", kvout, kvout[t * 128:(t + 1) * 128, :],
                  qb, qb[:, D:3 * D], unordered_dst=True, sem_buf=qb)
    _ctx.__exit__(None, None, None)
    if standalone:
        return p.finish()
    return None


def build_l3ffn_h2(p=None, xsrc=None, xdst=None, pref="", yin=None,
                   qkvin=None, pkv_fn=None):
    standalone = p is None
    if standalone:
        p = P()
        xin = p.dram_in("xin", (TCORE, D), F32)
        yin = p.dram_in("yin", (TCORE, D), F32)
        qkvin = p.dram_in("qkvin", (TCORE, 3 * D), BF16)
        pkvin = p.dram_in("pkvin", (TCORE, 2 * D), BF16)
        pkv_fn = lambda t: (pkvin, pkvin[t * 128:(t + 1) * 128, :])
        xout_t = p.dram_out("xout", (TCORE, D), F32)
        xsrc = lambda t: (xin, xin[t * 128:(t + 1) * 128, :])
        xdst = lambda t: (xout_t, xout_t[t * 128:(t + 1) * 128, :])
    _ctx = p.stage()
    _ctx.__enter__()
    b1in = p.dram_in(pref + "b1in", (128, 1), F32)
    opwin = p.dram_in(pref + "opwin", (D, D), F32)
    opbin = p.dram_in(pref + "opbin", (128, D), F32)
    ln2win = p.dram_in(pref + "ln2win", (128, D), F32)
    ln2bin = p.dram_in(pref + "ln2bin", (128, D), F32)
    rwin = p.dram_in(pref + "rwin", (D, 8), F32)
    w1in = p.dram_in(pref + "w1in", (8, D, 2048), F32)
    w2in = p.dram_in(pref + "w2in", (8, 2048, D), F32)
    identin = p.dram_in(pref + "identin", (128, 128), F32)

    ident = p.sb("ident", (128, 128), BF16)
    p.dma("dpool", ident, ident.ap(), identin, identin.ap())
    ident32 = p.sb("ident32", (128, 128), F32)
    p.dma("dsp", ident32, ident32.ap(), identin, identin.ap())
    b1f = p.sb("b1f", (128, 1), F32)
    p.dma("dsp", b1f, b1f.ap(), b1in, b1in.ap())
    opw = p.sb("opw", (128, 8 * D), BF16)
    p.dma("dpool", opw, opw.ap().rearrange("p (dc n) -> p dc n", dc=8),
          opwin, opwin.ap().rearrange("(dc p) n -> p dc n", p=128))
    opb = p.sb("opb", (128, D), F32)
    p.dma("dsp", opb, opb.ap(), opbin, opbin.ap())
    ln2w = p.sb("ln2w", (128, D), F32)
    p.dma("dsp", ln2w, ln2w.ap(), ln2win, ln2win.ap())
    ln2b = p.sb("ln2b", (128, D), F32)
    p.dma("dsp", ln2b, ln2b.ap(), ln2bin, ln2bin.ap())
    rw = p.sb("rw", (128, 8 * 8), F32)
    p.dma("dsp", rw, rw.ap().rearrange("p (dc e) -> p dc e", dc=8),
          rwin, rwin.ap().rearrange("(dc p) e -> p dc e", p=128))
    yT32 = p.sb("yT32", (128, 128), F32)

    w1b = p.sb("w1b", (128, 64 * 128), BF16)
    w2b = p.sb("w2b", (128, 8 * D), BF16)

    xt = [p.sb("xt0", (128, D), F32)] * 2
    yt = [p.sb("yt0", (128, D), F32)] * 2
    qt = [p.sb("qt0", (128, 3 * D), BF16)] * 2
    pt = [p.sb("pt0", (128, 2 * D), BF16)] * 2
    scr = p.sb("scr", (128, D), F32)
    scr2 = p.sb("scr2", (128, D), F32)
    st = p.sb("st", (128, 16), F32)
    sw = p.sb("sw", (128, 6 * NH), F32)
    hn2 = p.sb("hn2", (128, D), F32)
    hnb = p.sb("hnb", (128, D), BF16)
    atT = p.sb("atT", (128, D), BF16)
    xT = p.sb("xT", (128, 8 * TCORE), BF16)
    acc = p.sb("acc", (128, NTT * D), F32)
    cw = p.sb("cw", (128, NTT * 8), F32)
    rsc = p.sb("rsc", (128, 8), F32)
    rs2 = p.sb("rs2", (128, 8), F32)
    rz = p.sb("rz", (128, 8), F32)
    rzero = p.sb("rzero", (128, 8), F32)
    rst = p.sb("rst", (128, 16), F32)
    gT = p.sb("gT", (128, 8 * 512), BF16)

    tr_ps = p.psum("tr_ps", (128, 1024), BF16)
    tr32_ps = p.psum("tr32_ps", (128, 128), F32)
    h_ps = p.psum("h_ps", (128, 512), F32)
    o_ps = [p.psum(f"o_ps{i}", (128, D), F32) for i in range(2)]
    r_ps = p.psum("r_ps", (128, 16), F32)

    accv = acc.ap().rearrange("p (t d) -> p t d", t=NTT)
    cwv = cw.ap().rearrange("p (t e) -> p t e", t=NTT)
    xTv = xT.ap().rearrange("p (dc tb) -> p dc tb", dc=8)
    swv = sw.ap().rearrange("p (k h) -> p k h", k=6)

    p.op("dve", lambda e: e.memset(rzero.ap(), 0.0), reads=[], writes=[rzero])
    for t in range(NTT):
        yb_, qb, pb = yt[t % 2], qt[t % 2], pt[t % 2]
        p.dma("dsp", yb_, yb_.ap(), yin, yin[t * 128:(t + 1) * 128, :])
        p.dma("dsp", qb, qb.ap(), qkvin, qkvin[t * 128:(t + 1) * 128, :])
        p.dma("dsp", pb, pb.ap(), *pkv_fn(t))
        q3 = qb.ap()[:, 0:D].rearrange("p (h d) -> p h d", h=NH)
        ko3 = qb.ap()[:, D:2 * D].rearrange("p (h d) -> p h d", h=NH)
        vo3 = qb.ap()[:, 2 * D:3 * D].rearrange("p (h d) -> p h d", h=NH)
        kp3 = pb.ap()[:, 0:D].rearrange("p (h d) -> p h d", h=NH)
        vp3 = pb.ap()[:, D:2 * D].rearrange("p (h d) -> p h d", h=NH)
        p.op("dve", lambda e: e.tensor_mul(
                scr.ap().rearrange("p (h d) -> p h d", h=NH), q3, kp3),
             reads=[qb, pb], writes=[scr])
        p.op("dve", lambda e: e.reduce_sum(
                swv[:, 0, :], scr.ap().rearrange("p (h d) -> p h d", h=NH),
                axis=AX.X),
             reads=[scr, sw], writes=[sw])
        p.op("dve", lambda e: e.tensor_mul(
                scr.ap().rearrange("p (h d) -> p h d", h=NH), q3, ko3),
             reads=[qb], writes=[scr])
        p.op("dve", lambda e: e.reduce_sum(
                swv[:, 1, :], scr.ap().rearrange("p (h d) -> p h d", h=NH),
                axis=AX.X),
             reads=[scr, sw], writes=[sw])
        p.op("dve", lambda e: e.tensor_sub(swv[:, 2, :], swv[:, 0, :],
                                           swv[:, 1, :]),
             reads=[sw], writes=[sw])
        p.op("act", lambda e: e.activation(swv[:, 3, :], swv[:, 2, :],
                                           AF.Sigmoid, scale=0.125),
             reads=[sw], writes=[sw])
        p.op("dve", lambda e: e.tensor_scalar_mul(swv[:, 4, :], swv[:, 3, :],
                                                  b1f[:, 0:1]),
             reads=[sw, b1f], writes=[sw])
        p.op("dve", lambda e: e.tensor_scalar(swv[:, 5, :], swv[:, 4, :],
                                              -1.0, 1.0, op0=ALU.mult, op1=ALU.add),
             reads=[sw], writes=[sw])
        p.op("dve", lambda e: e.tensor_tensor(
                scr.ap().rearrange("p (h d) -> p h d", h=NH),
                vp3, swv[:, 4, :].rearrange("p (h o) -> p h o", o=1)
                .broadcast_to((128, NH, HD)), op=ALU.mult),
             reads=[pb, sw], writes=[scr])
        p.op("dve", lambda e: e.tensor_tensor(
                scr2.ap().rearrange("p (h d) -> p h d", h=NH),
                vo3, swv[:, 5, :].rearrange("p (h o) -> p h o", o=1)
                .broadcast_to((128, NH, HD)), op=ALU.mult),
             reads=[qb, sw], writes=[scr2])
        p.op("dve", lambda e: e.tensor_add(scr.ap(), scr.ap(), scr2.ap()),
             reads=[scr2, scr], writes=[scr])
        p.op("dve", lambda e: e.tensor_copy(hnb.ap(), scr.ap()),
             reads=[scr], writes=[hnb])
        for dc in range(8):
            sl = slice(dc * 128, (dc + 1) * 128)
            p.op("pe", lambda e, sl=sl: e.transpose(tr_ps[:, sl], hnb[:, sl],
                                                    ident.ap()),
                 reads=[hnb, ident], writes=[tr_ps])
        opwv = opw.ap().rearrange("p (dc n) -> p dc n", dc=8)
        p.op("act", lambda e: e.copy(atT.ap(), tr_ps.ap()),
             reads=[tr_ps], writes=[atT])
        for nh in range(2):
            ns = slice(nh * 512, (nh + 1) * 512)
            ps = o_ps[nh]
            for dc in range(8):
                p.op("pe", lambda e, dc=dc, ns=ns, ps=ps: e.matmul(
                        ps.ap()[:, 0:512], atT[:, dc * 128:(dc + 1) * 128],
                        opwv[:, dc, ns], start=(dc == 0), stop=(dc == 7)),
                     reads=[atT, opw], writes=[ps])
        for nh in range(2):
            ns = slice(nh * 512, (nh + 1) * 512)
            ps = o_ps[nh]
            p.op("dve", lambda e, ns=ns, ps=ps: e.tensor_add(
                    scr[:, ns], ps.ap()[:, 0:512], opb[:, ns]),
                 reads=[ps, opb], writes=[scr])
        p.op("dve", lambda e, t=t: e.tensor_add(accv[:, t, :], yb_.ap(), scr.ap()),
             reads=[yb_, scr, acc], writes=[acc])
        a_ap = accv[:, t, :]
        mean2, rstd2 = emit_ln_stats(p, acc, a_ap, scr2, st, 8)
        p.op("dve", lambda e, a_ap=a_ap: e.tensor_scalar(
                hn2.ap(), a_ap, st[:, 8:9], st[:, 11:12],
                op0=ALU.subtract, op1=ALU.mult),
             reads=[acc, st], writes=[hn2])
        p.op("dve", lambda e: e.tensor_mul(hn2.ap(), hn2.ap(), ln2w.ap()),
             reads=[ln2w, hn2], writes=[hn2])
        p.op("dve", lambda e: e.tensor_add(hn2.ap(), hn2.ap(), ln2b.ap()),
             reads=[ln2b, hn2], writes=[hn2])
        p.op("dve", lambda e: e.tensor_copy(hnb.ap(), hn2.ap()),
             reads=[hn2], writes=[hnb])
        for dc in range(8):
            sl = slice(dc * 128, (dc + 1) * 128)
            p.op("pe", lambda e, sl=sl: e.transpose(tr_ps[:, sl], hnb[:, sl],
                                                    ident.ap()),
                 reads=[hnb, ident], writes=[tr_ps])
        p.op("act", lambda e, t=t: e.copy(
                xTv[:, :, t * 128:(t + 1) * 128],
                tr_ps.ap().rearrange("p (dc j) -> p dc j", dc=8)),
             reads=[tr_ps], writes=[xT])
        for dc in range(8):
            sl = slice(dc * 128, (dc + 1) * 128)
            p.op("pe", lambda e, sl=sl: e.transpose(tr32_ps.ap(), hn2[:, sl],
                                                    ident32.ap()),
                 reads=[hn2, ident32], writes=[tr32_ps])
            p.op("dve", lambda e: e.tensor_copy(yT32.ap(), tr32_ps.ap()),
                 reads=[tr32_ps], writes=[yT32])
            p.op("pe", lambda e, dc=dc: e.matmul(
                    r_ps[:, 0:8], yT32.ap(),
                    rw.ap().rearrange("p (dc e) -> p dc e", dc=8)[:, dc, :],
                    start=(dc == 0), stop=(dc == 7)),
                 reads=[yT32, rw], writes=[r_ps])
        p.op("dve", lambda e: e.reduce_max(rst[:, 0:1], r_ps[:, 0:8], axis=AX.X),
             reads=[r_ps, rst], writes=[rst])
        p.op("dve", lambda e: e.tensor_scalar_mul(rst[:, 0:1], rst[:, 0:1], -1.0),
             reads=[rst], writes=[rst])
        p.op("act", lambda e: e.activation(rsc.ap(), r_ps[:, 0:8], AF.Exp,
                                           bias=rst[:, 0:1], accum_out=rst[:, 1:2]),
             reads=[r_ps, rst], writes=[rsc, rst])
        p.op("dve", lambda e: e.reciprocal(rst[:, 2:3], rst[:, 1:2]),
             reads=[rst], writes=[rst])
        p.op("dve", lambda e: e.tensor_scalar_mul(rsc.ap(), rsc.ap(), rst[:, 2:3]),
             reads=[rst, rsc], writes=[rsc])
        p.op("dve", lambda e: e.reduce_max(rst[:, 3:4], rsc.ap(), axis=AX.X),
             reads=[rsc, rst], writes=[rst])
        p.op("dve", lambda e: e.tensor_single_scalar(rs2.ap(), rsc.ap(),
                                                     rst[:, 3:4], op=ALU.is_ge),
             reads=[rsc, rst], writes=[rs2])
        p.op("dve", lambda e: e.tensor_tensor_scan(rz.ap(), rs2.ap(), rzero.ap(),
                                                   0.0, op0=ALU.add, op1=ALU.add),
             reads=[rs2, rzero], writes=[rz])
        p.op("dve", lambda e: e.tensor_single_scalar(rz.ap(), rz.ap(), 1.0,
                                                     op=ALU.is_le),
             reads=[rz], writes=[rz])
        p.op("dve", lambda e: e.tensor_mul(rs2.ap(), rs2.ap(), rz.ap()),
             reads=[rz, rs2], writes=[rs2])
        p.op("dve", lambda e, t=t: e.tensor_scalar_mul(
                cwv[:, t, :], rs2.ap(), rst[:, 3:4]),
             reads=[rs2, rst, cw], writes=[cw])
        p.op("dve", lambda e: e.scalar_tensor_tensor(
                rs2.ap(), rs2.ap(), -1.0, rsc.ap(), op0=ALU.mult, op1=ALU.add),
             reads=[rsc, rs2], writes=[rs2])
        p.op("dve", lambda e: e.reduce_max(rst[:, 4:5], rs2.ap(), axis=AX.X),
             reads=[rs2, rst], writes=[rst])
        p.op("dve", lambda e: e.tensor_single_scalar(rs2.ap(), rs2.ap(),
                                                     rst[:, 4:5], op=ALU.is_ge),
             reads=[rs2, rst], writes=[rs2])
        p.op("dve", lambda e: e.tensor_tensor_scan(rz.ap(), rs2.ap(), rzero.ap(),
                                                   0.0, op0=ALU.add, op1=ALU.add),
             reads=[rs2, rzero], writes=[rz])
        p.op("dve", lambda e: e.tensor_single_scalar(rz.ap(), rz.ap(), 1.0,
                                                     op=ALU.is_le),
             reads=[rz], writes=[rz])
        p.op("dve", lambda e: e.tensor_mul(rs2.ap(), rs2.ap(), rz.ap()),
             reads=[rz, rs2], writes=[rs2])
        p.op("dve", lambda e, t=t: e.scalar_tensor_tensor(
                cwv[:, t, :], rs2.ap(), rst[:, 4:5], cwv[:, t, :],
                op0=ALU.mult, op1=ALU.add),
             reads=[rs2, rst, cw], writes=[cw])

    w1v = w1b.ap().rearrange("p (dc fc j) -> p dc fc j", dc=8, fc=8)
    w2v = w2b.ap().rearrange("p (fc d) -> p fc d", fc=8)
    gTv = gT.ap().rearrange("p (fc tb) -> p fc tb", fc=8)
    for e_i in range(8):
        for half in range(2):
            fsl = slice(half * 1024, (half + 1) * 1024)
            p.dma("dpool", w1b, w1v,
                  w1in, w1in[e_i, :, fsl].rearrange("(dc p) (fc j) -> p dc fc j",
                                                    p=128, j=128))
            p.dma("dpool", w2b, w2v,
                  w2in, w2in[e_i, fsl, :].rearrange("(fc p) d -> p fc d", p=128))
            for tb in range(4):
                tsl = slice(tb * 512, (tb + 1) * 512)
                for fc in range(8):
                    for dc in range(8):
                        p.op("pe", lambda e, fc=fc, dc=dc, tsl=tsl: e.matmul(
                                h_ps.ap(), w1v[:, dc, fc, :], xTv[:, dc, tsl],
                                start=(dc == 0), stop=(dc == 7)),
                             reads=[w1b, xT], writes=[h_ps])
                    p.op("act", lambda e, fc=fc: e.activation(
                            gTv[:, fc, :], h_ps.ap(), AF.Gelu),
                         reads=[h_ps], writes=[gT])
                for tc in range(4):
                    t = tb * 4 + tc
                    csl = slice(tc * 128, (tc + 1) * 128)
                    ops = o_ps[tc % 2]
                    for nh in range(2):
                        ns = slice(nh * 512, (nh + 1) * 512)
                        for fc in range(8):
                            p.op("pe", lambda e, fc=fc, csl=csl, ns=ns: e.matmul(
                                    ops[:, ns], gTv[:, fc, csl], w2v[:, fc, ns],
                                    start=(fc == 0), stop=(fc == 7)),
                                 reads=[gT, w2b], writes=[ops])
                    p.op("dve", lambda e, t=t, e_i=e_i, ops=ops: e.scalar_tensor_tensor(
                            accv[:, t, :], ops.ap(), cwv[:, t, e_i:e_i + 1],
                            accv[:, t, :], op0=ALU.mult, op1=ALU.add),
                         reads=[ops, cw, acc], writes=[acc])

    gT32 = gT.ap_.bitcast(F32)
    xo_aps = [gT32[:, 0:D], gT32[:, D:2 * D]]
    for t in range(NTT):
        xb = xt[t % 2]
        ob_ap = xo_aps[t % 2]
        p.dma("dsp", xb, xb.ap(), *xsrc(t))
        a_ap = accv[:, t, :]
        mean2, rstd2 = emit_ln_stats(p, acc, a_ap, scr, st, 8)
        p.op("dve", lambda e, a_ap=a_ap: e.tensor_scalar(
                scr.ap(), a_ap, st[:, 8:9], st[:, 11:12],
                op0=ALU.subtract, op1=ALU.mult),
             reads=[acc, st], writes=[scr])
        p.op("dve", lambda e, ob_ap=ob_ap: e.tensor_add(ob_ap, xb.ap(), scr.ap()),
             reads=[xb, scr], writes=[gT])
        dbuf, dap = xdst(t)
        p.dma("dsp", dbuf, dap, gT, ob_ap, unordered_dst=True, sem_buf=gT)
    _ctx.__exit__(None, None, None)
    if standalone:
        return p.finish()
    return None


def _rep(v, rows=128):
    return np.broadcast_to(np.asarray(v, np.float32), (rows,) + np.asarray(v).shape).copy()




def build_l3ffn_f(p=None, xsrc=None, xdst=None, pref=""):
    y_d = p.dram_tmp(pref + "y_d", (TCORE, D), F32)
    qkv_d = p.dram_tmp(pref + "qkv_d", (TCORE, 3 * D), BF16)
    kv_c = p.dram_tmp(pref + "kv_c", (TCORE, 2 * D), BF16)
    kv_ag = p.dram_tmp(pref + "kv_ag", (2 * TCORE, 2 * D), BF16)
    build_l3ffn_h1(p=p, xsrc=xsrc, pref=pref, yout=y_d, qkvout=qkv_d,
                   kvout=kv_c)
    p.cc("AllGather", ALU.bypass, [[0, 4], [1, 5], [2, 6], [3, 7]],
         kv_c, kv_c.ap(), kv_ag, kv_ag.ap())
    prin = p.dram_in(pref + "prank", (1, 1), I32)
    prs = p.sb("prs", (1, 1), I32)
    p.dma("dsp", prs, prs.ap(), prin, prin.ap())
    preg = p.sval("sp", prs, prs[0:1, 0:1])
    pkv_d = p.dram_tmp(pref + "pkv_d", (TCORE, 2 * D), BF16)
    p.dma("dsp", pkv_d, pkv_d.ap(), kv_ag,
          kv_ag.ap().rearrange("(r t) d -> r t d", r=2)[bass.ts(preg, 1), :, :]
          .squeeze(0) if False else
          kv_ag.ap().rearrange("(r t) d -> r t d", r=2)[bass.ts(preg, 1), :, :])

    def pkv_fn(t):
        return (pkv_d, pkv_d[t * 128:(t + 1) * 128, :])
    build_l3ffn_h2(p=p, xsrc=xsrc, xdst=xdst, pref=pref, yin=y_d,
                   qkvin=qkv_d, pkv_fn=pkv_fn)
    return None


def prep_l3ffn_f(inputs, xflat, c, pref=""):
    ident = np.eye(128, dtype=np.float32)
    d = {"ipwin": np.ascontiguousarray(inputs["L3_ffn_in_proj_w"].T.astype(np.float32)),
         "ipbin": _rep(inputs["L3_ffn_in_proj_b"]),
         "ln1win": _rep(inputs["L3_ffn_ln1_w"]),
         "ln1bin": _rep(inputs["L3_ffn_ln1_b"]),
         "b1in": np.full((128, 1), 1.0 if c >= 4 else 0.0, np.float32),
         "opwin": np.ascontiguousarray(inputs["L3_ffn_out_proj_w"].T.astype(np.float32)),
         "opbin": _rep(inputs["L3_ffn_out_proj_b"]),
         "ln2win": _rep(inputs["L3_ffn_ln2_w"]),
         "ln2bin": _rep(inputs["L3_ffn_ln2_b"]),
         "rwin": np.ascontiguousarray(inputs["L3_ffn_router_w"].T.astype(np.float32)),
         "w1in": inputs["L3_ffn_w1"].astype(np.float32),
         "w2in": inputs["L3_ffn_w2"].astype(np.float32),
         "identin": ident,
         "prank": np.array([[1 if c < 4 else 0]], np.int32)}
    return {pref + k: v for k, v in d.items()}


def run_l3ffn(inputs, xflat):
    if "l3ffn_h1" not in _nc_cache:
        _nc_cache["l3ffn_h1"] = build_l3ffn_h1()
        _nc_cache["l3ffn_h2"] = build_l3ffn_h2()
    ident = np.eye(128, dtype=np.float32)
    in1 = []
    for c in range(NCORES):
        in1.append({"xin": slice_tokens(xflat, c),
                    "ipwin": np.ascontiguousarray(
                        inputs["L3_ffn_in_proj_w"].T.astype(np.float32)),
                    "ipbin": _rep(inputs["L3_ffn_in_proj_b"]),
                    "ln1win": _rep(inputs["L3_ffn_ln1_w"]),
                    "ln1bin": _rep(inputs["L3_ffn_ln1_b"]),
                    "identin": ident})
    r1 = run_bass_kernel_spmd(_nc_cache["l3ffn_h1"], in1,
                              core_ids=list(range(NCORES)))
    ys = [r1.results[c]["yout"] for c in range(NCORES)]
    qkvs = [r1.results[c]["qkvout"] for c in range(NCORES)]
    in2 = []
    for c in range(NCORES):
        partner = (c + 4) % 8
        in2.append({"xin": slice_tokens(xflat, c),
                    "yin": ys[c].astype(np.float32),
                    "qkvin": qkvs[c],
                    "pkvin": np.ascontiguousarray(qkvs[partner][:, D:3 * D]),
                    "b1in": np.full((128, 1), 1.0 if c >= 4 else 0.0, np.float32),
                    "opwin": np.ascontiguousarray(
                        inputs["L3_ffn_out_proj_w"].T.astype(np.float32)),
                    "opbin": _rep(inputs["L3_ffn_out_proj_b"]),
                    "ln2win": _rep(inputs["L3_ffn_ln2_w"]),
                    "ln2bin": _rep(inputs["L3_ffn_ln2_b"]),
                    "rwin": np.ascontiguousarray(
                        inputs["L3_ffn_router_w"].T.astype(np.float32)),
                    "w1in": inputs["L3_ffn_w1"].astype(np.float32),
                    "w2in": inputs["L3_ffn_w2"].astype(np.float32),
                    "identin": ident})
    r2 = run_bass_kernel_spmd(_nc_cache["l3ffn_h2"], in2,
                              core_ids=list(range(NCORES)))
    return unslice_tokens([r2.results[c]["xout"] for c in range(NCORES)])





def build_attn_kv(p=None, xsrc=None, pref="", qTout=None, kTout=None,
                  vout=None, qnTout=None, kmaxout=None, xcopy=None):
    standalone = p is None
    if standalone:
        p = P()
        xin = p.dram_in("xin", (TCORE, D), F32)
        xsrc = lambda t: (xin, xin[t * 128:(t + 1) * 128, :])
        qTout = p.dram_out("qTout", (D, TCORE), BF16)
        kTout = p.dram_out("kTout", (D, TCORE), BF16)
        vout = p.dram_out("vout", (TCORE, D), BF16)
        qnout = p.dram_out("qnout", (TCORE, NH), F32)
        knout = p.dram_out("knout", (TCORE, NH), F32)
    _ctx = p.stage()
    _ctx.__enter__()
    wqin = p.dram_in(pref + "wqin", (D, D), F32)
    wkin = p.dram_in(pref + "wkin", (D, D), F32)
    wvin = p.dram_in(pref + "wvin", (D, D), F32)
    bqin = p.dram_in(pref + "bqin", (128, D), F32)
    bkin = p.dram_in(pref + "bkin", (128, D), F32)
    bvin = p.dram_in(pref + "bvin", (128, D), F32)
    identin = p.dram_in(pref + "identin", (128, 128), F32)
    if not standalone:
        ident32 = p.sb("ident32", (128, 128), F32)
        p.dma("dsp", ident32, ident32.ap(), identin, identin.ap())
        qnT_sb = p.sb("qnT_sb", (16, TCORE), F32)
        knT_sb = p.sb("knT_sb", (16, TCORE), F32)
        nt_ps = p.psum("nt_ps", (128, 128), F32)

    ident = p.sb("ident", (128, 128), BF16)
    p.dma("dpool", ident, ident.ap(), identin, identin.ap())
    wq = p.sb("wq", (128, 8 * D), BF16)
    wk = p.sb("wk", (128, 8 * D), BF16)
    wv = p.sb("wv", (128, 8 * D), BF16)
    for wsb, win in ((wq, wqin), (wk, wkin), (wv, wvin)):
        p.dma("dpool", wsb, wsb.ap().rearrange("p (dc n) -> p dc n", dc=8),
              win, win.ap().rearrange("(dc p) n -> p dc n", p=128))
    bq = p.sb("bq", (128, D), F32)
    bk = p.sb("bk", (128, D), F32)
    bv = p.sb("bv", (128, D), F32)
    for bsb, bin_ in ((bq, bqin), (bk, bkin), (bv, bvin)):
        p.dma("dsp", bsb, bsb.ap(), bin_, bin_.ap())

    xt = [p.sb("xt0", (128, D), F32)] * 2
    scr = p.sb("scr", (128, D), F32)
    st = p.sb("st", (128, 16), F32)
    yf = p.sb("yf", (128, D), F32)
    yb = p.sb("yb", (128, D), BF16)
    yT = p.sb("yT", (128, D), BF16)
    qb_ = p.sb("qb_", (128, D), BF16)
    kb_ = p.sb("kb_", (128, D), BF16)
    vb_ = p.sb("vb_", (128, D), BF16)
    qkT = p.sb("qkT", (128, D), BF16)
    nrm = p.sb("nrm", (128, NH), F32)
    tr_ps = p.psum("tr_ps", (128, 1024), BF16)
    pj_ps = [p.psum(f"pj_ps{i}", (128, 512), F32) for i in range(2)]

    for t in range(NTT):
        xb = xt[t % 2]
        p.dma("dsp", xb, xb.ap(), *xsrc(t))
        if xcopy is not None:
            p.dma("dsp", xcopy, xcopy[t * 128:(t + 1) * 128, :], xb, xb.ap(),
                  unordered_dst=True, sem_buf=xb)
        mean, rstd = emit_ln_stats(p, xb, xb.ap(), scr, st, 0)
        emit_ln_apply(p, yf, yf.ap(), xb, xb.ap(), st, mean, rstd)
        p.op("dve", lambda e: e.tensor_copy(yb.ap(), yf.ap()),
             reads=[yf], writes=[yb])
        for dc in range(8):
            sl = slice(dc * 128, (dc + 1) * 128)
            p.op("pe", lambda e, sl=sl: e.transpose(tr_ps[:, sl], yb[:, sl],
                                                    ident.ap()),
                 reads=[yb, ident], writes=[tr_ps])
        p.op("act", lambda e: e.copy(yT.ap(), tr_ps.ap()),
             reads=[tr_ps], writes=[yT])
        for wsb, bsb, dst in ((wq, bq, qb_), (wk, bk, kb_), (wv, bv, vb_)):
            wv_ = wsb.ap().rearrange("p (dc n) -> p dc n", dc=8)
            for nh in range(2):
                ns = slice(nh * 512, (nh + 1) * 512)
                ps = pj_ps[nh]
                for dc in range(8):
                    p.op("pe", lambda e, dc=dc, ns=ns, ps=ps, wv_=wv_: e.matmul(
                            ps.ap(), yT[:, dc * 128:(dc + 1) * 128], wv_[:, dc, ns],
                            start=(dc == 0), stop=(dc == 7)),
                         reads=[yT, wsb], writes=[ps])
                p.op("dve", lambda e, ns=ns, ps=ps, dst=dst, bsb=bsb: e.tensor_add(
                        dst[:, ns], ps.ap(), bsb[:, ns]),
                     reads=[ps, bsb, dst], writes=[dst])
        p.dma("dsp", vout, vout[t * 128:(t + 1) * 128, :], vb_, vb_.ap(),
              unordered_dst=True, sem_buf=vb_)
        for src_, tout, which in ((qb_, qTout, "q"), (kb_, kTout, "k")):
            for dc in range(8):
                sl = slice(dc * 128, (dc + 1) * 128)
                p.op("pe", lambda e, sl=sl, src_=src_: e.transpose(
                        tr_ps[:, sl], src_[:, sl], ident.ap()),
                     reads=[src_, ident], writes=[tr_ps])
            p.op("act", lambda e: e.copy(qkT.ap(), tr_ps.ap()),
                 reads=[tr_ps], writes=[qkT])
            p.dma("dsp", tout,
                  tout.ap().rearrange("(rc p) t -> p rc t", p=128)[:, :, t * 128:(t + 1) * 128],
                  qkT, qkT.ap().rearrange("p (rc j) -> p rc j", rc=8),
                  unordered_dst=True, sem_buf=qkT)
            p.op("act", lambda e, src_=src_: e.activation(scr.ap(), src_.ap(),
                                                          AF.Square),
                 reads=[src_], writes=[scr])
            p.op("dve", lambda e: e.reduce_sum(
                    nrm.ap(), scr.ap().rearrange("p (h d) -> p h d", h=NH),
                    axis=AX.X),
                 reads=[scr, nrm], writes=[nrm])
            p.op("act", lambda e: e.activation(nrm.ap(), nrm.ap(), AF.Sqrt),
                 reads=[nrm], writes=[nrm])
            if standalone:
                nout = qnout if which == "q" else knout
                p.dma("dsp", nout, nout[t * 128:(t + 1) * 128, :], nrm, nrm.ap(),
                      unordered_dst=True, sem_buf=nrm)
            else:
                nTdst = qnT_sb if which == "q" else knT_sb
                p.op("pe", lambda e: e.transpose(nt_ps.ap_[0:16, :], nrm.ap(),
                                                 ident32.ap()),
                     reads=[nrm, ident32], writes=[nt_ps])
                p.op("dve", lambda e, t=t, nTdst=nTdst: e.tensor_copy(
                        nTdst.ap_[:, t * 128:(t + 1) * 128], nt_ps.ap_[0:16, :]),
                     reads=[nt_ps], writes=[nTdst])
    if not standalone:
        kml = p.sb("kml", (16, 1), F32)
        p.op("dve", lambda e: e.reduce_max(kml.ap(), knT_sb.ap(), axis=AX.X),
             reads=[knT_sb, kml], writes=[kml])
        p.dma("dsp", kmaxout, kmaxout.ap(), kml, kml.ap())
        p.dma("dsp", qnTout, qnTout.ap(), qnT_sb, qnT_sb.ap())
    _ctx.__exit__(None, None, None)
    if standalone:
        return p.finish()
    return None


def build_attn_core(causal, p=None, xsrc=None, xdst=None, pref="",
                    qT_src=None, k_src=None, v_src=None, qnT_src=None,
                    kmax_fn=None, plan_fn=None):
    SK = 8192
    NKT = SK // 128
    standalone = p is None
    if standalone:
        p = P()
        xin = p.dram_in("xin", (TCORE, D), F32)
        xsrc = lambda t: (xin, xin[t * 128:(t + 1) * 128, :])
        xout_t = p.dram_out("xout", (TCORE, D), F32)
        xdst = lambda t: (xout_t, xout_t[t * 128:(t + 1) * 128, :])
        qTin = p.dram_in("qTin", (D, TCORE), BF16)
        kTin = p.dram_in("kTin", (D, SK), BF16)
        vin = p.dram_in("vin", (SK, D), BF16)
        qnTin = p.dram_in("qnTin", (NH, TCORE), F32)
        knTin = p.dram_in("knTin", (NH, SK), F32)
        qT_src = (qTin, qTin.ap())
        k_src = lambda kt, hg: (kTin, kTin.ap()[hg * 256:(hg + 1) * 256,
                                                kt * 128:(kt + 1) * 128])
        v_src = lambda kt, hg: (vin, vin.ap()[kt * 128:(kt + 1) * 128,
                                              hg * 256:(hg + 1) * 256])
        qnT_src = (qnTin, qnTin.ap())
    _ctx = p.stage()
    _ctx.__enter__()
    augin = p.dram_in(pref + "augin", (NKT, 512), F32)
    woin = p.dram_in(pref + "woin", (D, D), F32)
    boin = p.dram_in(pref + "boin", (128, D), F32)

    wo = p.sb("wo", (128, 8 * D), BF16)
    p.dma("dpool", wo, wo.ap().rearrange("p (dc n) -> p dc n", dc=8),
          woin, woin.ap().rearrange("(dc p) n -> p dc n", p=128))
    bo = p.sb("bo", (128, D), F32)
    p.dma("dsp", bo, bo.ap(), boin, boin.ap())
    qT = p.sb("qT", (128, 8 * TCORE), BF16)
    p.dma("dsp", qT, qT.ap().rearrange("p (rc t) -> p rc t", rc=8),
          qT_src[0], qT_src[1].rearrange("(rc p) t -> p rc t", p=128))
    kmax = p.sb("kmax", (16, 8), F32)
    if standalone:
        knc = p.sb("knc", (16, 2048), F32)
        for i in range(4):
            p.dma("dsp", knc, knc.ap(), knTin, knTin[:, i * 2048:(i + 1) * 2048])
            p.op("dve", lambda e, i=i: e.reduce_max(kmax[:, i:i + 1], knc.ap(),
                                                    axis=AX.X),
                 reads=[knc, kmax], writes=[kmax])
        p.op("dve", lambda e: e.reduce_max(kmax[:, 4:5], kmax[:, 0:4], axis=AX.X),
             reads=[kmax], writes=[kmax])
    else:
        kmax_fn(p, kmax)
    p.op("dve", lambda e: e.tensor_scalar_mul(kmax[:, 5:6], kmax[:, 4:5], -1.0),
         reads=[kmax], writes=[kmax])
    qnT = p.sb("qnT", (16, TCORE), F32)
    p.dma("dsp", qnT, qnT.ap(), qnT_src[0], qnT_src[1])
    brow = p.sb("brow", (16, TCORE), F32)
    p.op("dve", lambda e: e.tensor_scalar_mul(brow.ap(), qnT.ap(), kmax[:, 5:6]),
         reads=[qnT, kmax], writes=[brow])

    ktb = [p.sb(f"ktb{i}", (128, 4 * 128), BF16) for i in range(2)]
    vtb = [p.sb(f"vtb{i}", (128, 4 * 65), BF16) for i in range(2)]
    for b_ in vtb:
        p.op("dve", lambda e, b_=b_: e.memset(
                b_.ap().rearrange("p (s j) -> p s j", s=4)[:, :, 64:65], 1.0),
             reads=[], writes=[b_])
    ones64 = p.sb("ones64", (1, 64), BF16)
    p.op("dve", lambda e: e.memset(ones64.ap(), 1.0), reads=[], writes=[ones64])
    qp = p.sb("qp", (128, 16 * 512), BF16)
    pt = p.sb("pt", (128, 512), BF16)
    pt2 = p.sb("pt2", (128, 512), BF16)
    oT = p.sb("oT", (128, 8 * 512), BF16)
    invz = p.sb("invz", (1, 512), F32)
    invzb = p.sb("invzb", (1, 512), BF16)
    ores = p.sb("ores", (128, D), F32)
    xt = [p.sb("xt0", (128, D), F32)] * 2
    scr = p.sb("scr", (128, D), F32)
    st = p.sb("st", (128, 16), F32)

    av_ps = [p.psum(f"av_ps{i}", (128, 512), F32) for i in range(4)]
    s_ps = [p.psum(f"s_ps{i}", (128, 512), F32) for i in range(2)]
    rp_ps = s_ps[0]
    pj_ps = p.psum("pj_ps", (128, 1024), F32)

    qTv = qT.ap().rearrange("p (rc t) -> p rc t", rc=8)
    qpv = qp.ap().rearrange("p (h q) -> p h q", h=16)
    ktbv = [b.ap().rearrange("p (s j) -> p s j", s=4) for b in ktb]
    vtbv = [b.ap().rearrange("p (s j) -> p s j", s=4) for b in vtb]

    if plan_fn is None:
        def plan_fn(qb):
            nkt = (48 + 4 * qb + 4) if causal else NKT
            plan = []
            for kt in range(nkt):
                mask_i = None
                if causal and kt >= 48:
                    i = kt - 48 - 4 * qb
                    if 0 <= i < 4:
                        mask_i = i
                plan.append((k_src, v_src, kt, kt, mask_i))
            return plan

    for qb in range(4):
        qsl = slice(qb * 512, (qb + 1) * 512)
        for h in range(NH):
            rc, odd = divmod(h, 2)
            psrc = slice(64, 128) if odd else slice(0, 64)
            p.dma("dpool", qp, qp.ap_[0:64, h * 512:(h + 1) * 512],
                  qT, qT.ap_[psrc, rc * TCORE + qb * 512: rc * TCORE + (qb + 1) * 512])
            p.dma("dpool", qp, qp.ap_[64:65, h * 512:(h + 1) * 512],
                  brow, brow.ap_[h:h + 1, qsl])
        plan = plan_fn(qb)
        nkt = len(plan)
        for hg in range(4):
            for ki, (kfn, vfn, kkt, aug_row, mask_i) in enumerate(plan):
                kb = ktb[ki % 2]
                vb = vtb[ki % 2]
                kbv = ktbv[ki % 2]
                vbv = vtbv[ki % 2]
                kbuf, kap = kfn(kkt, hg)
                p.dma("dsp", kb, kb.ap_[0:64, :].rearrange("p (s j) -> p s j", s=4),
                      kbuf, kap.rearrange("(s dd) t -> dd s t", dd=64))
                p.dma("dpool", kb, kb.ap_[64:65, :],
                      augin, augin.ap()[aug_row:aug_row + 1, :])
                vbuf, vap = vfn(kkt, hg)
                p.dma("dsp", vb,
                      vb.ap_[:, :].rearrange("p (s j) -> p s j", s=4)[:, :, 0:64],
                      vbuf, vap.rearrange("t (s dd) -> t s dd", s=4))
                for hi in range(4):
                    h = hg * 4 + hi
                    sps = s_ps[hi % 2]
                    kap2 = kb.ap_[0:65, hi * 128:(hi + 1) * 128]
                    qap = qp.ap_[0:65, h * 512:(h + 1) * 512]
                    p.op("pe", lambda e, kap2=kap2, qap=qap, sps=sps: e.matmul(
                            sps.ap(), kap2, qap, start=True, stop=True),
                         reads=[kb, qp], writes=[sps])
                    p.op("act", lambda e, sps=sps: e.activation(
                            pt.ap(), sps.ap(), AF.Exp, scale=0.125),
                         reads=[sps], writes=[pt])
                    ptuse = pt
                    if mask_i is not None:
                        p.op("pool", lambda e, i=mask_i: e.affine_select(
                                pt2.ap(), pt.ap(), pattern=[[1, 512]],
                                compare_op=ALU.is_ge, fill=0.0,
                                base=-128 * i, channel_multiplier=-1),
                             reads=[pt], writes=[pt2])
                        ptuse = pt2
                    p.op("pe", lambda e, vbv=vbv, hi=hi, ptuse=ptuse, av=av_ps[hi], ki=ki: e.matmul(
                            av.ap_[0:65, :], vbv[:, hi, :], ptuse.ap(),
                            start=(ki == 0), stop=(ki == nkt - 1)),
                         reads=[vb, ptuse], writes=[av_ps[hi]])
            for hi in range(4):
                h = hg * 4 + hi
                av = av_ps[hi]
                p.op("dve", lambda e, av=av: e.reciprocal(invz.ap(), av.ap_[64:65, :]),
                     reads=[av, invz], writes=[invz])
                p.op("dve", lambda e: e.tensor_copy(invzb.ap(), invz.ap()),
                     reads=[invz], writes=[invzb])
                p.op("pe", lambda e: e.matmul(rp_ps.ap_[0:64, :], ones64.ap(),
                                              invzb.ap(), start=True, stop=True),
                     reads=[ones64, invzb], writes=[rp_ps])
                rc, odd = divmod(h, 2)
                obase = 64 * odd
                p.op("act", lambda e, av=av, obase=obase, rc=rc: e.copy(
                        oT.ap_[obase:obase + 64, rc * 512:(rc + 1) * 512],
                        av.ap_[0:64, :]),
                     reads=[av], writes=[oT])
                p.op("dve", lambda e, obase=obase, rc=rc: e.tensor_mul(
                        oT.ap_[obase:obase + 64, rc * 512:(rc + 1) * 512],
                        oT.ap_[obase:obase + 64, rc * 512:(rc + 1) * 512],
                        rp_ps.ap_[0:64, :]),
                     reads=[rp_ps, oT], writes=[oT])
        wov = wo.ap().rearrange("p (dc n) -> p dc n", dc=8)
        for tc in range(4):
            t = qb * 4 + tc
            for nh in range(2):
                ns = slice(nh * 512, (nh + 1) * 512)
                for dc in range(8):
                    p.op("pe", lambda e, dc=dc, ns=ns, tc=tc: e.matmul(
                            pj_ps.ap_[:, ns], oT.ap_[:, dc * 512 + tc * 128:
                                                     dc * 512 + (tc + 1) * 128],
                            wov[:, dc, ns], start=(dc == 0), stop=(dc == 7)),
                         reads=[oT, wo], writes=[pj_ps])
            p.op("dve", lambda e: e.tensor_add(ores.ap(), pj_ps.ap(), bo.ap()),
                 reads=[pj_ps, bo], writes=[ores])
            xb = xt[t % 2]
            p.dma("dsp", xb, xb.ap(), *xsrc(t))
            mean2, rstd2 = emit_ln_stats(p, ores, ores.ap(), scr, st, 8)
            p.op("dve", lambda e: e.tensor_scalar(
                    scr.ap(), ores.ap(), st[:, 8:9], st[:, 11:12],
                    op0=ALU.subtract, op1=ALU.mult),
                 reads=[ores, st], writes=[scr])
            p.op("dve", lambda e: e.tensor_add(ores.ap(), xb.ap(), scr.ap()),
                 reads=[xb, scr], writes=[ores])
            dbuf, dap = xdst(t)
            p.dma("dsp", dbuf, dap, ores, ores.ap(),
                  unordered_dst=True, sem_buf=ores)
    _ctx.__exit__(None, None, None)
    if standalone:
        return p.finish()
    return None


def _attn_hostprep_kv(inputs, xflat, c, wq, wk, wv, bq, bk, bv):
    return {"xin": slice_tokens(xflat, c),
            "wqin": np.ascontiguousarray(wq.T.astype(np.float32)),
            "wkin": np.ascontiguousarray(wk.T.astype(np.float32)),
            "wvin": np.ascontiguousarray(wv.T.astype(np.float32)),
            "bqin": _rep(bq), "bkin": _rep(bk), "bvin": _rep(bv),
            "identin": np.eye(128, dtype=np.float32)}


def _gather_global(per_core, axis):
    out = {}
    for b in range(2):
        chunks = []
        for g in range(8):
            j = g if g < 4 else 7 - g
            arr = per_core[4 * b + j]
            sl = [slice(None)] * arr.ndim
            sl[axis] = slice(0, 1024) if g < 4 else slice(1024, 2048)
            chunks.append(arr[tuple(sl)])
        out[b] = np.concatenate(chunks, axis=axis)
    return out


def run_attn(inputs, xflat, which):
    if "attn_kv" not in _nc_cache:
        _nc_cache["attn_kv"] = build_attn_kv()
    if which == "l1":
        key = "attn_core_causal"
        if key not in _nc_cache:
            _nc_cache[key] = build_attn_core(True)
        wq, wk, wv, wo = (inputs["L1_attn_wq"], inputs["L1_attn_wk"],
                          inputs["L1_attn_wv"], inputs["L1_attn_wo"])
        zb = np.zeros(D, np.float32)
        bq = bk = bv = bo = zb
    else:
        key = "attn_core_full"
        if key not in _nc_cache:
            _nc_cache[key] = build_attn_core(False)
        wq, wk, wv, wo = (inputs["L2_attn_Wq"], inputs["L2_attn_Wk"],
                          inputs["L2_attn_Wv"], inputs["L2_attn_Wo"])
        bq, bk, bv, bo = (inputs["L2_attn_bq"], inputs["L2_attn_bk"],
                          inputs["L2_attn_bv"], inputs["L2_attn_bo"])
    in1 = [_attn_hostprep_kv(inputs, xflat, c, wq, wk, wv, bq, bk, bv)
           for c in range(NCORES)]
    r1 = run_bass_kernel_spmd(_nc_cache["attn_kv"], in1,
                              core_ids=list(range(NCORES)))
    kT_g = _gather_global([r1.results[c]["kTout"] for c in range(NCORES)], 1)
    v_g = _gather_global([r1.results[c]["vout"] for c in range(NCORES)], 0)

    def mk_augt(aug):
        return np.ascontiguousarray(
            np.tile(aug.reshape(64, 128), (1, 4)).astype(np.float32))
    kn_g = _gather_global([r1.results[c]["knout"] for c in range(NCORES)], 0)
    qTs = [r1.results[c]["qTout"] for c in range(NCORES)]
    qns = [r1.results[c]["qnout"] for c in range(NCORES)]
    ident = np.eye(128, dtype=np.float32)
    ones64 = np.ones((1, 64), np.float32)
    in2 = []
    if which == "l2":
        for c in range(NCORES):
            b = c // 4
            in2.append({"xin": slice_tokens(xflat, c),
                        "qTin": qTs[c],
                        "kTin": np.ascontiguousarray(kT_g[b]),
                        "vin": np.ascontiguousarray(v_g[b]),
                        "qnTin": np.ascontiguousarray(qns[c].T),
                        "knTin": np.ascontiguousarray(kn_g[b].T.astype(np.float32)),
                        "augin": np.ones((64, 512), np.float32),
                        "woin": np.ascontiguousarray(wo.T.astype(np.float32)),
                        "boin": _rep(bo)})
        res = run_bass_kernel_spmd(_nc_cache[key], in2,
                                   core_ids=list(range(NCORES)))
        return unslice_tokens([res.results[c]["xout"] for c in range(NCORES)])
    xg = _gather_global([slice_tokens(xflat, c) for c in range(NCORES)], 0)
    qT_g = _gather_global(qTs, 1)
    qn_g = _gather_global(qns, 0)
    outs_g = {0: np.empty((8192, D), np.float32), 1: np.empty((8192, D), np.float32)}
    for c in range(NCORES):
        b, j = divmod(c, 4)
        qsl = slice(2048 * j, 2048 * (j + 1))
        npad = 6144 - 2048 * j
        kT_sh = np.concatenate([np.zeros((D, npad), ml_dtypes_bf16()),
                                kT_g[b][:, :2048 * (j + 1)]], axis=1)
        v_sh = np.concatenate([np.zeros((npad, D), ml_dtypes_bf16()),
                               v_g[b][:2048 * (j + 1), :]], axis=0)
        aug_np = np.concatenate([np.full(npad, 30.0, np.float32),
                                 np.ones(8192 - npad, np.float32)])
        kn_sh = np.concatenate([np.zeros((16, npad), np.float32),
                                kn_g[b].T[:, :2048 * (j + 1)].astype(np.float32)], axis=1)
        aug = np.concatenate([np.full((1, npad), 30.0, np.float32),
                              np.ones((1, 8192 - npad), np.float32)], axis=1)
        in2.append({"xin": np.ascontiguousarray(xg[b][qsl]),
                    "qTin": np.ascontiguousarray(qT_g[b][:, qsl]),
                    "kTin": np.ascontiguousarray(kT_sh),
                    "vin": np.ascontiguousarray(v_sh),
                    "qnTin": np.ascontiguousarray(qn_g[b][qsl].T),
                    "knTin": kn_sh,
                    "augin": mk_augt(aug_np),
                    "woin": np.ascontiguousarray(wo.T.astype(np.float32)),
                    "boin": _rep(bo)})
    res = run_bass_kernel_spmd(_nc_cache[key], in2, core_ids=list(range(NCORES)))
    for c in range(NCORES):
        b, j = divmod(c, 4)
        outs_g[b][2048 * j:2048 * (j + 1)] = res.results[c]["xout"]
    flat = np.empty((16384, D), np.float32)
    for b in range(2):
        flat[8192 * b:8192 * (b + 1)] = outs_g[b]
    return flat


def ml_dtypes_bf16():
    import ml_dtypes
    return ml_dtypes.bfloat16


def run_l1attn(inputs, xflat):
    return run_attn(inputs, xflat, "l1")


def run_l2attn(inputs, xflat):
    return run_attn(inputs, xflat, "l2")





def build_l1ffn_d1(p=None, xsrc=None, pref="", yout=None, lgTout=None):
    standalone = p is None
    if standalone:
        p = P()
        xin = p.dram_in("xin", (TCORE, D), F32)
        xsrc = lambda t: (xin, xin[t * 128:(t + 1) * 128, :])
        yout = p.dram_out("yout", (TCORE, D), BF16)
        lgout = p.dram_out("lgout", (TCORE, 16), F32)
    _ctx = p.stage()
    _ctx.__enter__()
    rw1in = p.dram_in(pref + "rw1in", (D, 128), F32)
    rw2in = p.dram_in(pref + "rw2in", (128, 16), F32)
    identin = p.dram_in(pref + "identin", (128, 128), F32)

    ident32 = p.sb("ident32", (128, 128), F32)
    p.dma("dsp", ident32, ident32.ap(), identin, identin.ap())
    rw1 = p.sb("rw1", (128, 8 * 128), F32)
    p.dma("dsp", rw1, rw1.ap().rearrange("p (dc n) -> p dc n", dc=8),
          rw1in, rw1in.ap().rearrange("(dc p) n -> p dc n", p=128))
    rw2 = p.sb("rw2", (128, 16), F32)
    p.dma("dsp", rw2, rw2.ap(), rw2in, rw2in.ap())

    xt = [p.sb("xt0", (128, D), F32)] * 2
    scr = p.sb("scr", (128, D), F32)
    st = p.sb("st", (128, 16), F32)
    yf = p.sb("yf", (128, D), F32)
    yb = p.sb("yb", (128, D), BF16)
    yT32 = p.sb("yT32", (128, 128), F32)
    sr = p.sb("sr", (128, 128), F32)
    srT = p.sb("srT", (128, 128), F32)
    lg = p.sb("lg", (128, 16), F32)
    lgT = p.sb("lgT", (16, TCORE), F32) if lgTout is not None else None
    tr32_ps = p.psum("tr32_ps", (128, 128), F32)
    r_ps = p.psum("r_ps", (128, 128), F32)
    l_ps = p.psum("l_ps", (128, 16), F32)
    rw1v = rw1.ap().rearrange("p (dc n) -> p dc n", dc=8)

    for t in range(NTT):
        xb = xt[t % 2]
        p.dma("dsp", xb, xb.ap(), *xsrc(t))
        mean, rstd = emit_ln_stats(p, xb, xb.ap(), scr, st, 0)
        emit_ln_apply(p, yf, yf.ap(), xb, xb.ap(), st, mean, rstd)
        p.op("dve", lambda e: e.tensor_copy(yb.ap(), yf.ap()),
             reads=[yf], writes=[yb])
        p.dma("dsp", yout, yout[t * 128:(t + 1) * 128, :], yb, yb.ap(),
              unordered_dst=True, sem_buf=yb)
        for dc in range(8):
            sl = slice(dc * 128, (dc + 1) * 128)
            p.op("pe", lambda e, sl=sl: e.transpose(tr32_ps.ap(), yf[:, sl],
                                                    ident32.ap()),
                 reads=[yf, ident32], writes=[tr32_ps])
            p.op("dve", lambda e: e.tensor_copy(yT32.ap(), tr32_ps.ap()),
                 reads=[tr32_ps], writes=[yT32])
            p.op("pe", lambda e, dc=dc: e.matmul(
                    r_ps.ap(), yT32.ap(), rw1v[:, dc, :],
                    start=(dc == 0), stop=(dc == 7)),
                 reads=[yT32, rw1], writes=[r_ps])
        p.op("act", lambda e: e.activation(sr.ap(), r_ps.ap(), AF.Silu),
             reads=[r_ps], writes=[sr])
        p.op("pe", lambda e: e.transpose(tr32_ps.ap(), sr.ap(), ident32.ap()),
             reads=[sr, ident32], writes=[tr32_ps])
        p.op("dve", lambda e: e.tensor_copy(srT.ap(), tr32_ps.ap()),
             reads=[tr32_ps], writes=[srT])
        p.op("pe", lambda e: e.matmul(l_ps.ap(), srT.ap(), rw2.ap(),
                                      start=True, stop=True),
             reads=[srT, rw2], writes=[l_ps])
        p.op("dve", lambda e: e.tensor_copy(lg.ap(), l_ps.ap()),
             reads=[l_ps], writes=[lg])
        if standalone:
            p.dma("dsp", lgout, lgout[t * 128:(t + 1) * 128, :], lg, lg.ap(),
                  unordered_dst=True, sem_buf=lg)
        else:
            p.op("pe", lambda e: e.transpose(tr32_ps.ap_[0:16, :], lg.ap(),
                                             ident32.ap()),
                 reads=[lg, ident32], writes=[tr32_ps])
            p.op("dve", lambda e, t=t: e.tensor_copy(
                    lgT.ap_[:, t * 128:(t + 1) * 128], tr32_ps.ap_[0:16, :]),
                 reads=[tr32_ps], writes=[lgT])
    if not standalone:
        p.dma("dsp", lgTout, lgTout.ap(), lgT, lgT.ap())
    _ctx.__exit__(None, None, None)
    if standalone:
        return p.finish()
    return None


def build_l1ffn_d2(p=None, pref="", lg_fn=None, yin=None, woutout=None,
                   idxout=None):
    CAP = 256
    standalone = p is None
    if standalone:
        p = P()
        lgin = p.dram_in("lgin", (4, 8192), F32)
        lg_fn = lambda pp, b: pp.dma("dsp", b, b.ap(), lgin, lgin.ap())
        yin = p.dram_in("yin", (8192, D), BF16)
        woutout = p.dram_out("woutout", (4, CAP, D), F32)
        idxout = p.dram_out("idxout", (4, CAP), U32)
    _ctx = p.stage()
    _ctx.__enter__()
    w1in = p.dram_in(pref + "w1in", (4, D, 2048), F32)
    w2in = p.dram_in(pref + "w2in", (4, D, 2048), F32)
    w3in = p.dram_in(pref + "w3in", (4, 2048, D), F32)
    identin = p.dram_in(pref + "identin2", (128, 128), F32)

    ident = p.sb("ident", (128, 128), BF16)
    p.dma("dpool", ident, ident.ap(), identin, identin.ap())
    lgs = p.sb("lgs", (4, 8192), F32)
    lg_fn(p, lgs)
    topv = p.sb("topv", (4, CAP), F32)
    topi = p.sb("topi", (4, CAP), U32)
    m8 = p.sb("m8", (4, 8), F32)
    i8 = p.sb("i8", (4, 8), U32)
    rst = p.sb("rst", (4, 8), F32)
    ew = p.sb("ew", (4, CAP), F32)
    idxi = topi
    scrE = p.sb("scrE", (4, 8192), F32)

    for it in range(32):
        p.op("dve", lambda e: e.max(out=m8.ap(), in_=lgs.ap()),
             reads=[lgs], writes=[m8])
        p.op("dve", lambda e: e.max_index(out=i8.ap(), in_max=m8.ap(),
                                          in_values=lgs.ap()),
             reads=[lgs, m8], writes=[i8])
        p.op("dve", lambda e, it=it: e.tensor_copy(topv[:, it * 8:(it + 1) * 8],
                                                   m8.ap()),
             reads=[m8, topv], writes=[topv])
        p.op("dve", lambda e, it=it: e.tensor_copy(topi[:, it * 8:(it + 1) * 8],
                                                   i8.ap()),
             reads=[i8, topi], writes=[topi])
        p.op("dve", lambda e: e.match_replace(out=lgs.ap(), in_to_replace=m8.ap(),
                                              in_values=lgs.ap(), imm_value=-1e30),
             reads=[m8, lgs], writes=[lgs])
    p.op("dve", lambda e: e.tensor_scalar_mul(rst[:, 0:1], topv[:, 0:1], -1.0),
         reads=[topv, rst], writes=[rst])
    p.op("act", lambda e: e.activation(scrE.ap(), lgs.ap(), AF.Exp,
                                       bias=rst[:, 0:1], accum_out=rst[:, 1:2]),
         reads=[lgs, rst], writes=[scrE, rst])
    p.op("act", lambda e: e.activation(ew.ap(), topv.ap(), AF.Exp,
                                       bias=rst[:, 0:1], accum_out=rst[:, 2:3]),
         reads=[topv, rst], writes=[ew, rst])
    p.op("dve", lambda e: e.tensor_add(rst[:, 3:4], rst[:, 1:2], rst[:, 2:3]),
         reads=[rst], writes=[rst])
    p.op("dve", lambda e: e.scalar_tensor_tensor(rst[:, 4:5], rst[:, 3:4], 1e-9,
                                                 rst[:, 2:3], op0=ALU.mult,
                                                 op1=ALU.add),
         reads=[rst], writes=[rst])
    p.op("dve", lambda e: e.reciprocal(rst[:, 4:5], rst[:, 4:5]),
         reads=[rst], writes=[rst])
    p.op("dve", lambda e: e.tensor_scalar_mul(ew.ap(), ew.ap(), rst[:, 4:5]),
         reads=[rst, ew], writes=[ew])
    p.dma("dsp", idxout, idxout.ap(), idxi, idxi.ap())
    ewd = p.dram_tmp("ewd", (4, CAP), F32)
    p.dma("dsp", ewd, ewd.ap(), ew, ew.ap())

    ein = p.sb("ein", (128, 2 * D), BF16)
    einT = p.sb("einT", (128, 8 * 256), BF16)
    gT = p.sb("gT", (128, 8 * 256), BF16)
    w1b = p.sb("w1b", (128, 64 * 128), BF16)
    w2b = p.sb("w2b", (128, 64 * 128), BF16)
    w3b = p.sb("w3b", (128, 8 * D), BF16)
    wout = p.sb("wout", (128, 2 * D), F32)
    wcol = p.sb("wcol", (128, 2), F32)
    idx128 = p.sb("idx128", (128, 2), U32)
    tr_ps = p.psum("tr_ps", (128, 1024), BF16)
    h_ps = p.psum("h_ps", (128, 256), F32)
    h2_ps = p.psum("h2_ps", (128, 256), F32)
    o_ps = [p.psum(f"o_ps{i}", (128, D), F32) for i in range(2)]

    einv = ein.ap().rearrange("p (rc d) -> p rc d", rc=2)
    einTv = einT.ap().rearrange("p (dc r) -> p dc r", dc=8)
    gTv = gT.ap().rearrange("p (fc r) -> p fc r", fc=8)
    w1v = w1b.ap().rearrange("p (dc fc j) -> p dc fc j", dc=8, fc=8)
    w2v = w2b.ap().rearrange("p (dc fc j) -> p dc fc j", dc=8, fc=8)
    w3v = w3b.ap().rearrange("p (fc d) -> p fc d", fc=8)

    for pr in range(4):
        p.dma("dsp", idx128, idx128.ap(),
              idxout, idxout.ap()[pr].rearrange("(c p) -> p c", p=128),
              allow_slow_non_contiguous=True)
        p.dma("dsp", wcol, wcol.ap(),
              ewd, ewd.ap()[pr].rearrange("(c p) -> p c", p=128),
              allow_slow_non_contiguous=True)
        for rc in range(2):
            p.op("pool", lambda e, rc=rc: e.indirect_dma_start(
                    einv[:, rc, :], None, yin.ap(),
                    bass.IndirectOffsetOnAxis(idx128[:, rc:rc + 1], 0),
                    bounds_check=8191, oob_is_err=False),
                 reads=[yin, idx128], writes=[ein], dma16=True)
        for rc in range(2):
            for dc in range(8):
                p.op("pe", lambda e, rc=rc, dc=dc: e.transpose(
                        tr_ps.ap_[:, dc * 128:(dc + 1) * 128],
                        einv[:, rc, dc * 128:(dc + 1) * 128], ident.ap()),
                     reads=[ein, ident], writes=[tr_ps])
            p.op("act", lambda e, rc=rc: e.copy(
                    einTv[:, :, rc * 128:(rc + 1) * 128],
                    tr_ps.ap().rearrange("p (dc j) -> p dc j", dc=8)),
                 reads=[tr_ps], writes=[einT])
        for half in range(2):
            fsl = slice(half * 1024, (half + 1) * 1024)
            p.dma("dpool", w1b, w1v,
                  w1in, w1in[pr, :, fsl].rearrange("(dc p) (fc j) -> p dc fc j",
                                                   p=128, j=128))
            p.dma("dpool", w2b, w2v,
                  w2in, w2in[pr, :, fsl].rearrange("(dc p) (fc j) -> p dc fc j",
                                                   p=128, j=128))
            p.dma("dpool", w3b, w3v,
                  w3in, w3in[pr, fsl, :].rearrange("(fc p) d -> p fc d", p=128))
            for fc in range(8):
                for dc in range(8):
                    p.op("pe", lambda e, fc=fc, dc=dc: e.matmul(
                            h_ps.ap(), w1v[:, dc, fc, :], einTv[:, dc, :],
                            start=(dc == 0), stop=(dc == 7)),
                         reads=[w1b, einT], writes=[h_ps])
                p.op("act", lambda e, fc=fc: e.activation(
                        gTv[:, fc, :], h_ps.ap(), AF.Silu),
                     reads=[h_ps], writes=[gT])
                for dc in range(8):
                    p.op("pe", lambda e, fc=fc, dc=dc: e.matmul(
                            h2_ps.ap(), w2v[:, dc, fc, :], einTv[:, dc, :],
                            start=(dc == 0), stop=(dc == 7)),
                         reads=[w2b, einT], writes=[h2_ps])
                p.op("dve", lambda e, fc=fc: e.tensor_mul(
                        gTv[:, fc, :], gTv[:, fc, :], h2_ps.ap()),
                     reads=[h2_ps, gT], writes=[gT])
            for rc in range(2):
                for nh in range(2):
                    ns = slice(nh * 512, (nh + 1) * 512)
                    for fc in range(8):
                        p.op("pe", lambda e, rc=rc, nh=nh, ns=ns, fc=fc, half=half: e.matmul(
                                o_ps[rc][:, ns], gTv[:, fc, rc * 128:(rc + 1) * 128],
                                w3v[:, fc, ns],
                                start=(half == 0 and fc == 0),
                                stop=(half == 1 and fc == 7)),
                             reads=[gT, w3b], writes=[o_ps[rc]])
        for rc in range(2):
            p.op("dve", lambda e, rc=rc: e.tensor_scalar_mul(
                    wout.ap()[:, rc * D:(rc + 1) * D], o_ps[rc].ap(),
                    wcol[:, rc:rc + 1]),
                 reads=[o_ps[rc], wcol], writes=[wout])
        p.dma("dsp", woutout, woutout.ap()[pr].rearrange("(rc p) d -> p rc d", p=128),
              wout, wout.ap().rearrange("p (rc d) -> p rc d", rc=2),
              unordered_dst=True, sem_buf=wout)
    _ctx.__exit__(None, None, None)
    if standalone:
        return p.finish()
    return None


def build_l1ffn_d3(p=None, xsrc=None, xdst=None, pref="", winsin=None,
                   idxsin=None):
    CAP = 256
    standalone = p is None
    if standalone:
        p = P()
        xin = p.dram_in("xin", (TCORE, D), F32)
        xsrc = lambda t: (xin, xin[t * 128:(t + 1) * 128, :])
        xout_t = p.dram_out("xout", (TCORE, D), F32)
        xdst = lambda t: (xout_t, xout_t[t * 128:(t + 1) * 128, :])
        winsin = p.dram_in("winsin", (16, CAP, D), F32)
        idxsin = p.dram_in("idxsin", (16, CAP), U32)
    _ctx = p.stage()
    _ctx.__enter__()
    basesin = p.dram_in(pref + "basesin", (16, 2), U32)
    accs = [p.dram_tmp(pref + "accA", (1024, D), F32),
            p.dram_tmp(pref + "accB", (1024, D), F32)]

    idxs = p.sb("idxs", (16, CAP), U32)
    p.dma("dsp", idxs, idxs.ap(), idxsin, idxsin.ap())
    bases = p.sb("bases", (16, 2), U32)
    p.dma("dsp", bases, bases.ap(), basesin, basesin.ap())
    sh = p.sb("sh", (16, CAP), U32)
    shd = p.dram_tmp("shd", (16, CAP), U32)
    idx128 = p.sb("idx128", (128, 2), U32)
    wsb = p.sb("wsb", (128, D), F32)
    zt = p.sb("zt", (128, D), F32)

    bndreg = p.eng["pool"].alloc_register("bnd")
    p.eng["pool"].reg_mov(bndreg, 1023)
    p.op("dve", lambda e: e.memset(zt.ap(), 0.0), reads=[], writes=[zt])
    zev = None
    for t in range(NTT):
        acc = accs[t // 8]
        zev = p.dma("dsp", acc, acc[(t % 8) * 128:(t % 8 + 1) * 128, :],
                    zt, zt.ap(), unordered_dst=True, sem_buf=zt)
    for acc in accs:
        acc.w_ev = zev
    for rng in range(2):
        p.op("dve", lambda e, rng=rng: e.tensor_tensor(
                sh.ap(), idxs.ap(),
                bases[:, rng:rng + 1].to_broadcast((16, CAP)),
                op=ALU.subtract),
             reads=[idxs, bases], writes=[sh])
        p.dma("dsp", shd, shd.ap(), sh, sh.ap())
        for pair in range(16):
            for rc in range(2):
                p.dma("dsp", idx128, idx128.ap()[:, rc:rc + 1],
                      shd, shd.ap()[pair, rc * 128:(rc + 1) * 128]
                      .rearrange("(p o) -> p o", o=1))
                p.dma("dsp", wsb, wsb.ap(),
                      winsin, winsin[pair, rc * 128:(rc + 1) * 128, :])
                p.op("pool", lambda e, rng=rng: e.indirect_dma_start(
                        accs[rng].ap(),
                        bass.IndirectOffsetOnAxis(idx128[:, rc:rc + 1], 0),
                        wsb.ap(), None,
                        bounds_check=bndreg, oob_is_err=False,
                        compute_op=ALU.add),
                     reads=[wsb, idx128, accs[rng]], writes=[accs[rng]], dma16=True)
    xt = [p.sb("xt0", (128, D), F32)] * 2
    at = [p.sb("at0", (128, D), F32)] * 2
    scr = p.sb("scr", (128, D), F32)
    st = p.sb("st", (128, 16), F32)
    ob = p.sb("ob", (128, D), F32)
    for t in range(NTT):
        xb, ab = xt[t % 2], at[t % 2]
        p.dma("dsp", xb, xb.ap(), *xsrc(t))
        p.dma("dsp", ab, ab.ap(), accs[t // 8],
              accs[t // 8][(t % 8) * 128:(t % 8 + 1) * 128, :])
        mean2, rstd2 = emit_ln_stats(p, ab, ab.ap(), scr, st, 8)
        p.op("dve", lambda e: e.tensor_scalar(
                scr.ap(), ab.ap(), st[:, 8:9], st[:, 11:12],
                op0=ALU.subtract, op1=ALU.mult),
             reads=[ab, st], writes=[scr])
        p.op("dve", lambda e: e.tensor_add(ob.ap(), xb.ap(), scr.ap()),
             reads=[xb, scr], writes=[ob])
        dbuf, dap = xdst(t)
        p.dma("dsp", dbuf, dap, ob, ob.ap(), unordered_dst=True, sem_buf=ob)
    _ctx.__exit__(None, None, None)
    if standalone:
        return p.finish()
    return None


def build_l1ffn_f(p=None, xsrc=None, xdst=None, pref=""):
    CAP = 256
    y_c = p.dram_tmp(pref + "y_c", (TCORE, D), BF16)
    y_ag = p.dram_tmp(pref + "y_ag", (4 * TCORE, D), BF16)
    lgT_c = p.dram_tmp(pref + "lgT_c", (16, TCORE), F32)
    lgT_ag = p.dram_tmp(pref + "lgT_ag", (4 * 16, TCORE), F32)
    wout_c = p.dram_tmp(pref + "wout_c", (4, CAP, D), F32)
    idx_c = p.dram_tmp(pref + "idx_c", (4, CAP), U32)
    wout_ag = p.dram_tmp(pref + "wout_ag", (16 * CAP, D), F32)
    idx_ag = p.dram_tmp(pref + "idx_ag", (16, CAP), U32)
    groups = [[0, 1, 2, 3], [4, 5, 6, 7]]

    build_l1ffn_d1(p=p, xsrc=xsrc, pref=pref, yout=y_c, lgTout=lgT_c)
    p.cc("AllGather", ALU.bypass, groups, y_c, y_c.ap(), y_ag, y_ag.ap())
    p.cc("AllGather", ALU.bypass, groups, lgT_c, lgT_c.ap(), lgT_ag, lgT_ag.ap())

    ebin = p.dram_in(pref + "ebase", (1, 1), I32)
    ebs = p.sb("ebs", (1, 1), I32)
    p.dma("dsp", ebs, ebs.ap(), ebin, ebin.ap())
    ereg = p.sval("sp", ebs, ebs[0:1, 0:1])

    def lg_fn(pp, lgs_buf):
        v = lgT_ag.ap().rearrange("(r e) t -> e r t", r=4)
        pp.dma("dsp", lgs_buf,
               lgs_buf.ap().rearrange("e (r t) -> e r t", r=4),
               lgT_ag, v[bass.ts(ereg, 4), :, :])

    build_l1ffn_d2(p=p, pref=pref, lg_fn=lg_fn, yin=y_ag, woutout=wout_c,
                   idxout=idx_c)
    p.cc("AllGather", ALU.bypass, groups, wout_c,
         wout_c.ap().rearrange("q c d -> (q c) d"), wout_ag, wout_ag.ap())
    p.cc("AllGather", ALU.bypass, groups, idx_c, idx_c.ap(), idx_ag, idx_ag.ap())
    wins_view = Buf(wout_ag.name,
                    wout_ag.ap().rearrange("(q c) d -> q c d", c=CAP))
    wins_view.w_ev = wout_ag.w_ev
    build_l1ffn_d3(p=p, xsrc=xsrc, xdst=xdst, pref=pref,
                   winsin=wins_view, idxsin=idx_ag)
    return None


def prep_l1ffn_f(inputs, xflat, c, pref=""):
    b, j = divmod(c, 4)
    es = slice(4 * j, 4 * j + 4)
    rbase = j * TCORE
    bases = np.broadcast_to(
        np.array([rbase, rbase + 1024], np.uint32), (16, 2)).copy()
    d = {"rw1in": inputs["L1_ffn_router_w1"].astype(np.float32),
         "rw2in": inputs["L1_ffn_router_w2"].astype(np.float32),
         "identin": np.eye(128, dtype=np.float32),
         "identin2": np.eye(128, dtype=np.float32),
         "w1in": inputs["L1_ffn_w1"][es].astype(np.float32),
         "w2in": inputs["L1_ffn_w2"][es].astype(np.float32),
         "w3in": inputs["L1_ffn_w3"][es].astype(np.float32),
         "ebase": np.array([[j]], np.int32),
         "basesin": bases}
    return {pref + k: v for k, v in d.items()}


def run_l1ffn(inputs, xflat):
    for k, b in (("l1ffn_d1", build_l1ffn_d1), ("l1ffn_d2", build_l1ffn_d2),
                 ("l1ffn_d3", build_l1ffn_d3)):
        if k not in _nc_cache:
            _nc_cache[k] = b()
    ident = np.eye(128, dtype=np.float32)
    in1 = [{"xin": slice_tokens(xflat, c),
            "rw1in": inputs["L1_ffn_router_w1"].astype(np.float32),
            "rw2in": inputs["L1_ffn_router_w2"].astype(np.float32),
            "identin": ident} for c in range(NCORES)]
    r1 = run_bass_kernel_spmd(_nc_cache["l1ffn_d1"], in1,
                              core_ids=list(range(NCORES)))
    y_g = _gather_global([r1.results[c]["yout"] for c in range(NCORES)], 0)
    lg_g = _gather_global([r1.results[c]["lgout"] for c in range(NCORES)], 0)
    in2 = []
    for c in range(NCORES):
        b, j = divmod(c, 4)
        es = slice(4 * j, 4 * j + 4)
        in2.append({"lgin": np.ascontiguousarray(lg_g[b].T[es].astype(np.float32)),
                    "yin": y_g[b],
                    "w1in": inputs["L1_ffn_w1"][es].astype(np.float32),
                    "w2in": inputs["L1_ffn_w2"][es].astype(np.float32),
                    "w3in": inputs["L1_ffn_w3"][es].astype(np.float32),
                    "identin": ident})
    r2 = run_bass_kernel_spmd(_nc_cache["l1ffn_d2"], in2,
                              core_ids=list(range(NCORES)))
    in3 = []
    for c in range(NCORES):
        b, j = divmod(c, 4)
        wins = np.concatenate([r2.results[4 * b + jj]["woutout"]
                               for jj in range(4)], axis=0)
        idxs = np.concatenate([r2.results[4 * b + jj]["idxout"]
                               for jj in range(4)], axis=0)
        (r1_, n1), (r2_, n2) = core_row_slices(c)
        bases = np.broadcast_to(
            np.array([r1_ - 8192 * b, r2_ - 8192 * b], np.uint32), (16, 2)).copy()
        in3.append({"xin": slice_tokens(xflat, c),
                    "winsin": wins, "idxsin": idxs, "basesin": bases})
    r3 = run_bass_kernel_spmd(_nc_cache["l1ffn_d3"], in3,
                              core_ids=list(range(NCORES)))
    return unslice_tokens([r3.results[c]["xout"] for c in range(NCORES)])





def build_l3attn_g1(p=None, xsrc=None, pref="", nxout=None, plout=None,
                    lamout=None):
    standalone = p is None
    if standalone:
        p = P()
        xin = p.dram_in("xin", (TCORE, D), F32)
        xsrc = lambda t: (xin, xin[t * 128:(t + 1) * 128, :])
        nxout = p.dram_out("nxout", (128, 8 * TCORE), F32)
        plout = p.dram_out("plout", (128, 8 * TCORE), F32)
        lamout = p.dram_out("lamout", (128, 8 * 2), F32)
    _ctx = p.stage()
    _ctx.__enter__()
    igwin = p.dram_in(pref + "igwin", (8, 128, 128), F32)
    igbin = p.dram_in(pref + "igbin", (128, 8), F32)
    agwin = p.dram_in(pref + "agwin", (8, 128, 128), F32)
    agbin = p.dram_in(pref + "agbin", (128, 8), F32)
    apin = p.dram_in(pref + "apin", (128, 8), F32)
    identin = p.dram_in(pref + "identin", (128, 128), F32)

    ident = p.sb("ident", (128, 128), BF16)
    p.dma("dpool", ident, ident.ap(), identin, identin.ap())
    igw = p.sb("igw", (128, 8 * 128), BF16)
    p.dma("dpool", igw, igw.ap().rearrange("p (h n) -> p h n", h=8),
          igwin, igwin.ap().rearrange("h p n -> p h n"))
    agw = p.sb("agw", (128, 8 * 128), BF16)
    p.dma("dpool", agw, agw.ap().rearrange("p (h n) -> p h n", h=8),
          agwin, agwin.ap().rearrange("h p n -> p h n"))
    igb = p.sb("igb", (128, 8), F32)
    p.dma("dsp", igb, igb.ap(), igbin, igbin.ap())
    agb = p.sb("agb", (128, 8), F32)
    p.dma("dsp", agb, agb.ap(), agbin, agbin.ap())
    ap_ = p.sb("ap_", (128, 8), F32)
    p.dma("dsp", ap_, ap_.ap(), apin, apin.ap())

    sp = p.sb("sp", (128, 8), F32)
    spt = p.sb("spt", (128, 8), F32)
    p.op("act", lambda e: e.activation(spt.ap(), ap_.ap(), AF.Abs),
         reads=[ap_], writes=[spt])
    p.op("act", lambda e: e.activation(spt.ap(), spt.ap(), AF.Exp, scale=-1.0),
         reads=[spt], writes=[spt])
    p.op("act", lambda e: e.activation(spt.ap(), spt.ap(), AF.Ln, bias=1.0),
         reads=[spt], writes=[spt])
    p.op("dve", lambda e: e.tensor_scalar_max(sp.ap(), ap_.ap(), 0.0),
         reads=[ap_], writes=[sp])
    p.op("dve", lambda e: e.tensor_add(sp.ap(), sp.ap(), spt.ap()),
         reads=[spt, sp], writes=[sp])
    p.op("dve", lambda e: e.tensor_scalar_mul(sp.ap(), sp.ap(), -8.0),
         reads=[sp], writes=[sp])

    xt = [p.sb("xt0", (128, D), F32)] * 2
    scr = p.sb("scr", (128, D), F32)
    st = p.sb("st", (128, 16), F32)
    yf = p.sb("yf", (128, D), F32)
    yb = p.sb("yb", (128, D), BF16)
    yT = p.sb("yT", (128, 8 * 512), BF16)
    ga = p.sb("ga", (128, 512), F32)
    gx = p.sb("gx", (128, 512), F32)
    la = p.sb("la", (128, 512), F32)
    mlt = p.sb("mlt", (128, 512), F32)
    nx = p.sb("nx", (128, 512), F32)
    dtl = p.sb("dtl", (128, 512), F32)
    pl = p.sb("pl", (128, 512), F32)
    zz = p.sb("zz", (128, 512), F32)
    carA = p.sb("carA", (128, 8), F32)
    carP = p.sb("carP", (128, 8), F32)
    tr_ps = p.psum("tr_ps", (128, 1024), BF16)
    g_ps = p.psum("g_ps", (128, 512), F32)
    g2_ps = p.psum("g2_ps", (128, 512), F32)

    p.op("dve", lambda e: e.memset(zz.ap(), 0.0), reads=[], writes=[zz])
    yTv = yT.ap().rearrange("p (dc t) -> p dc t", dc=8)
    nxo = nxout.ap().rearrange("p (dc t) -> p dc t", dc=8)
    plo = plout.ap().rearrange("p (dc t) -> p dc t", dc=8)
    igwv = igw.ap().rearrange("p (h n) -> p h n", h=8)
    agwv = agw.ap().rearrange("p (h n) -> p h n", h=8)

    for blk in range(4):
        chunk_start = (blk % 2 == 0)
        for tc in range(4):
            t = blk * 4 + tc
            xb = xt[t % 2]
            p.dma("dsp", xb, xb.ap(), *xsrc(t))
            mean, rstd = emit_ln_stats(p, xb, xb.ap(), scr, st, 0)
            emit_ln_apply(p, yf, yf.ap(), xb, xb.ap(), st, mean, rstd)
            p.op("dve", lambda e: e.tensor_copy(yb.ap(), yf.ap()),
                 reads=[yf], writes=[yb])
            for dc in range(8):
                sl = slice(dc * 128, (dc + 1) * 128)
                p.op("pe", lambda e, sl=sl: e.transpose(tr_ps[:, sl], yb[:, sl],
                                                        ident.ap()),
                     reads=[yb, ident], writes=[tr_ps])
            p.op("act", lambda e, tc=tc: e.copy(
                    yTv[:, :, tc * 128:(tc + 1) * 128],
                    tr_ps.ap().rearrange("p (dc j) -> p dc j", dc=8)),
                 reads=[tr_ps], writes=[yT])
        for dc in range(8):
            p.op("pe", lambda e, dc=dc: e.matmul(
                    g_ps.ap(), igwv[:, dc, :], yTv[:, dc, :],
                    start=True, stop=True),
                 reads=[igw, yT], writes=[g_ps])
            p.op("act", lambda e, dc=dc: e.activation(
                    gx.ap(), g_ps.ap(), AF.Sigmoid, bias=igb[:, dc:dc + 1]),
                 reads=[g_ps, igb], writes=[gx])
            p.op("pe", lambda e, dc=dc: e.matmul(
                    g2_ps.ap(), agwv[:, dc, :], yTv[:, dc, :],
                    start=True, stop=True),
                 reads=[agw, yT], writes=[g2_ps])
            p.op("act", lambda e, dc=dc: e.activation(
                    ga.ap(), g2_ps.ap(), AF.Sigmoid, bias=agb[:, dc:dc + 1]),
                 reads=[g2_ps, agb], writes=[ga])
            p.op("dve", lambda e, dc=dc: e.tensor_scalar_mul(
                    la.ap(), ga.ap(), sp[:, dc:dc + 1]),
                 reads=[ga, sp], writes=[la])
            p.op("act", lambda e: e.activation(mlt.ap(), la.ap(), AF.Exp,
                                               scale=2.0),
                 reads=[la], writes=[mlt])
            p.op("act", lambda e: e.activation(la.ap(), la.ap(), AF.Exp),
                 reads=[la], writes=[la])
            p.op("dve", lambda e: e.tensor_scalar(mlt.ap(), mlt.ap(), -1.0, 1.0,
                                                  op0=ALU.mult, op1=ALU.add),
                 reads=[mlt], writes=[mlt])
            p.op("dve", lambda e: e.tensor_scalar_max(mlt.ap(), mlt.ap(), 1e-7),
                 reads=[mlt], writes=[mlt])
            p.op("act", lambda e: e.activation(mlt.ap(), mlt.ap(), AF.Sqrt),
                 reads=[mlt], writes=[mlt])
            p.op("dve", lambda e, dc=dc: e.tensor_tensor(
                    nx.ap(), yTv[:, dc, :], gx.ap(), op=ALU.mult),
                 reads=[yT, gx], writes=[nx])
            p.op("dve", lambda e: e.tensor_mul(nx.ap(), nx.ap(), mlt.ap()),
                 reads=[mlt, nx], writes=[nx])
            p.dma("dsp", nxout, nxo[:, dc, blk * 512:(blk + 1) * 512],
                  nx, nx.ap(), unordered_dst=True, sem_buf=nx)
            p.op("dve", lambda e: e.tensor_copy(dtl[:, 1:512], la[:, 0:511]),
                 reads=[la], writes=[dtl])
            if chunk_start:
                p.op("dve", lambda e: e.memset(dtl[:, 0:1], 1.0),
                     reads=[], writes=[dtl])
            else:
                p.op("dve", lambda e, dc=dc: e.tensor_copy(dtl[:, 0:1],
                                                           carA[:, dc:dc + 1]),
                     reads=[carA], writes=[dtl])
            if chunk_start:
                p.op("dve", lambda e: e.tensor_tensor_scan(
                        pl.ap(), dtl.ap(), zz.ap(), 1.0,
                        op0=ALU.mult, op1=ALU.add),
                     reads=[dtl, zz], writes=[pl])
            else:
                p.op("dve", lambda e, dc=dc: e.tensor_tensor_scan(
                        pl.ap(), dtl.ap(), zz.ap(), carP[:, dc:dc + 1],
                        op0=ALU.mult, op1=ALU.add),
                     reads=[dtl, zz, carP], writes=[pl])
            p.dma("dsp", plout, plo[:, dc, blk * 512:(blk + 1) * 512],
                  pl, pl.ap(), unordered_dst=True, sem_buf=pl)
            p.op("dve", lambda e, dc=dc: e.tensor_tensor(
                    carP[:, dc:dc + 1], pl[:, 511:512], la[:, 511:512],
                    op=ALU.mult),
                 reads=[pl, la, carP], writes=[carP])
            p.op("dve", lambda e, dc=dc: e.tensor_copy(carA[:, dc:dc + 1],
                                                       la[:, 511:512]),
                 reads=[la, carA], writes=[carA])
        if blk % 2 == 1:
            ch = blk // 2
            lamv = lamout.ap().rearrange("p (dc c) -> p dc c", dc=8)
            p.dma("dsp", lamout, lamv[:, :, ch:ch + 1],
                  carP, carP.ap().rearrange("p (dc o) -> p dc o", o=1),
                  unordered_dst=True, sem_buf=carP,
                  allow_slow_non_contiguous=True)
    _ctx.__exit__(None, None, None)
    if standalone:
        return p.finish()
    return None


def build_l3attn_g2(p=None, pref="", nxin=None, plin=None, lam_fn=None,
                    a1out=None, c2out=None, totout=None):
    standalone = p is None
    if standalone:
        p = P()
        nxin = p.dram_in("nxin", (128, 8 * TCORE), F32)
        plin = p.dram_in("plin", (128, 8 * TCORE), F32)
        lamin = p.dram_in("lamin", (128, 8 * 8), F32)
        lam_fn = lambda pp, lam_buf: pp.dma("dsp", lam_buf, lam_buf.ap(),
                                            lamin, lamin.ap())
        a1out = p.dram_out("a1out", (128, 8 * TCORE), F32)
        c2out = p.dram_out("c2out", (128, 8 * TCORE), F32)
        totout = p.dram_out("totout", (128, 8 * 2), F32)
    _ctx = p.stage()
    _ctx.__enter__()
    mskin = p.dram_in(pref + "mskin", (128, 2 * 8), F32)

    lam = p.sb("lam", (128, 64), F32)
    lam_fn(p, lam)
    msk = p.sb("msk", (128, 16), F32)
    p.dma("dsp", msk, msk.ap(), mskin, mskin.ap())
    pin = p.sb("pin", (128, 8 * 2), F32)
    invp = p.sb("invp", (128, 8 * 2), F32)
    tht = p.sb("tht", (128, 8 * 2), F32)
    tmp8 = p.sb("tmp8", (128, 8), F32)
    tot = p.sb("tot", (128, 8 * 2), F32)
    lamv = lam.ap().rearrange("p (dc g) -> p dc g", dc=8)
    mskv = msk.ap().rearrange("p (lc g) -> p lc g", lc=2)
    pinv = pin.ap().rearrange("p (dc lc) -> p dc lc", dc=8)
    invpv = invp.ap().rearrange("p (dc lc) -> p dc lc", dc=8)
    thtv = tht.ap().rearrange("p (dc lc) -> p dc lc", dc=8)
    totv = tot.ap().rearrange("p (dc lc) -> p dc lc", dc=8)

    for lc in range(2):
        p.op("dve", lambda e, lc=lc: e.memset(pinv[:, :, lc], 1.0),
             reads=[], writes=[pin])
        for g in range(8):
            p.op("dve", lambda e, lc=lc, g=g: e.scalar_tensor_tensor(
                    tmp8.ap(), lamv[:, :, g], msk[:, lc * 8 + g:lc * 8 + g + 1],
                    tmp8.ap(), op0=ALU.mult, op1=ALU.bypass),
                 reads=[lam, msk], writes=[tmp8])
            p.op("dve", lambda e: e.tensor_scalar_add(tmp8.ap(), tmp8.ap(), 1.0),
                 reads=[tmp8], writes=[tmp8])
            p.op("dve", lambda e, lc=lc, g=g: e.tensor_scalar(
                    tmp8.ap(), tmp8.ap(), msk[:, lc * 8 + g:lc * 8 + g + 1], None,
                    op0=ALU.subtract),
                 reads=[tmp8, msk], writes=[tmp8])
            p.op("dve", lambda e, lc=lc: e.tensor_mul(pinv[:, :, lc],
                                                      pinv[:, :, lc], tmp8.ap()),
                 reads=[tmp8, pin], writes=[pin])
    p.op("dve", lambda e: e.tensor_scalar_max(invp.ap(), pin.ap(), 1e-30),
         reads=[pin], writes=[invp])
    p.op("dve", lambda e: e.reciprocal(invp.ap(), invp.ap()),
         reads=[invp], writes=[invp])
    p.op("dve", lambda e: e.tensor_scalar_mul(tht.ap(), invp.ap(), 1e-7),
         reads=[invp], writes=[tht])

    nxv = nxin.ap().rearrange("p (dc t) -> p dc t", dc=8)
    plv = plin.ap().rearrange("p (dc t) -> p dc t", dc=8)
    a1o = a1out.ap().rearrange("p (dc t) -> p dc t", dc=8)
    c2o = c2out.ap().rearrange("p (dc t) -> p dc t", dc=8)
    nxb = p.sb("nxb", (128, 512), F32)
    plb = p.sb("plb", (128, 512), F32)
    mb = p.sb("mb", (128, 512), F32)
    q1 = p.sb("q1", (128, 512), F32)
    a1 = p.sb("a1", (128, 512), F32)
    c2 = p.sb("c2", (128, 512), F32)
    zz = p.sb("zz", (128, 512), F32)
    carA1 = p.sb("carA1", (128, 8), F32)
    carC2 = p.sb("carC2", (128, 8), F32)
    p.op("dve", lambda e: e.memset(zz.ap(), 0.0), reads=[], writes=[zz])

    for blk in range(4):
        lc = blk // 2
        chunk_start = (blk % 2 == 0)
        for dc in range(8):
            p.dma("dsp", nxb, nxb.ap(), nxin, nxv[:, dc, blk * 512:(blk + 1) * 512])
            p.dma("dsp", plb, plb.ap(), plin, plv[:, dc, blk * 512:(blk + 1) * 512])
            col = dc * 2 + lc
            p.op("dve", lambda e, col=col: e.tensor_single_scalar(
                    mb.ap(), plb.ap(), tht[:, col:col + 1], op=ALU.is_ge),
                 reads=[plb, tht], writes=[mb])
            p.op("dve", lambda e: e.tensor_scalar_max(q1.ap(), plb.ap(), 1e-30),
                 reads=[plb], writes=[q1])
            p.op("dve", lambda e: e.reciprocal(q1.ap(), q1.ap()),
                 reads=[q1], writes=[q1])
            p.op("dve", lambda e: e.tensor_mul(q1.ap(), q1.ap(), nxb.ap()),
                 reads=[nxb, q1], writes=[q1])
            p.op("dve", lambda e: e.tensor_mul(q1.ap(), q1.ap(), mb.ap()),
                 reads=[mb, q1], writes=[q1])
            if chunk_start:
                p.op("dve", lambda e: e.tensor_tensor_scan(
                        a1.ap(), q1.ap(), zz.ap(), 0.0, op0=ALU.add, op1=ALU.add),
                     reads=[q1, zz], writes=[a1])
            else:
                p.op("dve", lambda e, dc=dc: e.tensor_tensor_scan(
                        a1.ap(), q1.ap(), zz.ap(), carA1[:, dc:dc + 1],
                        op0=ALU.add, op1=ALU.add),
                     reads=[q1, zz, carA1], writes=[a1])
            p.op("dve", lambda e, dc=dc: e.tensor_copy(carA1[:, dc:dc + 1],
                                                       a1[:, 511:512]),
                 reads=[a1, carA1], writes=[carA1])
            p.dma("dsp", a1out, a1o[:, dc, blk * 512:(blk + 1) * 512],
                  a1, a1.ap(), unordered_dst=True, sem_buf=a1)
            p.op("dve", lambda e: e.tensor_mul(mb.ap(), mb.ap(), nxb.ap()),
                 reads=[nxb, mb], writes=[mb])
            p.op("dve", lambda e: e.tensor_sub(mb.ap(), nxb.ap(), mb.ap()),
                 reads=[nxb, mb], writes=[mb])
            if chunk_start:
                p.op("dve", lambda e: e.tensor_tensor_scan(
                        c2.ap(), mb.ap(), zz.ap(), 0.0, op0=ALU.add, op1=ALU.add),
                     reads=[mb, zz], writes=[c2])
            else:
                p.op("dve", lambda e, dc=dc: e.tensor_tensor_scan(
                        c2.ap(), mb.ap(), zz.ap(), carC2[:, dc:dc + 1],
                        op0=ALU.add, op1=ALU.add),
                     reads=[mb, zz, carC2], writes=[c2])
            p.op("dve", lambda e, dc=dc: e.tensor_copy(carC2[:, dc:dc + 1],
                                                       c2[:, 511:512]),
                 reads=[c2, carC2], writes=[carC2])
            p.dma("dsp", c2out, c2o[:, dc, blk * 512:(blk + 1) * 512],
                  c2, c2.ap(), unordered_dst=True, sem_buf=c2)
            if blk % 2 == 1:
                p.op("dve", lambda e, dc=dc, col=col: e.tensor_tensor(
                        tot[:, col:col + 1],
                        carA1[:, dc:dc + 1], invp[:, col:col + 1], op=ALU.mult),
                     reads=[carA1, invp, tot], writes=[tot])
                p.op("dve", lambda e, dc=dc, col=col: e.scalar_tensor_tensor(
                        tot[:, col:col + 1], carC2[:, dc:dc + 1], 1e7,
                        tot[:, col:col + 1], op0=ALU.mult, op1=ALU.add),
                     reads=[carC2, tot], writes=[tot])
    p.dma("dsp", totout, totout.ap(), tot, tot.ap())
    _ctx.__exit__(None, None, None)
    if standalone:
        return p.finish()
    return None


def build_l3attn_g3(p=None, xsrc=None, xdst=None, pref="", a1in=None,
                    c2in=None, plin=None, lam_fn=None, tot_fn=None):
    standalone = p is None
    if standalone:
        p = P()
        xin = p.dram_in("xin", (TCORE, D), F32)
        xsrc = lambda t: (xin, xin[t * 128:(t + 1) * 128, :])
        xout_t = p.dram_out("xout", (TCORE, D), F32)
        xdst = lambda t: (xout_t, xout_t[t * 128:(t + 1) * 128, :])
        a1in = p.dram_in("a1in", (128, 8 * TCORE), F32)
        c2in = p.dram_in("c2in", (128, 8 * TCORE), F32)
        plin = p.dram_in("plin", (128, 8 * TCORE), F32)
        lamin = p.dram_in("lamin", (128, 8 * 8), F32)
        totin = p.dram_in("totin", (128, 8 * 8), F32)
        lam_fn = lambda pp, b: pp.dma("dsp", b, b.ap(), lamin, lamin.ap())
        tot_fn = lambda pp, b: pp.dma("dsp", b, b.ap(), totin, totin.ap())
    _ctx = p.stage()
    _ctx.__enter__()
    mskin = p.dram_in(pref + "mskin3", (128, 2 * 8), F32)
    identin = p.dram_in(pref + "identin3", (128, 128), F32)

    ident32 = p.sb("ident32", (128, 128), F32)
    p.dma("dsp", ident32, ident32.ap(), identin, identin.ap())
    lam = p.sb("lam", (128, 64), F32)
    lam_fn(p, lam)
    tts = p.sb("tts", (128, 64), F32)
    tot_fn(p, tts)
    msk = p.sb("msk", (128, 16), F32)
    p.dma("dsp", msk, msk.ap(), mskin, mskin.ap())
    pin = p.sb("pin", (128, 16), F32)
    invp = p.sb("invp", (128, 16), F32)
    sin = p.sb("sin", (128, 16), F32)
    tmp8 = p.sb("tmp8", (128, 8), F32)
    lamv = lam.ap().rearrange("p (dc g) -> p dc g", dc=8)
    ttsv = tts.ap().rearrange("p (dc g) -> p dc g", dc=8)
    pinv = pin.ap().rearrange("p (dc lc) -> p dc lc", dc=8)

    for lc in range(2):
        p.op("dve", lambda e, lc=lc: e.memset(pinv[:, :, lc], 1.0),
             reads=[], writes=[pin])
        p.op("dve", lambda e, lc=lc: e.memset(
                sin.ap().rearrange("p (dc l) -> p dc l", dc=8)[:, :, lc], 0.0),
             reads=[], writes=[sin])
        for g in range(8):
            p.op("dve", lambda e, lc=lc, g=g: e.scalar_tensor_tensor(
                    tmp8.ap(), lamv[:, :, g], msk[:, lc * 8 + g:lc * 8 + g + 1],
                    tmp8.ap(), op0=ALU.mult, op1=ALU.bypass),
                 reads=[lam, msk], writes=[tmp8])
            p.op("dve", lambda e: e.tensor_scalar_add(tmp8.ap(), tmp8.ap(), 1.0),
                 reads=[tmp8], writes=[tmp8])
            p.op("dve", lambda e, lc=lc, g=g: e.tensor_scalar(
                    tmp8.ap(), tmp8.ap(), msk[:, lc * 8 + g:lc * 8 + g + 1], None,
                    op0=ALU.subtract),
                 reads=[tmp8, msk], writes=[tmp8])
            p.op("dve", lambda e, lc=lc: e.tensor_mul(pinv[:, :, lc],
                                                      pinv[:, :, lc], tmp8.ap()),
                 reads=[tmp8, pin], writes=[pin])
            p.op("dve", lambda e, lc=lc, g=g: e.scalar_tensor_tensor(
                    tmp8.ap(), ttsv[:, :, g], msk[:, lc * 8 + g:lc * 8 + g + 1],
                    tmp8.ap(), op0=ALU.mult, op1=ALU.bypass),
                 reads=[tts, msk], writes=[tmp8])
            p.op("dve", lambda e, lc=lc: e.tensor_add(
                    sin.ap().rearrange("p (dc l) -> p dc l", dc=8)[:, :, lc],
                    sin.ap().rearrange("p (dc l) -> p dc l", dc=8)[:, :, lc],
                    tmp8.ap()),
                 reads=[tmp8, sin], writes=[sin])
    p.op("dve", lambda e: e.tensor_scalar_max(invp.ap(), pin.ap(), 1e-30),
         reads=[pin], writes=[invp])
    p.op("dve", lambda e: e.reciprocal(invp.ap(), invp.ap()),
         reads=[invp], writes=[invp])

    a1v = a1in.ap().rearrange("p (dc t) -> p dc t", dc=8)
    c2v = c2in.ap().rearrange("p (dc t) -> p dc t", dc=8)
    plv = plin.ap().rearrange("p (dc t) -> p dc t", dc=8)
    a1b = p.sb("a1b", (128, 512), F32)
    c2b = p.sb("c2b", (128, 512), F32)
    plb = p.sb("plb", (128, 512), F32)
    hb = p.sb("hb", (128, 8 * 512), F32)
    hT = p.sb("hT", (128, D), F32)
    xt = [p.sb("xt0", (128, D), F32)] * 2
    scr = p.sb("scr", (128, D), F32)
    st = p.sb("st", (128, 16), F32)
    ob = p.sb("ob", (128, D), F32)
    tr32_ps = p.psum("tr32_ps", (128, 512), F32)
    hbv = hb.ap().rearrange("p (dc t) -> p dc t", dc=8)

    for blk in range(4):
        lc = blk // 2
        for dc in range(8):
            col = dc * 2 + lc
            p.dma("dsp", a1b, a1b.ap(), a1in, a1v[:, dc, blk * 512:(blk + 1) * 512])
            p.dma("dsp", c2b, c2b.ap(), c2in, c2v[:, dc, blk * 512:(blk + 1) * 512])
            p.dma("dsp", plb, plb.ap(), plin, plv[:, dc, blk * 512:(blk + 1) * 512])
            p.op("dve", lambda e, col=col: e.tensor_scalar(
                    a1b.ap(), a1b.ap(), invp[:, col:col + 1], sin[:, col:col + 1],
                    op0=ALU.mult, op1=ALU.add),
                 reads=[a1b, invp, sin], writes=[a1b])
            p.op("dve", lambda e: e.scalar_tensor_tensor(
                    a1b.ap(), c2b.ap(), 1e7, a1b.ap(), op0=ALU.mult, op1=ALU.add),
                 reads=[c2b, a1b], writes=[a1b])
            p.op("dve", lambda e, col=col: e.tensor_scalar(
                    plb.ap(), plb.ap(), pin[:, col:col + 1], 1e-7,
                    op0=ALU.mult, op1=ALU.max),
                 reads=[plb, pin], writes=[plb])
            p.op("dve", lambda e, dc=dc: e.tensor_tensor(
                    hbv[:, dc, :], plb.ap(), a1b.ap(), op=ALU.mult),
                 reads=[plb, a1b, hb], writes=[hb])
        for tc in range(4):
            t = blk * 4 + tc
            for dc in range(8):
                p.op("pe", lambda e, dc=dc, tc=tc: e.transpose(
                        tr32_ps.ap_[:, (dc % 4) * 128:(dc % 4 + 1) * 128],
                        hbv[:, dc, tc * 128:(tc + 1) * 128], ident32.ap()),
                     reads=[hb, ident32], writes=[tr32_ps])
                if dc % 4 == 3:
                    half = dc // 4
                    p.op("dve", lambda e, half=half: e.tensor_copy(
                            hT.ap()[:, half * 512:(half + 1) * 512], tr32_ps.ap()),
                         reads=[tr32_ps], writes=[hT])
            xb = xt[t % 2]
            p.dma("dsp", xb, xb.ap(), *xsrc(t))
            mean2, rstd2 = emit_ln_stats(p, hT, hT.ap(), scr, st, 8)
            p.op("dve", lambda e: e.tensor_scalar(
                    scr.ap(), hT.ap(), st[:, 8:9], st[:, 11:12],
                    op0=ALU.subtract, op1=ALU.mult),
                 reads=[hT, st], writes=[scr])
            p.op("dve", lambda e: e.tensor_add(ob.ap(), xb.ap(), scr.ap()),
                 reads=[xb, scr], writes=[ob])
            dbuf, dap = xdst(t)
            p.dma("dsp", dbuf, dap, ob, ob.ap(),
                  unordered_dst=True, sem_buf=ob)
    _ctx.__exit__(None, None, None)
    if standalone:
        return p.finish()
    return None




def _g_of(g):
    return (g, 0) if g < 4 else (7 - g, 1)


def build_l3attn_f(p=None, xsrc=None, xdst=None, pref=""):
    nx_d = p.dram_tmp(pref + "nx_d", (128, 8 * TCORE), F32)
    pl_d = p.dram_tmp(pref + "pl_d", (128, 8 * TCORE), F32)
    lam_c = p.dram_tmp(pref + "lam_c", (128, 16), F32)
    lam_ag = p.dram_tmp(pref + "lam_ag", (4 * 128, 16), F32)
    a1_d = p.dram_tmp(pref + "a1_d", (128, 8 * TCORE), F32)
    c2_d = p.dram_tmp(pref + "c2_d", (128, 8 * TCORE), F32)
    tot_c = p.dram_tmp(pref + "tot_c", (128, 16), F32)
    tot_ag = p.dram_tmp(pref + "tot_ag", (4 * 128, 16), F32)
    groups = [[0, 1, 2, 3], [4, 5, 6, 7]]

    build_l3attn_g1(p=p, xsrc=xsrc, pref=pref, nxout=nx_d, plout=pl_d,
                    lamout=lam_c)
    p.cc("AllGather", ALU.bypass, groups, lam_c, lam_c.ap(), lam_ag, lam_ag.ap())

    def mk_loader(ag_buf):
        def load(pp, sbuf_buf):
            v = sbuf_buf.ap().rearrange("p (dc g) -> p dc g", dc=8)
            for g in range(8):
                r, lc = _g_of(g)
                pp.dma("dsp", sbuf_buf, v[:, :, g:g + 1],
                       ag_buf, ag_buf.ap()[r * 128:(r + 1) * 128, lc::2]
                       .rearrange("p (dc o) -> p dc o", o=1),
                       allow_slow_non_contiguous=True)
        return load

    build_l3attn_g2(p=p, pref=pref, nxin=nx_d, plin=pl_d,
                    lam_fn=mk_loader(lam_ag), a1out=a1_d, c2out=c2_d,
                    totout=tot_c)
    p.cc("AllGather", ALU.bypass, groups, tot_c, tot_c.ap(), tot_ag, tot_ag.ap())
    build_l3attn_g3(p=p, xsrc=xsrc, xdst=xdst, pref=pref, a1in=a1_d,
                    c2in=c2_d, plin=pl_d, lam_fn=mk_loader(lam_ag),
                    tot_fn=mk_loader(tot_ag))
    return None


def prep_l3attn_f(inputs, xflat, c, pref=""):
    ident = np.eye(128, dtype=np.float32)
    ap_fm = np.ascontiguousarray(
        inputs["L3_attn_a_param"].astype(np.float32).reshape(8, 128).T)
    b, j = divmod(c, 4)
    m = np.zeros((2, 8), np.float32)
    m[0, :j] = 1.0
    m[1, :7 - j] = 1.0
    mm = np.broadcast_to(m.reshape(1, 16), (128, 16)).copy()
    d = {"igwin": inputs["L3_attn_ig_w"].astype(np.float32),
         "igbin": np.ascontiguousarray(inputs["L3_attn_ig_b"].astype(np.float32).T),
         "agwin": inputs["L3_attn_ag_w"].astype(np.float32),
         "agbin": np.ascontiguousarray(inputs["L3_attn_ag_b"].astype(np.float32).T),
         "apin": ap_fm, "identin": ident, "identin3": ident,
         "mskin": mm, "mskin3": mm}
    return {pref + k: v for k, v in d.items()}


def run_l3attn(inputs, xflat):
    for k, b in (("l3attn_g1", build_l3attn_g1), ("l3attn_g2", build_l3attn_g2),
                 ("l3attn_g3", build_l3attn_g3)):
        if k not in _nc_cache:
            _nc_cache[k] = b()
    ident = np.eye(128, dtype=np.float32)
    ap_fm = np.ascontiguousarray(
        inputs["L3_attn_a_param"].astype(np.float32).reshape(8, 128).T)
    in1 = [{"xin": slice_tokens(xflat, c),
            "igwin": inputs["L3_attn_ig_w"].astype(np.float32),
            "igbin": np.ascontiguousarray(inputs["L3_attn_ig_b"].astype(np.float32).T),
            "agwin": inputs["L3_attn_ag_w"].astype(np.float32),
            "agbin": np.ascontiguousarray(inputs["L3_attn_ag_b"].astype(np.float32).T),
            "apin": ap_fm, "identin": ident} for c in range(NCORES)]
    r1 = run_bass_kernel_spmd(_nc_cache["l3attn_g1"], in1,
                              core_ids=list(range(NCORES)))
    def gather_cols(res_key, ncol):
        out = {}
        for b in range(2):
            arr = np.empty((128, 8, 8), np.float32)
            for g in range(8):
                j = g if g < 4 else 7 - g
                col = 0 if g < 4 else 1
                arr[:, :, g] = res_key[4 * b + j].reshape(128, 8, ncol)[:, :, col]
            out[b] = arr.reshape(128, 64)
        return out
    lam_g = gather_cols([r1.results[c]["lamout"] for c in range(NCORES)], 2)
    def mk_msk(c):
        b, j = divmod(c, 4)
        m = np.zeros((2, 8), np.float32)
        m[0, :j] = 1.0
        m[1, :7 - j] = 1.0
        return np.broadcast_to(m.reshape(1, 16), (128, 16)).copy()
    in2 = []
    for c in range(NCORES):
        b = c // 4
        in2.append({"nxin": r1.results[c]["nxout"],
                    "plin": r1.results[c]["plout"],
                    "lamin": lam_g[b], "mskin": mk_msk(c)})
    r2 = run_bass_kernel_spmd(_nc_cache["l3attn_g2"], in2,
                              core_ids=list(range(NCORES)))
    tot_g = gather_cols([r2.results[c]["totout"] for c in range(NCORES)], 2)
    in3 = []
    for c in range(NCORES):
        b = c // 4
        in3.append({"xin": slice_tokens(xflat, c),
                    "a1in": r2.results[c]["a1out"],
                    "c2in": r2.results[c]["c2out"],
                    "plin": r1.results[c]["plout"],
                    "lamin": lam_g[b], "totin": tot_g[b],
                    "mskin": mk_msk(c), "identin": ident})
    r3 = run_bass_kernel_spmd(_nc_cache["l3attn_g3"], in3,
                              core_ids=list(range(NCORES)))
    return unslice_tokens([r3.results[c]["xout"] for c in range(NCORES)])





def _mk_xfn(b):
    return lambda t: (b, b[t * 128:(t + 1) * 128, :])


def build_chain(stage_fns):
    p = P()
    xin = p.dram_in("xin", (TCORE, D), F32)
    xout = p.dram_out("xout", (TCORE, D), F32)
    mids = [p.dram_tmp("xmid0", (TCORE, D), F32),
            p.dram_tmp("xmid1", (TCORE, D), F32)]
    cur = xin
    for i, (fn, pref) in enumerate(stage_fns):
        last = (i == len(stage_fns) - 1)
        nxt = xout if last else mids[i % 2]
        fn(p=p, xsrc=_mk_xfn(cur), xdst=_mk_xfn(nxt), pref=pref)
        cur = nxt
    return p.finish()


def run_chain(inputs, xflat, key, stage_fns, preps):
    if key not in _nc_cache:
        _nc_cache[key] = build_chain(stage_fns)
    in_maps = []
    for c in range(NCORES):
        d = {"xin": slice_tokens(xflat, c)}
        for prep, pref in preps:
            d.update(prep(inputs, xflat, c, pref=pref))
        in_maps.append(d)
    res = run_bass_kernel_spmd(_nc_cache[key], in_maps,
                               core_ids=list(range(NCORES)))
    return unslice_tokens([res.results[c]["xout"] for c in range(NCORES)])





def _attn_group_srcs(kT_ag, v_ag):
    def k_src(kt, hg):
        g = kt // 8
        r, half = _g_of(g)
        return (kT_ag, kT_ag.ap()[r * D + hg * 256: r * D + (hg + 1) * 256,
                                  half * 1024 + (kt % 8) * 128:
                                  half * 1024 + (kt % 8 + 1) * 128])

    def v_src(kt, hg):
        g = kt // 8
        r, half = _g_of(g)
        row = r * TCORE + half * 1024 + (kt % 8) * 128
        return (v_ag, v_ag.ap()[row:row + 128, hg * 256:(hg + 1) * 256])
    return k_src, v_src


def _mk_kmax_fn(p, kmax_ag):
    def kmax_fn(pp, kmax):
        pp.dma("dsp", kmax, kmax[:, 0:4],
               kmax_ag, kmax_ag.ap().rearrange("(r h) o -> h (r o)", r=4),
               allow_slow_non_contiguous=True)
        pp.op("dve", lambda e: e.reduce_max(kmax[:, 4:5], kmax[:, 0:4],
                                            axis=AX.X),
              reads=[kmax], writes=[kmax])
    return kmax_fn


GROUPS4 = [[0, 1, 2, 3], [4, 5, 6, 7]]


def build_l2attn_f(p=None, xsrc=None, xdst=None, pref=""):
    qT_c = p.dram_tmp(pref + "qT_c", (D, TCORE), BF16)
    kT_c = p.dram_tmp(pref + "kT_c", (D, TCORE), BF16)
    v_c = p.dram_tmp(pref + "v_c", (TCORE, D), BF16)
    qnT_c = p.dram_tmp(pref + "qnT_c", (16, TCORE), F32)
    kmax_c = p.dram_tmp(pref + "kmax_c", (16, 1), F32)
    kT_ag = p.dram_tmp(pref + "kT_ag", (4 * D, TCORE), BF16)
    v_ag = p.dram_tmp(pref + "v_ag", (4 * TCORE, D), BF16)
    kmax_ag = p.dram_tmp(pref + "kmax_ag", (64, 1), F32)
    build_attn_kv(p=p, xsrc=xsrc, pref=pref, qTout=qT_c, kTout=kT_c,
                  vout=v_c, qnTout=qnT_c, kmaxout=kmax_c)
    p.cc("AllGather", ALU.bypass, GROUPS4, kT_c, kT_c.ap(), kT_ag, kT_ag.ap())
    p.cc("AllGather", ALU.bypass, GROUPS4, v_c, v_c.ap(), v_ag, v_ag.ap())
    p.cc("AllGather", ALU.bypass, GROUPS4, kmax_c, kmax_c.ap(),
         kmax_ag, kmax_ag.ap())
    k_src, v_src = _attn_group_srcs(kT_ag, v_ag)
    build_attn_core(False, p=p, xsrc=xsrc, xdst=xdst, pref=pref,
                    qT_src=(qT_c, qT_c.ap()), k_src=k_src, v_src=v_src,
                    qnT_src=(qnT_c, qnT_c.ap()), kmax_fn=_mk_kmax_fn(p, kmax_ag))
    return None


def prep_l2attn_f(inputs, xflat, c, pref=""):
    d = {"wqin": np.ascontiguousarray(inputs["L2_attn_Wq"].T.astype(np.float32)),
         "wkin": np.ascontiguousarray(inputs["L2_attn_Wk"].T.astype(np.float32)),
         "wvin": np.ascontiguousarray(inputs["L2_attn_Wv"].T.astype(np.float32)),
         "bqin": _rep(inputs["L2_attn_bq"]), "bkin": _rep(inputs["L2_attn_bk"]),
         "bvin": _rep(inputs["L2_attn_bv"]),
         "identin": np.eye(128, dtype=np.float32),
         "augin": np.ones((64, 512), np.float32),
         "woin": np.ascontiguousarray(inputs["L2_attn_Wo"].T.astype(np.float32)),
         "boin": _rep(inputs["L2_attn_bo"])}
    return {pref + k: v for k, v in d.items()}


def build_l1attn_f(p=None, xsrc=None, xdst=None, pref=""):
    qT_c = p.dram_tmp(pref + "qT_c", (D, TCORE), BF16)
    kT_c = p.dram_tmp(pref + "kT_c", (D, TCORE), BF16)
    v_c = p.dram_tmp(pref + "v_c", (TCORE, D), BF16)
    qnT_c = p.dram_tmp(pref + "qnT_c", (16, TCORE), F32)
    kmax_c = p.dram_tmp(pref + "kmax_c", (16, 1), F32)
    x_c = p.dram_tmp(pref + "x_c", (TCORE, D), F32)
    qT_ag = p.dram_tmp(pref + "qT_ag", (4 * D, TCORE), BF16)
    kT_ag = p.dram_tmp(pref + "kT_ag", (4 * D, TCORE), BF16)
    v_ag = p.dram_tmp(pref + "v_ag", (4 * TCORE, D), BF16)
    qnT_ag = p.dram_tmp(pref + "qnT_ag", (64, TCORE), F32)
    kmax_ag = p.dram_tmp(pref + "kmax_ag", (64, 1), F32)
    x_ag = p.dram_tmp(pref + "x_ag", (4 * TCORE, D), F32)
    build_attn_kv(p=p, xsrc=xsrc, pref=pref, qTout=qT_c, kTout=kT_c,
                  vout=v_c, qnTout=qnT_c, kmaxout=kmax_c, xcopy=x_c)
    for cin, cout in ((qT_c, qT_ag), (kT_c, kT_ag), (v_c, v_ag),
                      (qnT_c, qnT_ag), (kmax_c, kmax_ag), (x_c, x_ag)):
        p.cc("AllGather", ALU.bypass, GROUPS4, cin, cin.ap(), cout, cout.ap())

    rhin = p.dram_in(pref + "rhin", (1, 8), I32)
    rhs_ = p.sb("rhs_", (1, 8), I32)
    p.dma("dsp", rhs_, rhs_.ap(), rhin, rhin.ap())
    regs_sp = [p.sval("sp", rhs_, rhs_[0:1, i:i + 1]) for i in range(4)]
    regs_pool = [p.sval("pool", rhs_, rhs_[0:1, i:i + 1]) for i in range(8)]
    regs_act = [p.sval("act", rhs_, rhs_[0:1, i:i + 1]) for i in range(4)]

    qT_my = p.dram_tmp(pref + "qT_my", (D, TCORE), BF16)
    kT_my = p.dram_tmp(pref + "kT_my", (D, TCORE), BF16)
    v_my = p.dram_tmp(pref + "v_my", (TCORE, D), BF16)
    x_my = p.dram_tmp(pref + "x_my", (TCORE, D), F32)
    qnT_my = p.dram_tmp(pref + "qnT_my", (16, TCORE), F32)

    for i in range(2):
        rk_sp, h_sp = regs_sp[2 * i], regs_sp[2 * i + 1]
        rk_a, h_a = regs_act[2 * i], regs_act[2 * i + 1]
        rk_p, h_p = regs_pool[2 * i], regs_pool[2 * i + 1]
        cs = slice(i * 1024, (i + 1) * 1024)
        for tag, tmy, tag_buf in (("q", qT_my, qT_ag), ("k", kT_my, kT_ag)):
            v4 = tag_buf.ap().rearrange("(rk ch) (h t) -> rk ch h t",
                                        rk=4, h=2)
            p.dma("dsp", tmy, tmy.ap()[:, cs].unsqueeze(0).unsqueeze(2),
                  tag_buf, v4[bass.ts(rk_sp, 1), :, bass.ts(h_sp, 1), :])
        for tmy, tag_buf in ((v_my, v_ag), (x_my, x_ag)):
            v4 = tag_buf.ap().rearrange("(rk h t) d -> rk h t d", rk=4, h=2)
            p.dma("dact", tmy, tmy.ap()[cs, :].unsqueeze(0).unsqueeze(0),
                  tag_buf, v4[bass.ts(rk_a, 1), bass.ts(h_a, 1), :, :])
        v4 = qnT_ag.ap().rearrange("(rk hd) (h t) -> rk hd h t", rk=4, h=2)
        p.dma("dpool", qnT_my, qnT_my.ap()[:, cs].unsqueeze(0).unsqueeze(2),
              qnT_ag, v4[bass.ts(rk_p, 1), :, bass.ts(h_p, 1), :])

    xp_c = p.dram_tmp(pref + "xp_c", (TCORE, D), F32)
    xp_ag = p.dram_tmp(pref + "xp_ag", (4 * TCORE, D), F32)
    k_src_g, v_src_g = _attn_group_srcs(kT_ag, v_ag)

    def k_src_my(ktl, hg):
        return (kT_my, kT_my.ap()[hg * 256:(hg + 1) * 256,
                                  ktl * 128:(ktl + 1) * 128])

    def v_src_my(ktl, hg):
        return (v_my, v_my.ap()[ktl * 128:(ktl + 1) * 128,
                                hg * 256:(hg + 1) * 256])

    def plan_fn(qb):
        plan = [(k_src_g, v_src_g, kt, kt, None) for kt in range(48)]
        for ktl in range(4 * qb + 4):
            mask_i = (ktl - 4 * qb) if ktl >= 4 * qb else None
            plan.append((k_src_my, v_src_my, ktl, 48 + ktl, mask_i))
        return plan

    build_attn_core(True, p=p,
                    xsrc=_mk_xfn(x_my), xdst=_mk_xfn(xp_c), pref=pref,
                    qT_src=(qT_my, qT_my.ap()), k_src=None, v_src=None,
                    qnT_src=(qnT_my, qnT_my.ap()),
                    kmax_fn=_mk_kmax_fn(p, kmax_ag), plan_fn=plan_fn)
    p.cc("AllGather", ALU.bypass, GROUPS4, xp_c, xp_c.ap(), xp_ag, xp_ag.ap())
    xnext = xdst(0)[0]
    for i in range(2):
        rk_p, h_p = regs_pool[4 + 2 * i], regs_pool[4 + 2 * i + 1]
        v4 = xp_ag.ap().rearrange("(rk h t) d -> rk h t d", rk=4, h=2)
        p.dma("dpool", xnext,
              xnext.ap_[i * 1024:(i + 1) * 1024, :].unsqueeze(0).unsqueeze(0),
              xp_ag, v4[bass.ts(rk_p, 1), bass.ts(h_p, 1), :, :])
    return None


def prep_l1attn_f(inputs, xflat, c, pref=""):
    b, j = divmod(c, 4)
    rh = []
    for g in (2 * j, 2 * j + 1):
        r, h = _g_of(g)
        rh += [r, h]
    for g in (j, 7 - j):
        rh += [g // 2, g % 2]
    aug = np.ones((64, 512), np.float32)
    npast = (2048 * j) // 128
    aug[npast:48] = 30.0
    d = {"wqin": np.ascontiguousarray(inputs["L1_attn_wq"].T.astype(np.float32)),
         "wkin": np.ascontiguousarray(inputs["L1_attn_wk"].T.astype(np.float32)),
         "wvin": np.ascontiguousarray(inputs["L1_attn_wv"].T.astype(np.float32)),
         "bqin": np.zeros((128, D), np.float32),
         "bkin": np.zeros((128, D), np.float32),
         "bvin": np.zeros((128, D), np.float32),
         "identin": np.eye(128, dtype=np.float32),
         "augin": aug,
         "woin": np.ascontiguousarray(inputs["L1_attn_wo"].T.astype(np.float32)),
         "boin": np.zeros((128, D), np.float32),
         "rhin": np.array([rh], np.int32)}
    return {pref + k: v for k, v in d.items()}



def kernel(**inputs):
    xflat = np.ascontiguousarray(
        np.asarray(inputs["L0_attn_y_de"], dtype=np.float32).reshape(16384, D))
    xflat = run_chain(
        inputs, xflat, "c_all",
        [(build_l0attn, "a_"), (build_l0ffn, "b_"), (build_l1attn_f, "g_"),
         (build_l1ffn_f, "c_"), (build_l2attn_f, "h_"), (build_l2ffn, "d_"),
         (build_l3attn_f, "e_"), (build_l3ffn_f, "f_")],
        [(prep_l0attn, "a_"), (prep_l0ffn, "b_"), (prep_l1attn_f, "g_"),
         (prep_l1ffn_f, "c_"), (prep_l2attn_f, "h_"), (prep_l2ffn, "d_"),
         (prep_l3attn_f, "e_"), (prep_l3ffn_f, "f_")])
    return xflat.reshape(2, 8192, D).astype(np.float32)


# version 09 (final)
# speedup vs baseline: 1.2304x; 1.2304x over previous
import numpy as np
from contextlib import ExitStack

import concourse.bass as bass
import concourse.mybir as mybir
from concourse.bass_utils import run_bass_kernel_spmd

F32 = mybir.dt.float32
BF16 = mybir.dt.bfloat16
I32 = mybir.dt.int32
U32 = mybir.dt.uint32
AF = mybir.ActivationFunctionType
ALU = mybir.AluOpType
AX = mybir.AxisListType

NCORES = 8
TCORE = 2048
NTT = TCORE // 128
D = 1024
NH = 16
HD = 64

_nc_cache = {}


def core_row_slices(c):
    b, j = divmod(c, 4)
    return [(8192 * b + 1024 * j, 1024), (8192 * b + 1024 * (7 - j), 1024)]

def slice_tokens(flat, c):
    (r1, n1), (r2, n2) = core_row_slices(c)
    return np.ascontiguousarray(
        np.concatenate([flat[r1:r1 + n1], flat[r2:r2 + n2]], axis=0))

def unslice_tokens(shards):
    flat = np.empty((16384,) + shards[0].shape[1:], dtype=shards[0].dtype)
    for c in range(NCORES):
        (r1, n1), (r2, n2) = core_row_slices(c)
        flat[r1:r1 + n1] = shards[c][:n1]
        flat[r2:r2 + n2] = shards[c][n1:]
    return flat


class Buf:
    __slots__ = ("name", "ap_", "w_ev", "r_evs")
    def __init__(self, name, ap_):
        self.name = name
        self.ap_ = ap_
        self.w_ev = None
        self.r_evs = []
    def ap(self):
        return self.ap_
    def __getitem__(self, key):
        return self.ap_[key]

ENGS = ("pe", "act", "dve", "pool", "sp")

class P:
    def __init__(self):
        nc = bass.Bass("TRN2", target_bir_lowering=False, debug=False)
        self.nc = nc
        self.root_es = ExitStack()
        self.es = self.root_es
        self.eng = {"pe": nc.tensor, "act": nc.scalar, "dve": nc.vector,
                    "pool": nc.gpsimd, "sp": nc.sync}
        self.sem = {}
        self.cnt = {}
        for e in ENGS:
            self.sem[e] = self.root_es.enter_context(nc.semaphore(f"S_{e}"))
            self.cnt[e] = 0
        self.hiwater = {}
        self.dma_sems = []
        self.stage_idx = 0
        self.sem_pool = []
        self._stage_sems = []
        self._in_stage = False

    from contextlib import contextmanager as _cm

    @_cm
    def stage(self):
        saved = self.es
        self.es = ExitStack()
        saved_sems = self._stage_sems
        self._stage_sems = []
        saved_in = self._in_stage
        self._in_stage = True
        try:
            yield
        finally:
            self.es.close()
            self.es = saved
            self.stage_idx += 1
            self.barrier()
            for skey in self._stage_sems:
                self.sem_pool.append((self.sem[skey], self.cnt[skey]))
                del self.sem[skey]
                del self.cnt[skey]
                self.dma_sems.remove(skey)
            self._stage_sems = saved_sems
            self._in_stage = saved_in

    def barrier(self):
        for e in ENGS:
            for s, c in list(self.cnt.items()):
                if c > 0 and s != e:
                    self._wait(e, (s, c))

    def cc(self, kind, op, groups, in_buf, in_ap, out_buf, out_ap):
        for ev in self._deps([in_buf], [out_buf]):
            self._wait("pool", ev)
        inst = self.eng["pool"].collective_compute(
            kind, op, replica_groups=groups, ins=[in_ap.opt()],
            outs=[out_ap.opt()])
        self.cnt["pool"] += 1
        inst.then_inc(self.sem["pool"], 1)
        ev = ("pool", self.cnt["pool"])
        in_buf.r_evs.append(ev)
        out_buf.w_ev = ev
        out_buf.r_evs = []
        return ev

    def sval(self, eng_name, buf, ap):
        for ev in self._deps([buf], []):
            self._wait(eng_name, ev)
        return self.eng[eng_name].value_load(ap)

    def sb(self, name, shape, dtype):
        h = self.es.enter_context(self.nc.sbuf_tensor(
            f"{name}_s{self.stage_idx}", list(shape), dtype))
        return Buf(name, h[:])

    def psum(self, name, shape, dtype=F32):
        h = self.es.enter_context(self.nc.psum_tensor(
            f"{name}_s{self.stage_idx}", list(shape), dtype))
        return Buf(name, h[:])

    def dram_in(self, name, shape, dtype):
        if not hasattr(self, "_dins"):
            self._dins = {}
        if name in self._dins:
            return self._dins[name]
        h = self.nc.dram_tensor(name, list(shape), dtype, kind="ExternalInput")
        b = Buf(name, h.ap())
        self._dins[name] = b
        return b

    def dram_out(self, name, shape, dtype):
        h = self.nc.dram_tensor(name, list(shape), dtype, kind="ExternalOutput")
        return Buf(name, h.ap())

    def dram_tmp(self, name, shape, dtype):
        h = self.nc.dram_tensor(name, list(shape), dtype, kind="Internal")
        return Buf(name, h.ap())

    def _wait(self, eng_name, ev):
        if ev is None:
            return
        sname, count = ev
        if sname == eng_name == "pe":
            return
        key = (eng_name, sname)
        if self.hiwater.get(key, -1) >= count:
            return
        self.eng[eng_name].wait_ge(self.sem[sname], count)
        self.hiwater[key] = count

    def _deps(self, reads, writes):
        evs = []
        for b in reads:
            if b.w_ev is not None:
                evs.append(b.w_ev)
        for b in writes:
            if b.w_ev is not None:
                evs.append(b.w_ev)
            evs.extend(b.r_evs)
        return evs

    def op(self, eng_name, fn, reads=(), writes=(), sig=True, dma16=False):
        for ev in self._deps(reads, writes):
            self._wait(eng_name, ev)
        inst = fn(self.eng[eng_name])
        ev = None
        if dma16:
            skey = "q_" + eng_name
            if skey not in self.sem:
                self.sem[skey] = self.root_es.enter_context(self.nc.semaphore(skey))
                self.cnt[skey] = 0
                self.dma_sems.append(skey)
            self.cnt[skey] += 16
            inst.then_inc(self.sem[skey], 16)
            ev = (skey, self.cnt[skey])
        elif sig:
            self.cnt[eng_name] += 1
            inst.then_inc(self.sem[eng_name], 1)
            ev = (eng_name, self.cnt[eng_name])
        for b in reads:
            if ev is not None:
                b.r_evs.append(ev)
        for b in writes:
            b.w_ev = ev
            b.r_evs = []
        return ev

    def dma(self, qname, dst_buf, dst_ap, src_buf, src_ap, unordered_dst=False,
            **kw):
        eng_name = {"dsp": "sp", "dact": "act", "dpool": "pool"}[qname]
        sem_buf = kw.pop("sem_buf", None) or dst_buf
        skey = "b_" + sem_buf.name
        if self._in_stage:
            skey = f"b{self.stage_idx}_" + sem_buf.name
        if skey not in self.sem:
            if self._in_stage and self.sem_pool:
                h, c0 = self.sem_pool.pop()
                self.sem[skey] = h
                self.cnt[skey] = c0
            else:
                self.sem[skey] = self.root_es.enter_context(
                    self.nc.semaphore(skey))
                self.cnt[skey] = 0
            self.dma_sems.append(skey)
            if self._in_stage:
                self._stage_sems.append(skey)
        deps = self._deps([src_buf], [dst_buf] if not unordered_dst else [])
        for ev in deps:
            self._wait(eng_name, ev)
        inst = self.eng[eng_name].dma_start(out=dst_ap, in_=src_ap, **kw)
        self.cnt[skey] += 16
        inst.then_inc(self.sem[skey], 16)
        ev = (skey, self.cnt[skey])
        src_buf.r_evs.append(ev)
        if not unordered_dst:
            dst_buf.w_ev = ev
            dst_buf.r_evs = []
        return ev

    def finish(self):
        for skey in self.dma_sems:
            if self.cnt[skey] > 0:
                self.eng["sp"].wait_ge(self.sem[skey], self.cnt[skey])
        self.root_es.close()
        return self.nc



def emit_ln_stats(p, xb, x_ap, scr, st, col, n=D):
    c0, c1, c2, c3 = col, col + 1, col + 2, col + 3
    p.op("act", lambda e: e.activation(scr.ap(), x_ap, AF.Square,
                                       accum_out=st[:, c1:c1 + 1]),
         reads=[xb], writes=[scr, st])
    p.op("dve", lambda e: e.reduce_sum(st[:, c0:c0 + 1], x_ap, axis=AX.X),
         reads=[xb, st], writes=[st])
    p.op("dve", lambda e: e.tensor_scalar_mul(st[:, c0:c0 + 1], st[:, c0:c0 + 1],
                                              1.0 / n),
         reads=[st], writes=[st])
    p.op("dve", lambda e: e.tensor_scalar_mul(st[:, c1:c1 + 1], st[:, c1:c1 + 1],
                                              1.0 / n),
         reads=[st], writes=[st])
    p.op("dve", lambda e: e.scalar_tensor_tensor(
            st[:, c2:c2 + 1], st[:, c0:c0 + 1], st[:, c0:c0 + 1],
            st[:, c1:c1 + 1], op0=ALU.mult, op1=ALU.subtract),
         reads=[st], writes=[st])
    p.op("dve", lambda e: e.tensor_scalar(st[:, c2:c2 + 1], st[:, c2:c2 + 1],
                                          -1.0, 1e-5, op0=ALU.mult, op1=ALU.add),
         reads=[st], writes=[st])
    p.op("dve", lambda e: e.reciprocal(st[:, c2:c2 + 1], st[:, c2:c2 + 1]),
         reads=[st], writes=[st])
    p.op("act", lambda e: e.activation(st[:, c3:c3 + 1], st[:, c2:c2 + 1], AF.Sqrt),
         reads=[st], writes=[st])
    return st[:, c0:c0 + 1], st[:, c3:c3 + 1]


def emit_ln_apply(p, dst, dst_ap, xb, x_ap, st, mean_ap, rstd_ap):
    p.op("dve", lambda e: e.tensor_scalar(dst_ap, x_ap, mean_ap, rstd_ap,
                                          op0=ALU.subtract, op1=ALU.mult),
         reads=[xb, st], writes=[dst])



def build_l0attn(debug=False, p=None, xsrc=None, xdst=None, pref=""):
    standalone = p is None
    if standalone:
        p = P()
        xin = p.dram_in("xin", (TCORE, D), F32)
        xout_t = p.dram_out("xout", (TCORE, D), F32)
        xsrc = lambda t: (xin, xin[t * 128:(t + 1) * 128, :])
        xdst = lambda t: (xout_t, xout_t[t * 128:(t + 1) * 128, :])
    _ctx = p.stage()
    _ctx.__enter__()
    maskin = p.dram_in(pref + "maskin", (TCORE, D), F32)
    wblk = p.dram_in(pref + "wblk", (3, 128, 128), F32)
    brep = p.dram_in(pref + "brep", (3, 128, D), F32)
    identin = p.dram_in(pref + "identin", (128, 128), F32)
    if debug:
        dbg = {n: p.dram_out("dbg_" + n, shp, F32) for n, shp in [
            ("yf", (128, D)), ("ym", (128, D)), ("ymT", (128, D)),
            ("q", (128, D)), ("k", (128, D)), ("v", (128, D)),
            ("s", (128, 256)), ("z", (128, 256)), ("tau", (128, 16)),
            ("pat", (128, 256)), ("oac", (128, D))]}

    ident = p.sb("ident", (128, 128), BF16)
    wsb = p.sb("wsb", (128, 3 * 128), BF16)
    bsb = p.sb("bsb", (128, 3 * D), F32)
    p.dma("dpool", ident, ident.ap(), identin, identin.ap())
    p.dma("dpool", wsb, wsb.ap().rearrange("p (w n) -> p w n", w=3),
          wblk, wblk.ap().rearrange("w p n -> p w n"))
    p.dma("dsp", bsb, bsb.ap().rearrange("p (w n) -> p w n", w=3),
          brep, brep.ap().rearrange("w p n -> p w n"))

    xt = [p.sb(f"xt{i}", (128, D), F32) for i in range(2)]
    mk = [p.sb(f"mk{i}", (128, D), F32) for i in range(2)]
    xo = [p.sb(f"xo{i}", (128, D), F32) for i in range(2)]
    scr = p.sb("scr", (128, D), F32)
    st = p.sb("st", (128, 16), F32)
    yf = p.sb("yf", (128, D), F32)
    ym = p.sb("ym", (128, D), BF16)
    ymT = p.sb("ymT", (128, D), BF16)
    qsb = p.sb("qsb", (128, D), BF16)
    ksb = p.sb("ksb", (128, D), BF16)
    vsb = p.sb("vsb", (128, D), BF16)
    prod = p.sb("prod", (128, D), F32)
    sS = p.sb("sS", (128, NH * NH), F32)
    zz = p.sb("zz", (128, NH * NH), F32)
    rr = p.sb("rr", (128, NH * NH), F32)
    rp = p.sb("rp", (128, NH * NH), F32)
    sm = p.sb("sm", (128, NH * 8), F32)
    pat = p.sb("pat", (128, NH * NH), BF16)
    oac = p.sb("oac", (128, D), F32)

    tr_ps = p.psum("tr_ps", (128, 1024), BF16)
    q_ps = p.psum("q_ps", (128, D), F32)
    k_ps = p.psum("k_ps", (128, D), F32)
    v_ps = p.psum("v_ps", (128, D), F32)

    NIT = 10

    for t in range(NTT):
        xb = xt[t % 2]
        mb = mk[t % 2]
        ob = xo[t % 2]
        p.dma("dsp", xb, xb.ap(), *xsrc(t))
        p.dma("dsp", mb, mb.ap(), maskin, maskin[t * 128:(t + 1) * 128, :])

        mean, rstd = emit_ln_stats(p, xb, xb.ap(), scr, st, 0)
        emit_ln_apply(p, yf, yf.ap(), xb, xb.ap(), st, mean, rstd)
        p.op("dve", lambda e: e.tensor_mul(ym.ap(), yf.ap(), mb.ap()),
             reads=[yf, mb], writes=[ym])

        for dc in range(8):
            sl = slice(dc * 128, (dc + 1) * 128)
            p.op("pe", lambda e, sl=sl: e.transpose(tr_ps[:, sl], ym[:, sl],
                                                    ident.ap()),
                 reads=[ym, ident], writes=[tr_ps])
        p.op("act", lambda e: e.copy(ymT.ap(), tr_ps.ap()),
             reads=[tr_ps], writes=[ymT])

        for dc in range(8):
            sl = slice(dc * 128, (dc + 1) * 128)
            for wi, ps in enumerate((q_ps, k_ps, v_ps)):
                p.op("pe", lambda e, sl=sl, wi=wi, ps=ps: e.matmul(
                        ps[:, sl], ymT[:, sl], wsb[:, wi * 128:(wi + 1) * 128],
                        start=True, stop=True),
                     reads=[ymT, wsb], writes=[ps])
        for wi, (ps, dst) in enumerate(((q_ps, qsb), (k_ps, ksb), (v_ps, vsb))):
            p.op("dve", lambda e, ps=ps, dst=dst, wi=wi: e.tensor_add(
                    dst.ap(), ps.ap(), bsb[:, wi * D:(wi + 1) * D]),
                 reads=[ps, bsb], writes=[dst])

        q3 = qsb.ap().rearrange("p (h d) -> p h d", h=NH)
        for g in range(NH):
            kg = ksb[:, g * HD:(g + 1) * HD].rearrange("p (o d) -> p o d", o=1)
            p.op("dve", lambda e, kg=kg: e.tensor_tensor(
                    prod.ap().rearrange("p (h d) -> p h d", h=NH), q3,
                    kg.broadcast_to((128, NH, HD)), op=ALU.mult),
                 reads=[qsb, ksb], writes=[prod])
            p.op("dve", lambda e, g=g: e.reduce_sum(
                    sS.ap().rearrange("p (h g) -> p h g", h=NH)[:, :, g:g + 1],
                    prod.ap().rearrange("p (h d) -> p h d", h=NH), axis=AX.X),
                 reads=[prod], writes=[sS])

        z3 = zz.ap().rearrange("p (h g) -> p h g", h=NH)
        s3 = sS.ap().rearrange("p (h g) -> p h g", h=NH)
        r3 = rr.ap().rearrange("p (h g) -> p h g", h=NH)
        rp3 = rp.ap().rearrange("p (h g) -> p h g", h=NH)
        cM, cF, cS1, cTau = 0, NH, 2 * NH, 3 * NH
        p.op("dve", lambda e: e.tensor_scalar_mul(zz.ap(), sS.ap(), 1.0 / 64.0),
             reads=[sS], writes=[zz])
        p.op("dve", lambda e: e.reduce_max(sm[:, cM:cM + NH], z3, axis=AX.X),
             reads=[zz, sm], writes=[sm])
        p.op("dve", lambda e: e.tensor_tensor(
                z3, z3, sm[:, cM:cM + NH].rearrange("p (h o) -> p h o", o=1)
                .broadcast_to((128, NH, NH)), op=ALU.subtract),
             reads=[sm, zz], writes=[zz])
        p.op("dve", lambda e: e.memset(sm[:, cTau:cTau + NH], -1.0),
             reads=[], writes=[sm])
        for it in range(NIT):
            tau_b = sm[:, cTau:cTau + NH].rearrange("p (h o) -> p h o", o=1) \
                .broadcast_to((128, NH, NH))
            p.op("dve", lambda e, tau_b=tau_b: e.tensor_tensor(
                    r3, z3, tau_b, op=ALU.subtract),
                 reads=[zz, sm], writes=[rr])
            p.op("dve", lambda e: e.tensor_scalar_max(rp.ap(), rr.ap(), 0.0),
                 reads=[rr], writes=[rp])
            p.op("act", lambda e: e.activation(rr.ap(), rp.ap(), AF.Square),
                 reads=[rp], writes=[rr])
            p.op("dve", lambda e: e.reduce_sum(sm[:, cF:cF + NH], r3, axis=AX.X),
                 reads=[rr, sm], writes=[sm])
            p.op("dve", lambda e: e.reduce_sum(sm[:, cS1:cS1 + NH], rp3, axis=AX.X),
                 reads=[rp, sm], writes=[sm])
            p.op("dve", lambda e: e.tensor_scalar_add(sm[:, cF:cF + NH],
                                                      sm[:, cF:cF + NH], -1.0),
                 reads=[sm], writes=[sm])
            p.op("dve", lambda e: e.tensor_scalar_mul(sm[:, cS1:cS1 + NH],
                                                      sm[:, cS1:cS1 + NH], 2.0),
                 reads=[sm], writes=[sm])
            p.op("dve", lambda e: e.reciprocal(sm[:, cS1:cS1 + NH],
                                               sm[:, cS1:cS1 + NH]),
                 reads=[sm], writes=[sm])
            p.op("dve", lambda e: e.tensor_tensor(sm[:, cF:cF + NH],
                                                  sm[:, cF:cF + NH],
                                                  sm[:, cS1:cS1 + NH],
                                                  op=ALU.mult),
                 reads=[sm], writes=[sm])
            p.op("dve", lambda e: e.tensor_tensor(sm[:, cTau:cTau + NH],
                                                  sm[:, cTau:cTau + NH],
                                                  sm[:, cF:cF + NH],
                                                  op=ALU.add),
                 reads=[sm], writes=[sm])
        tau_b = sm[:, cTau:cTau + NH].rearrange("p (h o) -> p h o", o=1) \
            .broadcast_to((128, NH, NH))
        p.op("dve", lambda e: e.tensor_tensor(r3, z3, tau_b, op=ALU.subtract),
             reads=[zz, sm], writes=[rr])
        p.op("dve", lambda e: e.tensor_scalar_max(rr.ap(), rr.ap(), 0.0),
             reads=[rr], writes=[rr])
        p.op("act", lambda e: e.activation(pat.ap(), rr.ap(), AF.Square),
             reads=[rr], writes=[pat])

        p4 = pat.ap().rearrange("p (h g) -> p h g", h=NH)
        for g in range(NH):
            pg = p4[:, :, g:g + 1].broadcast_to((128, NH, HD))
            vg = vsb[:, g * HD:(g + 1) * HD].rearrange("p (o d) -> p o d", o=1) \
                .broadcast_to((128, NH, HD))
            o3 = (oac if g == 0 else prod).ap().rearrange("p (h d) -> p h d", h=NH)
            p.op("dve", lambda e, pg=pg, vg=vg, o3=o3: e.tensor_tensor(
                    o3, pg, vg, op=ALU.mult),
                 reads=[pat, vsb], writes=[oac if g == 0 else prod])
            if g > 0:
                p.op("dve", lambda e: e.tensor_add(oac.ap(), oac.ap(), prod.ap()),
                     reads=[prod, oac], writes=[oac])

        if debug and t == 0:
            for nm, b in [("yf", yf), ("ym", ym), ("ymT", ymT), ("q", qsb),
                          ("k", ksb), ("v", vsb), ("s", sS), ("z", zz),
                          ("pat", pat), ("oac", oac)]:
                tmpb = p.sb("dbgtmp_" + nm, b.ap_.shape, F32)
                p.op("dve", lambda e, b=b, tmpb=tmpb: e.tensor_copy(tmpb.ap(), b.ap()),
                     reads=[b], writes=[tmpb])
                p.dma("dsp", dbg[nm], dbg[nm].ap(), tmpb, tmpb.ap())
            tmpb = p.sb("dbgtmp_tau", (128, 16), F32)
            p.op("dve", lambda e, tmpb=tmpb: e.tensor_copy(tmpb.ap(), sm[:, cTau:cTau+16]),
                 reads=[sm], writes=[tmpb])
            p.dma("dsp", dbg["tau"], dbg["tau"].ap(), tmpb, tmpb.ap())

        mean2, rstd2 = emit_ln_stats(p, oac, oac.ap(), scr, st, 8)
        emit_ln_apply(p, yf, yf.ap(), oac, oac.ap(), st, mean2, rstd2)
        p.op("dve", lambda e: e.tensor_add(ob.ap(), xb.ap(), yf.ap()),
             reads=[xb, yf], writes=[ob])
        p.dma("dsp", *(lambda b, a: (b, a))(*xdst(t)), ob, ob.ap(),
              unordered_dst=True, sem_buf=ob)

    _ctx.__exit__(None, None, None)
    if standalone:
        return p.finish()
    return None


def prep_l0attn(inputs, xflat, c, pref=""):
    Wq, bq = inputs["L0_attn_Wq"], inputs["L0_attn_bq"]
    Wk, bk = inputs["L0_attn_Wk"], inputs["L0_attn_bk"]
    Wv, bv = inputs["L0_attn_Wv"], inputs["L0_attn_bv"]
    def blk(W):
        z = np.zeros((128, 128), np.float32)
        z[:64, :64] = W.T
        z[64:, 64:] = W.T
        return z
    wblk = np.stack([blk(Wq), blk(Wk), blk(Wv)])
    def brow(b):
        r = np.concatenate([b, b])
        return np.broadcast_to(np.tile(r, 8)[None, :], (128, D)).copy()
    brep = np.stack([brow(bq), brow(bk), brow(bv)])
    b, j = divmod(c, 4)
    mask = np.ones((TCORE, D), np.float32)
    if j == 0:
        mask[:1024] = np.tril(np.ones((1024, D), np.float32), k=-1)
    d = {"maskin": mask, "wblk": wblk, "brep": brep,
         "identin": np.eye(128, dtype=np.float32)}
    d = {pref + k: v for k, v in d.items()}
    if not pref:
        d["xin"] = slice_tokens(xflat, c)
    return d


def run_l0attn(inputs, xflat):
    if "l0attn" not in _nc_cache:
        _nc_cache["l0attn"] = build_l0attn()
    nc = _nc_cache["l0attn"]
    in_maps = [prep_l0attn(inputs, xflat, c) for c in range(NCORES)]
    res = run_bass_kernel_spmd(nc, in_maps, core_ids=list(range(NCORES)))
    return unslice_tokens([res.results[c]["xout"] for c in range(NCORES)])





def build_l0ffn(p=None, xsrc=None, xdst=None, pref=""):
    TB = 2048
    standalone = p is None
    if standalone:
        p = P()
        xin = p.dram_in("xin", (TCORE, D), F32)
        xout_t = p.dram_out("xout", (TCORE, D), F32)
        xsrc = lambda t: (xin, xin[t * 128:(t + 1) * 128, :])
        xdst = lambda t: (xout_t, xout_t[t * 128:(t + 1) * 128, :])
    _ctx = p.stage()
    _ctx.__enter__()
    rwin = p.dram_in(pref + "rwin", (D, 8), F32)
    w1in = p.dram_in(pref + "w1in", (8, D, D), F32)
    w2in = p.dram_in(pref + "w2in", (8, D, D), F32)
    sguin = p.dram_in(pref + "sguin", (D, 2 * D), F32)
    sdnin = p.dram_in(pref + "sdnin", (D, D), F32)
    sgwin = p.dram_in(pref + "sgwin", (D, 1), F32)
    identin = p.dram_in(pref + "identin", (128, 128), F32)

    ident = p.sb("ident", (128, 128), BF16)
    p.dma("dpool", ident, ident.ap(), identin, identin.ap())
    rw = p.sb("rw", (128, 8 * 8), F32)
    p.dma("dpool", rw, rw.ap().rearrange("p (dc e) -> p dc e", dc=8),
          rwin, rwin.ap().rearrange("(dc p) e -> p dc e", p=128))
    ident32 = p.sb("ident32", (128, 128), F32)
    p.dma("dsp", ident32, ident32.ap(), identin, identin.ap())
    yT32 = p.sb("yT32", (128, 128), F32)
    w1b = [p.sb(f"w1b{i}", (128, 64 * 128), BF16) for i in range(2)]
    w2b = [p.sb("w2b0", (128, 8 * D), BF16)]
    p.dma("dpool", w1b[0], w1b[0].ap().rearrange("p (dc gc j) -> p dc gc j", dc=4, gc=16),
          sguin, sguin.ap()[0:512, :].rearrange("(dc p) (gc j) -> p dc gc j", p=128, j=128))
    p.dma("dpool", w2b[0], w2b[0].ap().rearrange("p (dc gc j) -> p dc gc j", dc=4, gc=16),
          sguin, sguin.ap()[512:1024, :].rearrange("(dc p) (gc j) -> p dc gc j", p=128, j=128))
    sdn = p.sb("sdn", (128, 8 * D), BF16)
    p.dma("dpool", sdn, sdn.ap().rearrange("p (fc d) -> p fc d", fc=8),
          sdnin, sdnin.ap().rearrange("(fc p) d -> p fc d", p=128))
    sgw = p.sb("sgw", (128, 8), BF16)
    p.dma("dpool", sgw, sgw.ap().rearrange("p (dc o) -> p dc o", dc=8),
          sgwin, sgwin.ap().rearrange("(dc p) o -> p dc o", p=128),
          allow_slow_non_contiguous=True)

    xt = [p.sb(f"xt{i}", (128, D), F32) for i in range(2)]
    scr = p.sb("scr", (128, D), F32)
    st = p.sb("st", (128, 16), F32)
    yb = p.sb("yb", (128, D), BF16)
    xT = p.sb("xT", (128, 8 * TB), BF16)
    acc = p.sb("acc", (128, NTT * D), F32)
    cw = p.sb("cw", (128, NTT * 8), F32)
    rsc = p.sb("rsc", (128, 8), F32)
    rs2 = p.sb("rs2", (128, 8), F32)
    rz = p.sb("rz", (128, 8), F32)
    rzero = p.sb("rzero", (128, 8), F32)
    rst = p.sb("rst", (128, 8), F32)
    gT = p.sb("gT", (128, 8 * 512), BF16)
    uT = p.sb("uT", (128, 8 * 512), BF16)
    hT = gT

    tr_ps = p.psum("tr_ps", (128, 1024), BF16)
    tr32_ps = p.psum("tr32_ps", (128, 128), F32)
    h_ps = p.psum("h_ps", (128, 512), F32)
    o_ps = [p.psum(f"o_ps{i}", (128, D), F32) for i in range(2)]
    r_ps = p.psum("r_ps", (128, 16), F32)

    p.op("dve", lambda e: e.memset(rzero.ap(), 0.0), reads=[], writes=[rzero])
    for t in range(NTT):
        xb = xt[t % 2]
        p.dma("dsp", xb, xb.ap(), *xsrc(t))
        mean, rstd = emit_ln_stats(p, xb, xb.ap(), scr, st, 0)
        yf32 = p.sb(f"yf32", (128, D), F32) if t == 0 else yf32
        emit_ln_apply(p, yf32, yf32.ap(), xb, xb.ap(), st, mean, rstd)
        p.op("dve", lambda e: e.tensor_copy(yb.ap(), yf32.ap()),
             reads=[yf32], writes=[yb])
        for dc in range(8):
            sl = slice(dc * 128, (dc + 1) * 128)
            p.op("pe", lambda e, sl=sl: e.transpose(tr_ps[:, sl], yb[:, sl],
                                                    ident.ap()),
                 reads=[yb, ident], writes=[tr_ps])
        p.op("act", lambda e, t=t: e.copy(
                xT.ap().rearrange("p (dc tb) -> p dc tb", dc=8)[:, :, t * 128:(t + 1) * 128],
                tr_ps.ap().rearrange("p (dc j) -> p dc j", dc=8)),
             reads=[tr_ps], writes=[xT])
        for dc in range(8):
            sl = slice(dc * 128, (dc + 1) * 128)
            p.op("pe", lambda e, sl=sl: e.transpose(tr32_ps.ap(), yf32[:, sl],
                                                    ident32.ap()),
                 reads=[yf32, ident32], writes=[tr32_ps])
            p.op("dve", lambda e: e.tensor_copy(yT32.ap(), tr32_ps.ap()),
                 reads=[tr32_ps], writes=[yT32])
            p.op("pe", lambda e, dc=dc: e.matmul(
                    r_ps[:, 0:8], yT32.ap(),
                    rw.ap().rearrange("p (dc e) -> p dc e", dc=8)[:, dc, :],
                    start=(dc == 0), stop=(dc == 7)),
                 reads=[yT32, rw], writes=[r_ps])
        p.op("dve", lambda e: e.reduce_max(rst[:, 0:1], r_ps[:, 0:8], axis=AX.X),
             reads=[r_ps, rst], writes=[rst])
        p.op("dve", lambda e: e.tensor_scalar_mul(rst[:, 0:1], rst[:, 0:1], -1.0),
             reads=[rst], writes=[rst])
        p.op("act", lambda e: e.activation(rsc.ap(), r_ps[:, 0:8], AF.Exp,
                                           bias=rst[:, 0:1], accum_out=rst[:, 1:2]),
             reads=[r_ps, rst], writes=[rsc, rst])
        p.op("dve", lambda e: e.reciprocal(rst[:, 2:3], rst[:, 1:2]),
             reads=[rst], writes=[rst])
        p.op("dve", lambda e: e.tensor_scalar_mul(rsc.ap(), rsc.ap(), rst[:, 2:3]),
             reads=[rst, rsc], writes=[rsc])
        p.op("dve", lambda e: e.reduce_max(rst[:, 3:4], rsc.ap(), axis=AX.X),
             reads=[rsc, rst], writes=[rst])
        p.op("dve", lambda e: e.tensor_single_scalar(rs2.ap(), rsc.ap(),
                                                     rst[:, 3:4], op=ALU.is_ge),
             reads=[rsc, rst], writes=[rs2])
        p.op("dve", lambda e: e.tensor_tensor_scan(rz.ap(), rs2.ap(), rzero.ap(),
                                                   0.0, op0=ALU.add, op1=ALU.add),
             reads=[rs2, rzero], writes=[rz])
        p.op("dve", lambda e: e.tensor_single_scalar(rz.ap(), rz.ap(), 1.0,
                                                     op=ALU.is_le),
             reads=[rz], writes=[rz])
        p.op("dve", lambda e: e.tensor_mul(rs2.ap(), rs2.ap(), rz.ap()),
             reads=[rz, rs2], writes=[rs2])
        cwv = cw.ap().rearrange("p (t e) -> p t e", t=NTT)
        p.op("dve", lambda e, t=t: e.tensor_scalar_mul(
                cwv[:, t, :], rs2.ap(), rst[:, 3:4]),
             reads=[rs2, rst, cw], writes=[cw])
        p.op("dve", lambda e: e.scalar_tensor_tensor(
                rs2.ap(), rs2.ap(), -1.0, rsc.ap(), op0=ALU.mult, op1=ALU.add),
             reads=[rsc, rs2], writes=[rs2], sig=True)
        p.op("dve", lambda e: e.reduce_max(rst[:, 4:5], rs2.ap(), axis=AX.X),
             reads=[rs2, rst], writes=[rst])
        p.op("dve", lambda e: e.tensor_single_scalar(rs2.ap(), rs2.ap(),
                                                     rst[:, 4:5], op=ALU.is_ge),
             reads=[rs2, rst], writes=[rs2])
        p.op("dve", lambda e: e.tensor_tensor_scan(rz.ap(), rs2.ap(), rzero.ap(),
                                                   0.0, op0=ALU.add, op1=ALU.add),
             reads=[rs2, rzero], writes=[rz])
        p.op("dve", lambda e: e.tensor_single_scalar(rz.ap(), rz.ap(), 1.0,
                                                     op=ALU.is_le),
             reads=[rz], writes=[rz])
        p.op("dve", lambda e: e.tensor_mul(rs2.ap(), rs2.ap(), rz.ap()),
             reads=[rz, rs2], writes=[rs2])
        p.op("dve", lambda e, t=t: e.scalar_tensor_tensor(
                cwv[:, t, :], rs2.ap(), rst[:, 4:5], cwv[:, t, :],
                op0=ALU.mult, op1=ALU.add),
             reads=[rs2, rst, cw], writes=[cw])

    accv = acc.ap().rearrange("p (t d) -> p t d", t=NTT)
    xTv = xT.ap().rearrange("p (dc tb) -> p dc tb", dc=8)
    sguvA = w1b[0].ap().rearrange("p (dc gc j) -> p dc gc j", dc=4, gc=16)
    sguvB = w2b[0].ap().rearrange("p (dc gc j) -> p dc gc j", dc=4, gc=16)
    def sguv_ap(dc, gc):
        return (sguvA if dc < 4 else sguvB)[:, dc % 4, gc, :]
    def sguv_buf(dc):
        return w1b[0] if dc < 4 else w2b[0]
    for tb in range(4):
        tsl = slice(tb * 512, (tb + 1) * 512)
        for gc in range(16):
            for dc in range(8):
                p.op("pe", lambda e, gc=gc, dc=dc, tsl=tsl: e.matmul(
                        h_ps.ap(), sguv_ap(dc, gc), xTv[:, dc, tsl],
                        start=(dc == 0), stop=(dc == 7)),
                     reads=[sguv_buf(dc), xT], writes=[h_ps])
            if gc < 8:
                p.op("act", lambda e, gc=gc: e.activation(
                        gT.ap().rearrange("p (fc tb) -> p fc tb", fc=8)[:, gc, :],
                        h_ps.ap(), AF.Silu),
                     reads=[h_ps], writes=[gT])
            else:
                p.op("dve", lambda e, gc=gc: e.tensor_mul(
                        uT.ap().rearrange("p (fc tb) -> p fc tb", fc=8)[:, gc - 8, :],
                        h_ps.ap(),
                        gT.ap().rearrange("p (fc tb) -> p fc tb", fc=8)[:, gc - 8, :]),
                     reads=[h_ps, gT], writes=[uT])
        uTv = uT.ap().rearrange("p (fc tb) -> p fc tb", fc=8)
        sgwv = sgw.ap().rearrange("p (dc o) -> p dc o", dc=8)
        sdnv = sdn.ap().rearrange("p (fc d) -> p fc d", fc=8)
        for tc in range(4):
            t = tb * 4 + tc
            csl = slice(tc * 128, (tc + 1) * 128)
            for dc in range(8):
                p.op("pe", lambda e, dc=dc, t=t, csl=None: e.matmul(
                        r_ps[:, 8:9], xTv[:, dc, t * 128:(t + 1) * 128],
                        sgwv[:, dc, :], start=(dc == 0), stop=(dc == 7)),
                     reads=[xT, sgw], writes=[r_ps])
            p.op("act", lambda e: e.activation(rst[:, 5:6], r_ps[:, 8:9],
                                               AF.Sigmoid),
                 reads=[r_ps, rst], writes=[rst])
            ops = o_ps[tc % 2]
            for half in range(2):
                hs = slice(half * 512, (half + 1) * 512)
                for fc in range(8):
                    p.op("pe", lambda e, fc=fc, csl=csl, hs=hs: e.matmul(
                            ops[:, hs], uTv[:, fc, csl], sdnv[:, fc, hs],
                            start=(fc == 0), stop=(fc == 7)),
                         reads=[uT, sdn], writes=[ops])
            p.op("dve", lambda e, t=t, ops=ops: e.tensor_scalar_mul(
                    accv[:, t, :], ops.ap(), rst[:, 5:6]),
                 reads=[ops, rst, acc], writes=[acc])

    for e_i in range(8):
        wb1, wb2 = w1b[e_i % 2], w2b[0]
        p.dma("dpool", wb1, wb1.ap().rearrange("p (dc fc j) -> p dc fc j", dc=8, fc=8),
              w1in, w1in[e_i].rearrange("(dc p) (fc j) -> p dc fc j", p=128, j=128))
        p.dma("dpool", wb2, wb2.ap().rearrange("p (fc d) -> p fc d", fc=8),
              w2in, w2in[e_i].rearrange("(fc p) d -> p fc d", p=128))
        w1v = wb1.ap().rearrange("p (dc fc j) -> p dc fc j", dc=8, fc=8)
        w2v = wb2.ap().rearrange("p (fc d) -> p fc d", fc=8)
        cwv = cw.ap().rearrange("p (t e) -> p t e", t=NTT)
        for tb in range(4):
            tsl = slice(tb * 512, (tb + 1) * 512)
            hTv = hT.ap().rearrange("p (fc tb) -> p fc tb", fc=8)
            for fc in range(8):
                for dc in range(8):
                    p.op("pe", lambda e, fc=fc, dc=dc, tsl=tsl: e.matmul(
                            h_ps.ap(), w1v[:, dc, fc, :], xTv[:, dc, tsl],
                            start=(dc == 0), stop=(dc == 7)),
                         reads=[wb1, xT], writes=[h_ps])
                p.op("act", lambda e, fc=fc: e.activation(
                        hTv[:, fc, :], h_ps.ap(), AF.Silu),
                     reads=[h_ps], writes=[hT])
            for tc in range(4):
                t = tb * 4 + tc
                csl = slice(tc * 128, (tc + 1) * 128)
                ops = o_ps[tc % 2]
                for half in range(2):
                    hs = slice(half * 512, (half + 1) * 512)
                    for fc in range(8):
                        p.op("pe", lambda e, fc=fc, csl=csl, hs=hs: e.matmul(
                                ops[:, hs], hTv[:, fc, csl], w2v[:, fc, hs],
                                start=(fc == 0), stop=(fc == 7)),
                             reads=[hT, wb2], writes=[ops])
                p.op("dve", lambda e, t=t, e_i=e_i, ops=ops: e.scalar_tensor_tensor(
                        accv[:, t, :], ops.ap(), cwv[:, t, e_i:e_i + 1],
                        accv[:, t, :], op0=ALU.mult, op1=ALU.add),
                     reads=[ops, cw, acc], writes=[acc])

    uT32 = uT.ap_.bitcast(F32)
    xo_aps = [uT32[:, 0:D], uT32[:, D:2 * D]]
    for t in range(NTT):
        xb = xt[t % 2]
        ob_ap = xo_aps[t % 2]
        p.dma("dsp", xb, xb.ap(), *xsrc(t))
        a_ap = accv[:, t, :]
        mean2, rstd2 = emit_ln_stats(p, acc, a_ap, scr, st, 8)
        p.op("dve", lambda e, a_ap=a_ap: e.tensor_scalar(
                scr.ap(), a_ap, st[:, 8:9], st[:, 11:12],
                op0=ALU.subtract, op1=ALU.mult),
             reads=[acc, st], writes=[scr])
        p.op("dve", lambda e, ob_ap=ob_ap: e.tensor_add(ob_ap, xb.ap(), scr.ap()),
             reads=[xb, scr], writes=[uT])
        dbuf, dap = xdst(t)
        p.dma("dsp", dbuf, dap, uT, ob_ap, unordered_dst=True, sem_buf=uT)

    _ctx.__exit__(None, None, None)
    if standalone:
        return p.finish()
    return None


def prep_l0ffn(inputs, xflat, c, pref=""):
    d = {"rwin": inputs["L0_ffn_router_w"].astype(np.float32),
         "w1in": inputs["L0_ffn_w1"].astype(np.float32),
         "w2in": inputs["L0_ffn_w2"].astype(np.float32),
         "sguin": inputs["L0_ffn_shared_gate_up"].astype(np.float32),
         "sdnin": inputs["L0_ffn_shared_down"].astype(np.float32),
         "sgwin": inputs["L0_ffn_shared_gate_w"].astype(np.float32),
         "identin": np.eye(128, dtype=np.float32)}
    d = {pref + k: v for k, v in d.items()}
    if not pref:
        d["xin"] = slice_tokens(xflat, c)
    return d


def run_l0ffn(inputs, xflat):
    if "l0ffn" not in _nc_cache:
        _nc_cache["l0ffn"] = build_l0ffn()
    nc = _nc_cache["l0ffn"]
    in_maps = [prep_l0ffn(inputs, xflat, c) for c in range(NCORES)]
    res = run_bass_kernel_spmd(nc, in_maps, core_ids=list(range(NCORES)))
    return unslice_tokens([res.results[c]["xout"] for c in range(NCORES)])





def build_l2ffn(p=None, xsrc=None, xdst=None, pref=""):
    standalone = p is None
    if standalone:
        p = P()
        xin = p.dram_in("xin", (TCORE, D), F32)
        xout_t = p.dram_out("xout", (TCORE, D), F32)
        xsrc = lambda t: (xin, xin[t * 128:(t + 1) * 128, :])
        xdst = lambda t: (xout_t, xout_t[t * 128:(t + 1) * 128, :])
    _ctx = p.stage()
    _ctx.__enter__()
    rwin = p.dram_in(pref + "rwin", (D, 8), F32)
    ebin = p.dram_in(pref + "ebin", (128, 8), F32)
    w1in = p.dram_in(pref + "w1in", (10, D, 2048), F32)
    w3in = p.dram_in(pref + "w3in", (10, D, 2048), F32)
    w2in = p.dram_in(pref + "w2in", (10, 2048, D), F32)
    identin = p.dram_in(pref + "identin", (128, 128), F32)

    ident = p.sb("ident", (128, 128), BF16)
    p.dma("dpool", ident, ident.ap(), identin, identin.ap())
    ident32 = p.sb("ident32", (128, 128), F32)
    p.dma("dsp", ident32, ident32.ap(), identin, identin.ap())
    rw = p.sb("rw", (128, 8 * 8), F32)
    p.dma("dsp", rw, rw.ap().rearrange("p (dc e) -> p dc e", dc=8),
          rwin, rwin.ap().rearrange("(dc p) e -> p dc e", p=128))
    eb = p.sb("eb", (128, 8), F32)
    p.dma("dsp", eb, eb.ap(), ebin, ebin.ap())
    yT32 = p.sb("yT32", (128, 128), F32)

    w1b = p.sb("w1b", (128, 64 * 128), BF16)
    w3b = p.sb("w3b", (128, 64 * 128), BF16)
    w2b = p.sb("w2b", (128, 8 * D), BF16)

    xt = [p.sb(f"xt{i}", (128, D), F32) for i in range(2)]
    scr = p.sb("scr", (128, D), F32)
    st = p.sb("st", (128, 16), F32)
    yf32 = p.sb("yf32", (128, D), F32)
    yb = p.sb("yb", (128, D), BF16)
    xT = p.sb("xT", (128, 8 * TCORE), BF16)
    acc = p.sb("acc", (128, NTT * D), F32)
    cw = p.sb("cw", (128, NTT * 8), F32)
    rsc = p.sb("rsc", (128, 8), F32)
    rs2 = p.sb("rs2", (128, 8), F32)
    rz = p.sb("rz", (128, 8), F32)
    rzero = p.sb("rzero", (128, 8), F32)
    rst = p.sb("rst", (128, 16), F32)
    gT = p.sb("gT", (128, 8 * 512), BF16)

    tr_ps = p.psum("tr_ps", (128, 1024), BF16)
    tr32_ps = p.psum("tr32_ps", (128, 128), F32)
    h_ps = p.psum("h_ps", (128, 512), F32)
    o_ps = [p.psum(f"o_ps{i}", (128, D), F32) for i in range(2)]
    r_ps = p.psum("r_ps", (128, 16), F32)

    accv = acc.ap().rearrange("p (t d) -> p t d", t=NTT)
    cwv = cw.ap().rearrange("p (t e) -> p t e", t=NTT)
    xTv = xT.ap().rearrange("p (dc tb) -> p dc tb", dc=8)

    p.op("dve", lambda e: e.memset(rzero.ap(), 0.0), reads=[], writes=[rzero])
    for t in range(NTT):
        xb = xt[t % 2]
        p.dma("dsp", xb, xb.ap(), *xsrc(t))
        mean, rstd = emit_ln_stats(p, xb, xb.ap(), scr, st, 0)
        emit_ln_apply(p, yf32, yf32.ap(), xb, xb.ap(), st, mean, rstd)
        p.op("dve", lambda e: e.tensor_copy(yb.ap(), yf32.ap()),
             reads=[yf32], writes=[yb])
        for dc in range(8):
            sl = slice(dc * 128, (dc + 1) * 128)
            p.op("pe", lambda e, sl=sl: e.transpose(tr_ps[:, sl], yb[:, sl],
                                                    ident.ap()),
                 reads=[yb, ident], writes=[tr_ps])
        p.op("act", lambda e, t=t: e.copy(
                xTv[:, :, t * 128:(t + 1) * 128],
                tr_ps.ap().rearrange("p (dc j) -> p dc j", dc=8)),
             reads=[tr_ps], writes=[xT])
        for dc in range(8):
            sl = slice(dc * 128, (dc + 1) * 128)
            p.op("pe", lambda e, sl=sl: e.transpose(tr32_ps.ap(), yf32[:, sl],
                                                    ident32.ap()),
                 reads=[yf32, ident32], writes=[tr32_ps])
            p.op("dve", lambda e: e.tensor_copy(yT32.ap(), tr32_ps.ap()),
                 reads=[tr32_ps], writes=[yT32])
            p.op("pe", lambda e, dc=dc: e.matmul(
                    r_ps[:, 0:8], yT32.ap(),
                    rw.ap().rearrange("p (dc e) -> p dc e", dc=8)[:, dc, :],
                    start=(dc == 0), stop=(dc == 7)),
                 reads=[yT32, rw], writes=[r_ps])
        p.op("dve", lambda e: e.tensor_add(rsc.ap(), r_ps[:, 0:8], eb.ap()),
             reads=[r_ps, eb], writes=[rsc])
        p.op("dve", lambda e: e.reduce_max(rst[:, 0:1], rsc.ap(), axis=AX.X),
             reads=[rsc, rst], writes=[rst])
        p.op("dve", lambda e: e.tensor_single_scalar(rs2.ap(), rsc.ap(),
                                                     rst[:, 0:1], op=ALU.is_ge),
             reads=[rsc, rst], writes=[rs2])
        p.op("dve", lambda e: e.tensor_tensor_scan(rz.ap(), rs2.ap(), rzero.ap(),
                                                   0.0, op0=ALU.add, op1=ALU.add),
             reads=[rs2, rzero], writes=[rz])
        p.op("dve", lambda e: e.tensor_single_scalar(rz.ap(), rz.ap(), 1.0,
                                                     op=ALU.is_le),
             reads=[rz], writes=[rz])
        p.op("dve", lambda e: e.tensor_mul(rs2.ap(), rs2.ap(), rz.ap()),
             reads=[rz, rs2], writes=[rs2])
        p.op("dve", lambda e: e.scalar_tensor_tensor(
                rz.ap(), rs2.ap(), -1e9, rsc.ap(), op0=ALU.mult, op1=ALU.add),
             reads=[rs2, rsc], writes=[rz])
        p.op("dve", lambda e: e.reduce_max(rst[:, 1:2], rz.ap(), axis=AX.X),
             reads=[rz, rst], writes=[rst])
        p.op("dve", lambda e: e.tensor_single_scalar(rz.ap(), rz.ap(),
                                                     rst[:, 1:2], op=ALU.is_ge),
             reads=[rz, rst], writes=[rz])
        p.op("dve", lambda e: e.tensor_tensor_scan(rsc.ap(), rz.ap(), rzero.ap(),
                                                   0.0, op0=ALU.add, op1=ALU.add),
             reads=[rz, rzero], writes=[rsc])
        p.op("dve", lambda e: e.tensor_single_scalar(rsc.ap(), rsc.ap(), 1.0,
                                                     op=ALU.is_le),
             reads=[rsc], writes=[rsc])
        p.op("dve", lambda e: e.tensor_mul(rz.ap(), rz.ap(), rsc.ap()),
             reads=[rsc, rz], writes=[rz])
        p.op("dve", lambda e: e.tensor_sub(rst[:, 2:3], rst[:, 0:1], rst[:, 1:2]),
             reads=[rst], writes=[rst])
        p.op("act", lambda e: e.activation(rst[:, 3:4], rst[:, 2:3], AF.Sigmoid),
             reads=[rst], writes=[rst])
        p.op("act", lambda e: e.activation(rst[:, 4:5], rst[:, 2:3], AF.Sigmoid,
                                           scale=-1.0),
             reads=[rst], writes=[rst])
        p.op("dve", lambda e, t=t: e.tensor_scalar_mul(
                cwv[:, t, :], rs2.ap(), rst[:, 3:4]),
             reads=[rs2, rst, cw], writes=[cw])
        p.op("dve", lambda e, t=t: e.scalar_tensor_tensor(
                cwv[:, t, :], rz.ap(), rst[:, 4:5], cwv[:, t, :],
                op0=ALU.mult, op1=ALU.add),
             reads=[rz, rst, cw], writes=[cw])
        p.op("dve", lambda e, t=t: e.memset(accv[:, t, :], 0.0),
             reads=[], writes=[acc])

    w1v = w1b.ap().rearrange("p (dc hc j) -> p dc hc j", dc=8, hc=8)
    w3v = w3b.ap().rearrange("p (dc hc j) -> p dc hc j", dc=8, hc=8)
    w2v = w2b.ap().rearrange("p (hc d) -> p hc d", hc=8)
    gTv = gT.ap().rearrange("p (hc tb) -> p hc tb", hc=8)
    for e_i in range(10):
        routed = e_i < 8
        for half in range(2):
            hsl = slice(half * 1024, (half + 1) * 1024)
            p.dma("dpool", w1b, w1v,
                  w1in, w1in[e_i, :, hsl].rearrange("(dc p) (hc j) -> p dc hc j",
                                                    p=128, j=128))
            p.dma("dpool", w3b, w3v,
                  w3in, w3in[e_i, :, hsl].rearrange("(dc p) (hc j) -> p dc hc j",
                                                    p=128, j=128))
            p.dma("dpool", w2b, w2v,
                  w2in, w2in[e_i, hsl, :].rearrange("(hc p) d -> p hc d", p=128))
            for tb in range(4):
                tsl = slice(tb * 512, (tb + 1) * 512)
                for hc in range(8):
                    for dc in range(8):
                        p.op("pe", lambda e, hc=hc, dc=dc, tsl=tsl: e.matmul(
                                h_ps.ap(), w1v[:, dc, hc, :], xTv[:, dc, tsl],
                                start=(dc == 0), stop=(dc == 7)),
                             reads=[w1b, xT], writes=[h_ps])
                    p.op("act", lambda e, hc=hc: e.activation(
                            gTv[:, hc, :], h_ps.ap(), AF.Silu),
                         reads=[h_ps], writes=[gT])
                    for dc in range(8):
                        p.op("pe", lambda e, hc=hc, dc=dc, tsl=tsl: e.matmul(
                                h_ps.ap(), w3v[:, dc, hc, :], xTv[:, dc, tsl],
                                start=(dc == 0), stop=(dc == 7)),
                             reads=[w3b, xT], writes=[h_ps])
                    p.op("dve", lambda e, hc=hc: e.tensor_mul(
                            gTv[:, hc, :], gTv[:, hc, :], h_ps.ap()),
                         reads=[h_ps, gT], writes=[gT])
                for tc in range(4):
                    t = tb * 4 + tc
                    csl = slice(tc * 128, (tc + 1) * 128)
                    ops = o_ps[tc % 2]
                    for nh in range(2):
                        ns = slice(nh * 512, (nh + 1) * 512)
                        for hc in range(8):
                            p.op("pe", lambda e, hc=hc, csl=csl, ns=ns: e.matmul(
                                    ops[:, ns], gTv[:, hc, csl], w2v[:, hc, ns],
                                    start=(hc == 0), stop=(hc == 7)),
                                 reads=[gT, w2b], writes=[ops])
                    if routed:
                        p.op("dve", lambda e, t=t, e_i=e_i, ops=ops: e.scalar_tensor_tensor(
                                accv[:, t, :], ops.ap(), cwv[:, t, e_i:e_i + 1],
                                accv[:, t, :], op0=ALU.mult, op1=ALU.add),
                             reads=[ops, cw, acc], writes=[acc])
                    else:
                        p.op("dve", lambda e, t=t, ops=ops: e.tensor_add(
                                accv[:, t, :], accv[:, t, :], ops.ap()),
                             reads=[ops, acc], writes=[acc])

    gT32 = gT.ap_.bitcast(F32)
    xo_aps = [gT32[:, 0:D], gT32[:, D:2 * D]]
    for t in range(NTT):
        xb = xt[t % 2]
        ob_ap = xo_aps[t % 2]
        p.dma("dsp", xb, xb.ap(), *xsrc(t))
        a_ap = accv[:, t, :]
        mean2, rstd2 = emit_ln_stats(p, acc, a_ap, scr, st, 8)
        p.op("dve", lambda e, a_ap=a_ap: e.tensor_scalar(
                scr.ap(), a_ap, st[:, 8:9], st[:, 11:12],
                op0=ALU.subtract, op1=ALU.mult),
             reads=[acc, st], writes=[scr])
        p.op("dve", lambda e, ob_ap=ob_ap: e.tensor_add(ob_ap, xb.ap(), scr.ap()),
             reads=[xb, scr], writes=[gT])
        dbuf, dap = xdst(t)
        p.dma("dsp", dbuf, dap, gT, ob_ap, unordered_dst=True, sem_buf=gT)

    _ctx.__exit__(None, None, None)
    if standalone:
        return p.finish()
    return None


def prep_l2ffn(inputs, xflat, c, pref=""):
    w1 = np.concatenate([inputs["L2_ffn_w1"], inputs["L2_ffn_sw1"]], 0)
    w3 = np.concatenate([inputs["L2_ffn_w3"], inputs["L2_ffn_sw3"]], 0)
    w2 = np.concatenate([inputs["L2_ffn_w2"], inputs["L2_ffn_sw2"]], 0)
    d = {"rwin": np.ascontiguousarray(inputs["L2_ffn_router_w"].T.astype(np.float32)),
         "ebin": np.broadcast_to(inputs["L2_ffn_expert_bias"].astype(np.float32),
                                 (128, 8)).copy(),
         "w1in": np.ascontiguousarray(w1.transpose(0, 2, 1).astype(np.float32)),
         "w3in": np.ascontiguousarray(w3.transpose(0, 2, 1).astype(np.float32)),
         "w2in": np.ascontiguousarray(w2.transpose(0, 2, 1).astype(np.float32)),
         "identin": np.eye(128, dtype=np.float32)}
    d = {pref + k: v for k, v in d.items()}
    if not pref:
        d["xin"] = slice_tokens(xflat, c)
    return d


def run_l2ffn(inputs, xflat):
    if "l2ffn" not in _nc_cache:
        _nc_cache["l2ffn"] = build_l2ffn()
    nc = _nc_cache["l2ffn"]
    in_maps = [prep_l2ffn(inputs, xflat, c) for c in range(NCORES)]
    res = run_bass_kernel_spmd(nc, in_maps, core_ids=list(range(NCORES)))
    return unslice_tokens([res.results[c]["xout"] for c in range(NCORES)])





def build_l3ffn_h1(p=None, xsrc=None, pref="", yout=None, qkvout=None,
                   kvout=None):
    standalone = p is None
    if standalone:
        p = P()
        xin = p.dram_in("xin", (TCORE, D), F32)
        xsrc = lambda t: (xin, xin[t * 128:(t + 1) * 128, :])
        yout = p.dram_out("yout", (TCORE, D), F32)
        qkvout = p.dram_out("qkvout", (TCORE, 3 * D), BF16)
    _ctx = p.stage()
    _ctx.__enter__()
    ipwin = p.dram_in(pref + "ipwin", (D, 3 * D), F32)
    ipbin = p.dram_in(pref + "ipbin", (128, 3 * D), F32)
    ln1win = p.dram_in(pref + "ln1win", (128, D), F32)
    ln1bin = p.dram_in(pref + "ln1bin", (128, D), F32)
    identin = p.dram_in(pref + "identin", (128, 128), F32)

    ident = p.sb("ident", (128, 128), BF16)
    p.dma("dpool", ident, ident.ap(), identin, identin.ap())
    ipw = p.sb("ipw", (128, 8 * 3 * D), BF16)
    p.dma("dpool", ipw, ipw.ap().rearrange("p (dc n) -> p dc n", dc=8),
          ipwin, ipwin.ap().rearrange("(dc p) n -> p dc n", p=128))
    ipb = p.sb("ipb", (128, 3 * D), F32)
    p.dma("dsp", ipb, ipb.ap(), ipbin, ipbin.ap())
    ln1w = p.sb("ln1w", (128, D), F32)
    p.dma("dsp", ln1w, ln1w.ap(), ln1win, ln1win.ap())
    ln1b = p.sb("ln1b", (128, D), F32)
    p.dma("dsp", ln1b, ln1b.ap(), ln1bin, ln1bin.ap())

    xt = [p.sb(f"xt{i}", (128, D), F32) for i in range(2)]
    scr = p.sb("scr", (128, D), F32)
    st = p.sb("st", (128, 16), F32)
    yf = [p.sb(f"yf{i}", (128, D), F32) for i in range(2)]
    hnb = p.sb("hnb", (128, D), BF16)
    hnT = p.sb("hnT", (128, D), BF16)
    qkv = [p.sb(f"qkv{i}", (128, 3 * D), BF16) for i in range(2)]
    tr_ps = p.psum("tr_ps", (128, 1024), BF16)
    q_ps = [p.psum(f"q_ps{i}", (128, 512), F32) for i in range(6)]

    for t in range(NTT):
        xb = xt[t % 2]
        yb_ = yf[t % 2]
        qb = qkv[t % 2]
        p.dma("dsp", xb, xb.ap(), *xsrc(t))
        mean, rstd = emit_ln_stats(p, xb, xb.ap(), scr, st, 0)
        emit_ln_apply(p, yb_, yb_.ap(), xb, xb.ap(), st, mean, rstd)
        p.dma("dsp", yout, yout[t * 128:(t + 1) * 128, :], yb_, yb_.ap(),
              unordered_dst=True, sem_buf=yb_)
        mean2, rstd2 = emit_ln_stats(p, yb_, yb_.ap(), scr, st, 8)
        p.op("dve", lambda e, yb_=yb_: e.tensor_scalar(
                scr.ap(), yb_.ap(), st[:, 8:9], st[:, 11:12],
                op0=ALU.subtract, op1=ALU.mult),
             reads=[yb_, st], writes=[scr])
        p.op("dve", lambda e: e.tensor_mul(scr.ap(), scr.ap(), ln1w.ap()),
             reads=[ln1w, scr], writes=[scr])
        p.op("dve", lambda e: e.tensor_add(scr.ap(), scr.ap(), ln1b.ap()),
             reads=[ln1b, scr], writes=[scr])
        p.op("dve", lambda e: e.tensor_copy(hnb.ap(), scr.ap()),
             reads=[scr], writes=[hnb])
        for dc in range(8):
            sl = slice(dc * 128, (dc + 1) * 128)
            p.op("pe", lambda e, sl=sl: e.transpose(tr_ps[:, sl], hnb[:, sl],
                                                    ident.ap()),
                 reads=[hnb, ident], writes=[tr_ps])
        p.op("act", lambda e: e.copy(hnT.ap(), tr_ps.ap()),
             reads=[tr_ps], writes=[hnT])
        ipwv = ipw.ap().rearrange("p (dc n) -> p dc n", dc=8)
        for nh in range(6):
            ns = slice(nh * 512, (nh + 1) * 512)
            ps = q_ps[nh]
            for dc in range(8):
                p.op("pe", lambda e, dc=dc, ns=ns, ps=ps: e.matmul(
                        ps.ap(), hnT[:, dc * 128:(dc + 1) * 128], ipwv[:, dc, ns],
                        start=(dc == 0), stop=(dc == 7)),
                     reads=[hnT, ipw], writes=[ps])
            p.op("dve", lambda e, ns=ns, ps=ps, qb=qb: e.tensor_add(
                    qb[:, ns], ps.ap(), ipb[:, ns]),
                 reads=[ps, ipb, qb], writes=[qb])
        p.dma("dsp", qkvout, qkvout[t * 128:(t + 1) * 128, :], qb, qb.ap(),
              unordered_dst=True, sem_buf=qb)
        if kvout is not None:
            p.dma("dsp", kvout, kvout[t * 128:(t + 1) * 128, :],
                  qb, qb[:, D:3 * D], unordered_dst=True, sem_buf=qb)
    _ctx.__exit__(None, None, None)
    if standalone:
        return p.finish()
    return None


def build_l3ffn_h2(p=None, xsrc=None, xdst=None, pref="", yin=None,
                   qkvin=None, pkv_fn=None):
    standalone = p is None
    if standalone:
        p = P()
        xin = p.dram_in("xin", (TCORE, D), F32)
        yin = p.dram_in("yin", (TCORE, D), F32)
        qkvin = p.dram_in("qkvin", (TCORE, 3 * D), BF16)
        pkvin = p.dram_in("pkvin", (TCORE, 2 * D), BF16)
        pkv_fn = lambda t: (pkvin, pkvin[t * 128:(t + 1) * 128, :])
        xout_t = p.dram_out("xout", (TCORE, D), F32)
        xsrc = lambda t: (xin, xin[t * 128:(t + 1) * 128, :])
        xdst = lambda t: (xout_t, xout_t[t * 128:(t + 1) * 128, :])
    _ctx = p.stage()
    _ctx.__enter__()
    b1in = p.dram_in(pref + "b1in", (128, 1), F32)
    opwin = p.dram_in(pref + "opwin", (D, D), F32)
    opbin = p.dram_in(pref + "opbin", (128, D), F32)
    ln2win = p.dram_in(pref + "ln2win", (128, D), F32)
    ln2bin = p.dram_in(pref + "ln2bin", (128, D), F32)
    rwin = p.dram_in(pref + "rwin", (D, 8), F32)
    w1in = p.dram_in(pref + "w1in", (8, D, 2048), F32)
    w2in = p.dram_in(pref + "w2in", (8, 2048, D), F32)
    identin = p.dram_in(pref + "identin", (128, 128), F32)

    ident = p.sb("ident", (128, 128), BF16)
    p.dma("dpool", ident, ident.ap(), identin, identin.ap())
    ident32 = p.sb("ident32", (128, 128), F32)
    p.dma("dsp", ident32, ident32.ap(), identin, identin.ap())
    b1f = p.sb("b1f", (128, 1), F32)
    p.dma("dsp", b1f, b1f.ap(), b1in, b1in.ap())
    opw = p.sb("opw", (128, 8 * D), BF16)
    p.dma("dpool", opw, opw.ap().rearrange("p (dc n) -> p dc n", dc=8),
          opwin, opwin.ap().rearrange("(dc p) n -> p dc n", p=128))
    opb = p.sb("opb", (128, D), F32)
    p.dma("dsp", opb, opb.ap(), opbin, opbin.ap())
    ln2w = p.sb("ln2w", (128, D), F32)
    p.dma("dsp", ln2w, ln2w.ap(), ln2win, ln2win.ap())
    ln2b = p.sb("ln2b", (128, D), F32)
    p.dma("dsp", ln2b, ln2b.ap(), ln2bin, ln2bin.ap())
    rw = p.sb("rw", (128, 8 * 8), F32)
    p.dma("dsp", rw, rw.ap().rearrange("p (dc e) -> p dc e", dc=8),
          rwin, rwin.ap().rearrange("(dc p) e -> p dc e", p=128))

    w1bs = [p.sb("w1b0", (128, 64 * 128), BF16)]
    w2bs = [p.sb("w2b0", (128, 8 * D), BF16)]

    xt = [p.sb("xt0", (128, D), F32)] * 2
    yt = xt
    qt = [p.sb("qt0", (128, 3 * D), BF16)] * 2
    pt = [p.sb("pt0", (128, 2 * D), BF16)] * 2
    scr = p.sb("scr", (128, D), F32)
    scr2 = p.sb("scr2", (128, D), F32)
    st = p.sb("st", (128, 16), F32)
    sw = p.sb("sw", (128, 6 * NH), F32)
    hn2 = p.sb("hn2", (128, D), F32)
    hnb = p.sb("hnb", (128, D), BF16)
    atT = p.sb("atT", (128, D), BF16)
    xT = p.sb("xT", (128, 8 * TCORE), BF16)
    acc = p.sb("acc", (128, NTT * D), F32)
    cw = p.sb("cw", (128, NTT * 8), F32)
    rsc = p.sb("rsc", (128, 8), F32)
    rs2 = p.sb("rs2", (128, 8), F32)
    rz = p.sb("rz", (128, 8), F32)
    rzero = p.sb("rzero", (128, 8), F32)
    rst = p.sb("rst", (128, 16), F32)
    gT = p.sb("gT", (128, 8 * 512), BF16)

    tr_ps = p.psum("tr_ps", (128, 1024), BF16)
    tr32_ps = p.psum("tr32_ps", (128, 128), F32)
    h_ps = p.psum("h_ps", (128, 512), F32)
    o_ps = [p.psum(f"o_ps{i}", (128, D), F32) for i in range(2)]
    r_ps = p.psum("r_ps", (128, 16), F32)

    accv = acc.ap().rearrange("p (t d) -> p t d", t=NTT)
    cwv = cw.ap().rearrange("p (t e) -> p t e", t=NTT)
    xTv = xT.ap().rearrange("p (dc tb) -> p dc tb", dc=8)
    swv = sw.ap().rearrange("p (k h) -> p k h", k=6)

    p.op("dve", lambda e: e.memset(rzero.ap(), 0.0), reads=[], writes=[rzero])
    for t in range(NTT):
        yb_, qb, pb = yt[t % 2], qt[t % 2], pt[t % 2]
        p.dma("dsp", yb_, yb_.ap(), yin, yin[t * 128:(t + 1) * 128, :])
        p.dma("dsp", qb, qb.ap(), qkvin, qkvin[t * 128:(t + 1) * 128, :])
        p.dma("dsp", pb, pb.ap(), *pkv_fn(t))
        q3 = qb.ap()[:, 0:D].rearrange("p (h d) -> p h d", h=NH)
        ko3 = qb.ap()[:, D:2 * D].rearrange("p (h d) -> p h d", h=NH)
        vo3 = qb.ap()[:, 2 * D:3 * D].rearrange("p (h d) -> p h d", h=NH)
        kp3 = pb.ap()[:, 0:D].rearrange("p (h d) -> p h d", h=NH)
        vp3 = pb.ap()[:, D:2 * D].rearrange("p (h d) -> p h d", h=NH)
        p.op("dve", lambda e: e.tensor_mul(
                scr.ap().rearrange("p (h d) -> p h d", h=NH), q3, kp3),
             reads=[qb, pb], writes=[scr])
        p.op("dve", lambda e: e.reduce_sum(
                swv[:, 0, :], scr.ap().rearrange("p (h d) -> p h d", h=NH),
                axis=AX.X),
             reads=[scr, sw], writes=[sw])
        p.op("dve", lambda e: e.tensor_mul(
                scr.ap().rearrange("p (h d) -> p h d", h=NH), q3, ko3),
             reads=[qb], writes=[scr])
        p.op("dve", lambda e: e.reduce_sum(
                swv[:, 1, :], scr.ap().rearrange("p (h d) -> p h d", h=NH),
                axis=AX.X),
             reads=[scr, sw], writes=[sw])
        p.op("dve", lambda e: e.tensor_sub(swv[:, 2, :], swv[:, 0, :],
                                           swv[:, 1, :]),
             reads=[sw], writes=[sw])
        p.op("act", lambda e: e.activation(swv[:, 3, :], swv[:, 2, :],
                                           AF.Sigmoid, scale=0.125),
             reads=[sw], writes=[sw])
        p.op("dve", lambda e: e.tensor_scalar_mul(swv[:, 4, :], swv[:, 3, :],
                                                  b1f[:, 0:1]),
             reads=[sw, b1f], writes=[sw])
        p.op("dve", lambda e: e.tensor_scalar(swv[:, 5, :], swv[:, 4, :],
                                              -1.0, 1.0, op0=ALU.mult, op1=ALU.add),
             reads=[sw], writes=[sw])
        p.op("dve", lambda e: e.tensor_tensor(
                scr.ap().rearrange("p (h d) -> p h d", h=NH),
                vp3, swv[:, 4, :].rearrange("p (h o) -> p h o", o=1)
                .broadcast_to((128, NH, HD)), op=ALU.mult),
             reads=[pb, sw], writes=[scr])
        p.op("dve", lambda e: e.tensor_tensor(
                scr2.ap().rearrange("p (h d) -> p h d", h=NH),
                vo3, swv[:, 5, :].rearrange("p (h o) -> p h o", o=1)
                .broadcast_to((128, NH, HD)), op=ALU.mult),
             reads=[qb, sw], writes=[scr2])
        p.op("dve", lambda e: e.tensor_add(scr.ap(), scr.ap(), scr2.ap()),
             reads=[scr2, scr], writes=[scr])
        p.op("dve", lambda e: e.tensor_copy(hnb.ap(), scr.ap()),
             reads=[scr], writes=[hnb])
        for dc in range(8):
            sl = slice(dc * 128, (dc + 1) * 128)
            p.op("pe", lambda e, sl=sl: e.transpose(tr_ps[:, sl], hnb[:, sl],
                                                    ident.ap()),
                 reads=[hnb, ident], writes=[tr_ps])
        opwv = opw.ap().rearrange("p (dc n) -> p dc n", dc=8)
        p.op("act", lambda e: e.copy(atT.ap(), tr_ps.ap()),
             reads=[tr_ps], writes=[atT])
        for nh in range(2):
            ns = slice(nh * 512, (nh + 1) * 512)
            ps = o_ps[nh]
            for dc in range(8):
                p.op("pe", lambda e, dc=dc, ns=ns, ps=ps: e.matmul(
                        ps.ap()[:, 0:512], atT[:, dc * 128:(dc + 1) * 128],
                        opwv[:, dc, ns], start=(dc == 0), stop=(dc == 7)),
                     reads=[atT, opw], writes=[ps])
        for nh in range(2):
            ns = slice(nh * 512, (nh + 1) * 512)
            ps = o_ps[nh]
            p.op("dve", lambda e, ns=ns, ps=ps: e.tensor_add(
                    scr[:, ns], ps.ap()[:, 0:512], opb[:, ns]),
                 reads=[ps, opb], writes=[scr])
        p.op("dve", lambda e, t=t: e.tensor_add(accv[:, t, :], yb_.ap(), scr.ap()),
             reads=[yb_, scr, acc], writes=[acc])
        a_ap = accv[:, t, :]
        mean2, rstd2 = emit_ln_stats(p, acc, a_ap, scr2, st, 8)
        p.op("dve", lambda e, a_ap=a_ap: e.tensor_scalar(
                hn2.ap(), a_ap, st[:, 8:9], st[:, 11:12],
                op0=ALU.subtract, op1=ALU.mult),
             reads=[acc, st], writes=[hn2])
        p.op("dve", lambda e: e.tensor_mul(hn2.ap(), hn2.ap(), ln2w.ap()),
             reads=[ln2w, hn2], writes=[hn2])
        p.op("dve", lambda e: e.tensor_add(hn2.ap(), hn2.ap(), ln2b.ap()),
             reads=[ln2b, hn2], writes=[hn2])
        p.op("dve", lambda e: e.tensor_copy(hnb.ap(), hn2.ap()),
             reads=[hn2], writes=[hnb])
        for dc in range(8):
            sl = slice(dc * 128, (dc + 1) * 128)
            p.op("pe", lambda e, sl=sl: e.transpose(tr_ps[:, sl], hnb[:, sl],
                                                    ident.ap()),
                 reads=[hnb, ident], writes=[tr_ps])
        p.op("act", lambda e, t=t: e.copy(
                xTv[:, :, t * 128:(t + 1) * 128],
                tr_ps.ap().rearrange("p (dc j) -> p dc j", dc=8)),
             reads=[tr_ps], writes=[xT])
        for dc in range(8):
            sl = slice(dc * 128, (dc + 1) * 128)
            p.op("pe", lambda e, sl=sl: e.transpose(tr32_ps.ap(), hn2[:, sl],
                                                    ident32.ap()),
                 reads=[hn2, ident32], writes=[tr32_ps])
            p.op("dve", lambda e: e.tensor_copy(scr[:, 0:128], tr32_ps.ap()),
                 reads=[tr32_ps], writes=[scr])
            p.op("pe", lambda e, dc=dc: e.matmul(
                    r_ps[:, 0:8], scr[:, 0:128],
                    rw.ap().rearrange("p (dc e) -> p dc e", dc=8)[:, dc, :],
                    start=(dc == 0), stop=(dc == 7)),
                 reads=[scr, rw], writes=[r_ps])
        p.op("dve", lambda e: e.reduce_max(rst[:, 0:1], r_ps[:, 0:8], axis=AX.X),
             reads=[r_ps, rst], writes=[rst])
        p.op("dve", lambda e: e.tensor_scalar_mul(rst[:, 0:1], rst[:, 0:1], -1.0),
             reads=[rst], writes=[rst])
        p.op("act", lambda e: e.activation(rsc.ap(), r_ps[:, 0:8], AF.Exp,
                                           bias=rst[:, 0:1], accum_out=rst[:, 1:2]),
             reads=[r_ps, rst], writes=[rsc, rst])
        p.op("dve", lambda e: e.reciprocal(rst[:, 2:3], rst[:, 1:2]),
             reads=[rst], writes=[rst])
        p.op("dve", lambda e: e.tensor_scalar_mul(rsc.ap(), rsc.ap(), rst[:, 2:3]),
             reads=[rst, rsc], writes=[rsc])
        p.op("dve", lambda e: e.reduce_max(rst[:, 3:4], rsc.ap(), axis=AX.X),
             reads=[rsc, rst], writes=[rst])
        p.op("dve", lambda e: e.tensor_single_scalar(rs2.ap(), rsc.ap(),
                                                     rst[:, 3:4], op=ALU.is_ge),
             reads=[rsc, rst], writes=[rs2])
        p.op("dve", lambda e: e.tensor_tensor_scan(rz.ap(), rs2.ap(), rzero.ap(),
                                                   0.0, op0=ALU.add, op1=ALU.add),
             reads=[rs2, rzero], writes=[rz])
        p.op("dve", lambda e: e.tensor_single_scalar(rz.ap(), rz.ap(), 1.0,
                                                     op=ALU.is_le),
             reads=[rz], writes=[rz])
        p.op("dve", lambda e: e.tensor_mul(rs2.ap(), rs2.ap(), rz.ap()),
             reads=[rz, rs2], writes=[rs2])
        p.op("dve", lambda e, t=t: e.tensor_scalar_mul(
                cwv[:, t, :], rs2.ap(), rst[:, 3:4]),
             reads=[rs2, rst, cw], writes=[cw])
        p.op("dve", lambda e: e.scalar_tensor_tensor(
                rs2.ap(), rs2.ap(), -1.0, rsc.ap(), op0=ALU.mult, op1=ALU.add),
             reads=[rsc, rs2], writes=[rs2])
        p.op("dve", lambda e: e.reduce_max(rst[:, 4:5], rs2.ap(), axis=AX.X),
             reads=[rs2, rst], writes=[rst])
        p.op("dve", lambda e: e.tensor_single_scalar(rs2.ap(), rs2.ap(),
                                                     rst[:, 4:5], op=ALU.is_ge),
             reads=[rs2, rst], writes=[rs2])
        p.op("dve", lambda e: e.tensor_tensor_scan(rz.ap(), rs2.ap(), rzero.ap(),
                                                   0.0, op0=ALU.add, op1=ALU.add),
             reads=[rs2, rzero], writes=[rz])
        p.op("dve", lambda e: e.tensor_single_scalar(rz.ap(), rz.ap(), 1.0,
                                                     op=ALU.is_le),
             reads=[rz], writes=[rz])
        p.op("dve", lambda e: e.tensor_mul(rs2.ap(), rs2.ap(), rz.ap()),
             reads=[rz, rs2], writes=[rs2])
        p.op("dve", lambda e, t=t: e.scalar_tensor_tensor(
                cwv[:, t, :], rs2.ap(), rst[:, 4:5], cwv[:, t, :],
                op0=ALU.mult, op1=ALU.add),
             reads=[rs2, rst, cw], writes=[cw])

    gTv = gT.ap().rearrange("p (fc tb) -> p fc tb", fc=8)
    for e_i in range(8):
        for half in range(2):
            w1b, w2b = w1bs[0], w2bs[0]
            w1v = w1b.ap().rearrange("p (dc fc j) -> p dc fc j", dc=8, fc=8)
            w2v = w2b.ap().rearrange("p (fc d) -> p fc d", fc=8)
            fsl = slice(half * 1024, (half + 1) * 1024)
            p.dma("dpool", w1b, w1v,
                  w1in, w1in[e_i, :, fsl].rearrange("(dc p) (fc j) -> p dc fc j",
                                                    p=128, j=128))
            p.dma("dpool", w2b, w2v,
                  w2in, w2in[e_i, fsl, :].rearrange("(fc p) d -> p fc d", p=128))
            for tb in range(4):
                tsl = slice(tb * 512, (tb + 1) * 512)
                for fc in range(8):
                    for dc in range(8):
                        p.op("pe", lambda e, fc=fc, dc=dc, tsl=tsl: e.matmul(
                                h_ps.ap(), w1v[:, dc, fc, :], xTv[:, dc, tsl],
                                start=(dc == 0), stop=(dc == 7)),
                             reads=[w1b, xT], writes=[h_ps])
                    p.op("act", lambda e, fc=fc: e.activation(
                            gTv[:, fc, :], h_ps.ap(), AF.Gelu),
                         reads=[h_ps], writes=[gT])
                for tc in range(4):
                    t = tb * 4 + tc
                    csl = slice(tc * 128, (tc + 1) * 128)
                    ops = o_ps[tc % 2]
                    for nh in range(2):
                        ns = slice(nh * 512, (nh + 1) * 512)
                        for fc in range(8):
                            p.op("pe", lambda e, fc=fc, csl=csl, ns=ns: e.matmul(
                                    ops[:, ns], gTv[:, fc, csl], w2v[:, fc, ns],
                                    start=(fc == 0), stop=(fc == 7)),
                                 reads=[gT, w2b], writes=[ops])
                    p.op("dve", lambda e, t=t, e_i=e_i, ops=ops: e.scalar_tensor_tensor(
                            accv[:, t, :], ops.ap(), cwv[:, t, e_i:e_i + 1],
                            accv[:, t, :], op0=ALU.mult, op1=ALU.add),
                         reads=[ops, cw, acc], writes=[acc])

    gT32 = gT.ap_.bitcast(F32)
    xo_aps = [gT32[:, 0:D], gT32[:, D:2 * D]]
    for t in range(NTT):
        xb = xt[t % 2]
        ob_ap = xo_aps[t % 2]
        p.dma("dsp", xb, xb.ap(), *xsrc(t))
        a_ap = accv[:, t, :]
        mean2, rstd2 = emit_ln_stats(p, acc, a_ap, scr, st, 8)
        p.op("dve", lambda e, a_ap=a_ap: e.tensor_scalar(
                scr.ap(), a_ap, st[:, 8:9], st[:, 11:12],
                op0=ALU.subtract, op1=ALU.mult),
             reads=[acc, st], writes=[scr])
        p.op("dve", lambda e, ob_ap=ob_ap: e.tensor_add(ob_ap, xb.ap(), scr.ap()),
             reads=[xb, scr], writes=[gT])
        dbuf, dap = xdst(t)
        p.dma("dsp", dbuf, dap, gT, ob_ap, unordered_dst=True, sem_buf=gT)
    _ctx.__exit__(None, None, None)
    if standalone:
        return p.finish()
    return None


def _rep(v, rows=128):
    return np.broadcast_to(np.asarray(v, np.float32), (rows,) + np.asarray(v).shape).copy()




def build_l3ffn_f(p=None, xsrc=None, xdst=None, pref=""):
    y_d = p.dram_tmp(pref + "y_d", (TCORE, D), F32)
    qkv_d = p.dram_tmp(pref + "qkv_d", (TCORE, 3 * D), BF16)
    kv_c = p.dram_tmp(pref + "kv_c", (TCORE, 2 * D), BF16)
    kv_ag = p.dram_tmp(pref + "kv_ag", (2 * TCORE, 2 * D), BF16)
    build_l3ffn_h1(p=p, xsrc=xsrc, pref=pref, yout=y_d, qkvout=qkv_d,
                   kvout=kv_c)
    p.cc("AllGather", ALU.bypass, [[0, 4], [1, 5], [2, 6], [3, 7]],
         kv_c, kv_c.ap(), kv_ag, kv_ag.ap())
    prin = p.dram_in(pref + "prank", (1, 1), I32)
    prs = p.sb("prs", (1, 1), I32)
    p.dma("dsp", prs, prs.ap(), prin, prin.ap())
    preg = p.sval("sp", prs, prs[0:1, 0:1])
    pkv_d = p.dram_tmp(pref + "pkv_d", (TCORE, 2 * D), BF16)
    p.dma("dsp", pkv_d, pkv_d.ap(), kv_ag,
          kv_ag.ap().rearrange("(r t) d -> r t d", r=2)[bass.ts(preg, 1), :, :]
          .squeeze(0) if False else
          kv_ag.ap().rearrange("(r t) d -> r t d", r=2)[bass.ts(preg, 1), :, :])

    def pkv_fn(t):
        return (pkv_d, pkv_d[t * 128:(t + 1) * 128, :])
    build_l3ffn_h2(p=p, xsrc=xsrc, xdst=xdst, pref=pref, yin=y_d,
                   qkvin=qkv_d, pkv_fn=pkv_fn)
    return None


def prep_l3ffn_f(inputs, xflat, c, pref=""):
    ident = np.eye(128, dtype=np.float32)
    d = {"ipwin": np.ascontiguousarray(inputs["L3_ffn_in_proj_w"].T.astype(np.float32)),
         "ipbin": _rep(inputs["L3_ffn_in_proj_b"]),
         "ln1win": _rep(inputs["L3_ffn_ln1_w"]),
         "ln1bin": _rep(inputs["L3_ffn_ln1_b"]),
         "b1in": np.full((128, 1), 1.0 if c >= 4 else 0.0, np.float32),
         "opwin": np.ascontiguousarray(inputs["L3_ffn_out_proj_w"].T.astype(np.float32)),
         "opbin": _rep(inputs["L3_ffn_out_proj_b"]),
         "ln2win": _rep(inputs["L3_ffn_ln2_w"]),
         "ln2bin": _rep(inputs["L3_ffn_ln2_b"]),
         "rwin": np.ascontiguousarray(inputs["L3_ffn_router_w"].T.astype(np.float32)),
         "w1in": inputs["L3_ffn_w1"].astype(np.float32),
         "w2in": inputs["L3_ffn_w2"].astype(np.float32),
         "identin": ident,
         "prank": np.array([[1 if c < 4 else 0]], np.int32)}
    return {pref + k: v for k, v in d.items()}


def run_l3ffn(inputs, xflat):
    if "l3ffn_h1" not in _nc_cache:
        _nc_cache["l3ffn_h1"] = build_l3ffn_h1()
        _nc_cache["l3ffn_h2"] = build_l3ffn_h2()
    ident = np.eye(128, dtype=np.float32)
    in1 = []
    for c in range(NCORES):
        in1.append({"xin": slice_tokens(xflat, c),
                    "ipwin": np.ascontiguousarray(
                        inputs["L3_ffn_in_proj_w"].T.astype(np.float32)),
                    "ipbin": _rep(inputs["L3_ffn_in_proj_b"]),
                    "ln1win": _rep(inputs["L3_ffn_ln1_w"]),
                    "ln1bin": _rep(inputs["L3_ffn_ln1_b"]),
                    "identin": ident})
    r1 = run_bass_kernel_spmd(_nc_cache["l3ffn_h1"], in1,
                              core_ids=list(range(NCORES)))
    ys = [r1.results[c]["yout"] for c in range(NCORES)]
    qkvs = [r1.results[c]["qkvout"] for c in range(NCORES)]
    in2 = []
    for c in range(NCORES):
        partner = (c + 4) % 8
        in2.append({"xin": slice_tokens(xflat, c),
                    "yin": ys[c].astype(np.float32),
                    "qkvin": qkvs[c],
                    "pkvin": np.ascontiguousarray(qkvs[partner][:, D:3 * D]),
                    "b1in": np.full((128, 1), 1.0 if c >= 4 else 0.0, np.float32),
                    "opwin": np.ascontiguousarray(
                        inputs["L3_ffn_out_proj_w"].T.astype(np.float32)),
                    "opbin": _rep(inputs["L3_ffn_out_proj_b"]),
                    "ln2win": _rep(inputs["L3_ffn_ln2_w"]),
                    "ln2bin": _rep(inputs["L3_ffn_ln2_b"]),
                    "rwin": np.ascontiguousarray(
                        inputs["L3_ffn_router_w"].T.astype(np.float32)),
                    "w1in": inputs["L3_ffn_w1"].astype(np.float32),
                    "w2in": inputs["L3_ffn_w2"].astype(np.float32),
                    "identin": ident})
    r2 = run_bass_kernel_spmd(_nc_cache["l3ffn_h2"], in2,
                              core_ids=list(range(NCORES)))
    return unslice_tokens([r2.results[c]["xout"] for c in range(NCORES)])





def build_attn_kv(p=None, xsrc=None, pref="", qTout=None, kTout=None,
                  vout=None, qnTout=None, kmaxout=None, xcopy=None):
    standalone = p is None
    if standalone:
        p = P()
        xin = p.dram_in("xin", (TCORE, D), F32)
        xsrc = lambda t: (xin, xin[t * 128:(t + 1) * 128, :])
        qTout = p.dram_out("qTout", (D, TCORE), BF16)
        kTout = p.dram_out("kTout", (D, TCORE), BF16)
        vout = p.dram_out("vout", (TCORE, D), BF16)
        qnout = p.dram_out("qnout", (TCORE, NH), F32)
        knout = p.dram_out("knout", (TCORE, NH), F32)
    _ctx = p.stage()
    _ctx.__enter__()
    wqin = p.dram_in(pref + "wqin", (D, D), F32)
    wkin = p.dram_in(pref + "wkin", (D, D), F32)
    wvin = p.dram_in(pref + "wvin", (D, D), F32)
    bqin = p.dram_in(pref + "bqin", (128, D), F32)
    bkin = p.dram_in(pref + "bkin", (128, D), F32)
    bvin = p.dram_in(pref + "bvin", (128, D), F32)
    identin = p.dram_in(pref + "identin", (128, 128), F32)
    if not standalone:
        ident32 = p.sb("ident32", (128, 128), F32)
        p.dma("dsp", ident32, ident32.ap(), identin, identin.ap())
        qnT_sb = p.sb("qnT_sb", (16, TCORE), F32)
        knT_sb = p.sb("knT_sb", (16, TCORE), F32)
        nt_ps = p.psum("nt_ps", (128, 128), F32)

    ident = p.sb("ident", (128, 128), BF16)
    p.dma("dpool", ident, ident.ap(), identin, identin.ap())
    wq = p.sb("wq", (128, 8 * D), BF16)
    wk = p.sb("wk", (128, 8 * D), BF16)
    wv = p.sb("wv", (128, 8 * D), BF16)
    for wsb, win in ((wq, wqin), (wk, wkin), (wv, wvin)):
        p.dma("dpool", wsb, wsb.ap().rearrange("p (dc n) -> p dc n", dc=8),
              win, win.ap().rearrange("(dc p) n -> p dc n", p=128))
    bq = p.sb("bq", (128, D), F32)
    bk = p.sb("bk", (128, D), F32)
    bv = p.sb("bv", (128, D), F32)
    for bsb, bin_ in ((bq, bqin), (bk, bkin), (bv, bvin)):
        p.dma("dsp", bsb, bsb.ap(), bin_, bin_.ap())

    xt = [p.sb("xt0", (128, D), F32)] * 2
    scr = p.sb("scr", (128, D), F32)
    st = p.sb("st", (128, 16), F32)
    yf = p.sb("yf", (128, D), F32)
    yb = p.sb("yb", (128, D), BF16)
    yT = p.sb("yT", (128, D), BF16)
    qb_ = p.sb("qb_", (128, D), BF16)
    kb_ = p.sb("kb_", (128, D), BF16)
    vb_ = p.sb("vb_", (128, D), BF16)
    qkT = p.sb("qkT", (128, D), BF16)
    nrm = p.sb("nrm", (128, NH), F32)
    tr_ps = p.psum("tr_ps", (128, 1024), BF16)
    pj_ps = [p.psum(f"pj_ps{i}", (128, 512), F32) for i in range(2)]

    for t in range(NTT):
        xb = xt[t % 2]
        p.dma("dsp", xb, xb.ap(), *xsrc(t))
        if xcopy is not None:
            p.dma("dsp", xcopy, xcopy[t * 128:(t + 1) * 128, :], xb, xb.ap(),
                  unordered_dst=True, sem_buf=xb)
        mean, rstd = emit_ln_stats(p, xb, xb.ap(), scr, st, 0)
        emit_ln_apply(p, yf, yf.ap(), xb, xb.ap(), st, mean, rstd)
        p.op("dve", lambda e: e.tensor_copy(yb.ap(), yf.ap()),
             reads=[yf], writes=[yb])
        for dc in range(8):
            sl = slice(dc * 128, (dc + 1) * 128)
            p.op("pe", lambda e, sl=sl: e.transpose(tr_ps[:, sl], yb[:, sl],
                                                    ident.ap()),
                 reads=[yb, ident], writes=[tr_ps])
        p.op("act", lambda e: e.copy(yT.ap(), tr_ps.ap()),
             reads=[tr_ps], writes=[yT])
        for wsb, bsb, dst in ((wq, bq, qb_), (wk, bk, kb_), (wv, bv, vb_)):
            wv_ = wsb.ap().rearrange("p (dc n) -> p dc n", dc=8)
            for nh in range(2):
                ns = slice(nh * 512, (nh + 1) * 512)
                ps = pj_ps[nh]
                for dc in range(8):
                    p.op("pe", lambda e, dc=dc, ns=ns, ps=ps, wv_=wv_: e.matmul(
                            ps.ap(), yT[:, dc * 128:(dc + 1) * 128], wv_[:, dc, ns],
                            start=(dc == 0), stop=(dc == 7)),
                         reads=[yT, wsb], writes=[ps])
                p.op("dve", lambda e, ns=ns, ps=ps, dst=dst, bsb=bsb: e.tensor_add(
                        dst[:, ns], ps.ap(), bsb[:, ns]),
                     reads=[ps, bsb, dst], writes=[dst])
        p.dma("dsp", vout, vout[t * 128:(t + 1) * 128, :], vb_, vb_.ap(),
              unordered_dst=True, sem_buf=vb_)
        for src_, tout, which in ((qb_, qTout, "q"), (kb_, kTout, "k")):
            for dc in range(8):
                sl = slice(dc * 128, (dc + 1) * 128)
                p.op("pe", lambda e, sl=sl, src_=src_: e.transpose(
                        tr_ps[:, sl], src_[:, sl], ident.ap()),
                     reads=[src_, ident], writes=[tr_ps])
            p.op("act", lambda e: e.copy(qkT.ap(), tr_ps.ap()),
                 reads=[tr_ps], writes=[qkT])
            p.dma("dsp", tout,
                  tout.ap().rearrange("(rc p) t -> p rc t", p=128)[:, :, t * 128:(t + 1) * 128],
                  qkT, qkT.ap().rearrange("p (rc j) -> p rc j", rc=8),
                  unordered_dst=True, sem_buf=qkT)
            p.op("act", lambda e, src_=src_: e.activation(scr.ap(), src_.ap(),
                                                          AF.Square),
                 reads=[src_], writes=[scr])
            p.op("dve", lambda e: e.reduce_sum(
                    nrm.ap(), scr.ap().rearrange("p (h d) -> p h d", h=NH),
                    axis=AX.X),
                 reads=[scr, nrm], writes=[nrm])
            p.op("act", lambda e: e.activation(nrm.ap(), nrm.ap(), AF.Sqrt),
                 reads=[nrm], writes=[nrm])
            if standalone:
                nout = qnout if which == "q" else knout
                p.dma("dsp", nout, nout[t * 128:(t + 1) * 128, :], nrm, nrm.ap(),
                      unordered_dst=True, sem_buf=nrm)
            else:
                nTdst = qnT_sb if which == "q" else knT_sb
                p.op("pe", lambda e: e.transpose(nt_ps.ap_[0:16, :], nrm.ap(),
                                                 ident32.ap()),
                     reads=[nrm, ident32], writes=[nt_ps])
                p.op("dve", lambda e, t=t, nTdst=nTdst: e.tensor_copy(
                        nTdst.ap_[:, t * 128:(t + 1) * 128], nt_ps.ap_[0:16, :]),
                     reads=[nt_ps], writes=[nTdst])
    if not standalone:
        kml = p.sb("kml", (16, 1), F32)
        p.op("dve", lambda e: e.reduce_max(kml.ap(), knT_sb.ap(), axis=AX.X),
             reads=[knT_sb, kml], writes=[kml])
        p.dma("dsp", kmaxout, kmaxout.ap(), kml, kml.ap())
        p.dma("dsp", qnTout, qnTout.ap(), qnT_sb, qnT_sb.ap())
    _ctx.__exit__(None, None, None)
    if standalone:
        return p.finish()
    return None


def build_attn_core(causal, p=None, xsrc=None, xdst=None, pref="",
                    qT_src=None, k_src=None, v_src=None, qnT_src=None,
                    kmax_fn=None, plan_fn=None):
    SK = 8192
    NKT = SK // 128
    standalone = p is None
    if standalone:
        p = P()
        xin = p.dram_in("xin", (TCORE, D), F32)
        xsrc = lambda t: (xin, xin[t * 128:(t + 1) * 128, :])
        xout_t = p.dram_out("xout", (TCORE, D), F32)
        xdst = lambda t: (xout_t, xout_t[t * 128:(t + 1) * 128, :])
        qTin = p.dram_in("qTin", (D, TCORE), BF16)
        kTin = p.dram_in("kTin", (D, SK), BF16)
        vin = p.dram_in("vin", (SK, D), BF16)
        qnTin = p.dram_in("qnTin", (NH, TCORE), F32)
        knTin = p.dram_in("knTin", (NH, SK), F32)
        qT_src = (qTin, qTin.ap())
        k_src = lambda kt, hg: (kTin, kTin.ap()[hg * 256:(hg + 1) * 256,
                                                kt * 128:(kt + 1) * 128])
        v_src = lambda kt, hg: (vin, vin.ap()[kt * 128:(kt + 1) * 128,
                                              hg * 256:(hg + 1) * 256])
        qnT_src = (qnTin, qnTin.ap())
    _ctx = p.stage()
    _ctx.__enter__()
    augin = p.dram_in(pref + "augin", (NKT, 512), F32)
    woin = p.dram_in(pref + "woin", (D, D), F32)
    boin = p.dram_in(pref + "boin", (128, D), F32)

    wo = p.sb("wo", (128, 8 * D), BF16)
    p.dma("dpool", wo, wo.ap().rearrange("p (dc n) -> p dc n", dc=8),
          woin, woin.ap().rearrange("(dc p) n -> p dc n", p=128))
    bo = p.sb("bo", (128, D), F32)
    p.dma("dsp", bo, bo.ap(), boin, boin.ap())
    qT = p.sb("qT", (128, 8 * TCORE), BF16)
    p.dma("dsp", qT, qT.ap().rearrange("p (rc t) -> p rc t", rc=8),
          qT_src[0], qT_src[1].rearrange("(rc p) t -> p rc t", p=128))
    kmax = p.sb("kmax", (16, 8), F32)
    if standalone:
        knc = p.sb("knc", (16, 2048), F32)
        for i in range(4):
            p.dma("dsp", knc, knc.ap(), knTin, knTin[:, i * 2048:(i + 1) * 2048])
            p.op("dve", lambda e, i=i: e.reduce_max(kmax[:, i:i + 1], knc.ap(),
                                                    axis=AX.X),
                 reads=[knc, kmax], writes=[kmax])
        p.op("dve", lambda e: e.reduce_max(kmax[:, 4:5], kmax[:, 0:4], axis=AX.X),
             reads=[kmax], writes=[kmax])
    else:
        kmax_fn(p, kmax)
    p.op("dve", lambda e: e.tensor_scalar_mul(kmax[:, 5:6], kmax[:, 4:5], -1.0),
         reads=[kmax], writes=[kmax])
    qnT = p.sb("qnT", (16, TCORE), F32)
    p.dma("dsp", qnT, qnT.ap(), qnT_src[0], qnT_src[1])
    brow = p.sb("brow", (16, TCORE), F32)
    p.op("dve", lambda e: e.tensor_scalar_mul(brow.ap(), qnT.ap(), kmax[:, 5:6]),
         reads=[qnT, kmax], writes=[brow])

    NKB = 4
    ktb = [p.sb(f"ktb{i}", (128, 4 * 128), BF16) for i in range(NKB)]
    vtb = [p.sb(f"vtb{i}", (128, 4 * 65), BF16) for i in range(NKB)]
    for b_ in vtb:
        p.op("dve", lambda e, b_=b_: e.memset(
                b_.ap().rearrange("p (s j) -> p s j", s=4)[:, :, 64:65], 1.0),
             reads=[], writes=[b_])
    ones64 = p.sb("ones64", (1, 64), BF16)
    p.op("dve", lambda e: e.memset(ones64.ap(), 1.0), reads=[], writes=[ones64])
    qp = p.sb("qp", (128, 16 * 512), BF16)
    pt = [p.sb(f"pt{i}", (128, 512), BF16) for i in range(4)]
    pt2 = [p.sb(f"pt2{i}", (128, 512), BF16) for i in range(4)]
    oT = p.sb("oT", (128, 8 * 512), BF16)
    invz = p.sb("invz", (1, 512), F32)
    invzb = p.sb("invzb", (1, 512), BF16)
    ores = p.sb("ores", (128, D), F32)
    xt = [p.sb("xt0", (128, D), F32)] * 2
    scr = p.sb("scr", (128, D), F32)
    st = p.sb("st", (128, 16), F32)

    av_ps = [p.psum(f"av_ps{i}", (128, 512), F32) for i in range(4)]
    s_ps = [p.psum(f"s_ps{i}", (128, 512), F32) for i in range(2)]
    rp_ps = s_ps[0]
    pj_ps = p.psum("pj_ps", (128, 1024), F32)

    qTv = qT.ap().rearrange("p (rc t) -> p rc t", rc=8)
    qpv = qp.ap().rearrange("p (h q) -> p h q", h=16)
    ktbv = [b.ap().rearrange("p (s j) -> p s j", s=4) for b in ktb]
    vtbv = [b.ap().rearrange("p (s j) -> p s j", s=4) for b in vtb]

    if plan_fn is None:
        def plan_fn(qb):
            nkt = (48 + 4 * qb + 4) if causal else NKT
            plan = []
            for kt in range(nkt):
                mask_i = None
                if causal and kt >= 48:
                    i = kt - 48 - 4 * qb
                    if 0 <= i < 4:
                        mask_i = i
                plan.append((k_src, v_src, kt, kt, mask_i))
            return plan

    for qb in range(4):
        qsl = slice(qb * 512, (qb + 1) * 512)
        for h in range(NH):
            rc, odd = divmod(h, 2)
            psrc = slice(64, 128) if odd else slice(0, 64)
            p.dma("dpool", qp, qp.ap_[0:64, h * 512:(h + 1) * 512],
                  qT, qT.ap_[psrc, rc * TCORE + qb * 512: rc * TCORE + (qb + 1) * 512])
            p.dma("dpool", qp, qp.ap_[64:65, h * 512:(h + 1) * 512],
                  brow, brow.ap_[h:h + 1, qsl])
        plan = plan_fn(qb)
        nkt = len(plan)
        for hg in range(4):
            for ki, (kfn, vfn, kkt, aug_row, mask_i) in enumerate(plan):
                kb = ktb[ki % NKB]
                vb = vtb[ki % NKB]
                kbv = ktbv[ki % NKB]
                vbv = vtbv[ki % NKB]
                kbuf, kap = kfn(kkt, hg)
                p.dma("dsp", kb, kb.ap_[0:64, :].rearrange("p (s j) -> p s j", s=4),
                      kbuf, kap.rearrange("(s dd) t -> dd s t", dd=64))
                p.dma("dpool", kb, kb.ap_[64:65, :],
                      augin, augin.ap()[aug_row:aug_row + 1, :])
                vbuf, vap = vfn(kkt, hg)
                p.dma("dsp", vb,
                      vb.ap_[:, :].rearrange("p (s j) -> p s j", s=4)[:, :, 0:64],
                      vbuf, vap.rearrange("t (s dd) -> t s dd", s=4))
                ptuse = [None] * 4
                for hi in range(4):
                    h = hg * 4 + hi
                    sps = s_ps[hi % 2]
                    kap2 = kb.ap_[0:65, hi * 128:(hi + 1) * 128]
                    qap = qp.ap_[0:65, h * 512:(h + 1) * 512]
                    p.op("pe", lambda e, kap2=kap2, qap=qap, sps=sps: e.matmul(
                            sps.ap(), kap2, qap, start=True, stop=True),
                         reads=[kb, qp], writes=[sps])
                    ptb = pt[hi]
                    p.op("act", lambda e, sps=sps, ptb=ptb: e.activation(
                            ptb.ap(), sps.ap(), AF.Exp, scale=0.125),
                         reads=[sps], writes=[ptb])
                    ptuse[hi] = ptb
                    if mask_i is not None:
                        p2b = pt2[hi]
                        p.op("pool", lambda e, i=mask_i, ptb=ptb, p2b=p2b:
                             e.affine_select(
                                p2b.ap(), ptb.ap(), pattern=[[1, 512]],
                                compare_op=ALU.is_ge, fill=0.0,
                                base=-128 * i, channel_multiplier=-1),
                             reads=[ptb], writes=[p2b])
                        ptuse[hi] = p2b
                for hi in range(4):
                    p.op("pe", lambda e, vbv=vbv, hi=hi, pu=ptuse[hi], av=av_ps[hi], ki=ki: e.matmul(
                            av.ap_[0:65, :], vbv[:, hi, :], pu.ap(),
                            start=(ki == 0), stop=(ki == nkt - 1)),
                         reads=[vb, ptuse[hi]], writes=[av_ps[hi]])
            for hi in range(4):
                h = hg * 4 + hi
                av = av_ps[hi]
                p.op("dve", lambda e, av=av: e.reciprocal(invz.ap(), av.ap_[64:65, :]),
                     reads=[av, invz], writes=[invz])
                p.op("dve", lambda e: e.tensor_copy(invzb.ap(), invz.ap()),
                     reads=[invz], writes=[invzb])
                p.op("pe", lambda e: e.matmul(rp_ps.ap_[0:64, :], ones64.ap(),
                                              invzb.ap(), start=True, stop=True),
                     reads=[ones64, invzb], writes=[rp_ps])
                rc, odd = divmod(h, 2)
                obase = 64 * odd
                p.op("act", lambda e, av=av, obase=obase, rc=rc: e.copy(
                        oT.ap_[obase:obase + 64, rc * 512:(rc + 1) * 512],
                        av.ap_[0:64, :]),
                     reads=[av], writes=[oT])
                p.op("dve", lambda e, obase=obase, rc=rc: e.tensor_mul(
                        oT.ap_[obase:obase + 64, rc * 512:(rc + 1) * 512],
                        oT.ap_[obase:obase + 64, rc * 512:(rc + 1) * 512],
                        rp_ps.ap_[0:64, :]),
                     reads=[rp_ps, oT], writes=[oT])
        wov = wo.ap().rearrange("p (dc n) -> p dc n", dc=8)
        for tc in range(4):
            t = qb * 4 + tc
            for nh in range(2):
                ns = slice(nh * 512, (nh + 1) * 512)
                for dc in range(8):
                    p.op("pe", lambda e, dc=dc, ns=ns, tc=tc: e.matmul(
                            pj_ps.ap_[:, ns], oT.ap_[:, dc * 512 + tc * 128:
                                                     dc * 512 + (tc + 1) * 128],
                            wov[:, dc, ns], start=(dc == 0), stop=(dc == 7)),
                         reads=[oT, wo], writes=[pj_ps])
            p.op("dve", lambda e: e.tensor_add(ores.ap(), pj_ps.ap(), bo.ap()),
                 reads=[pj_ps, bo], writes=[ores])
            xb = xt[t % 2]
            p.dma("dsp", xb, xb.ap(), *xsrc(t))
            mean2, rstd2 = emit_ln_stats(p, ores, ores.ap(), scr, st, 8)
            p.op("dve", lambda e: e.tensor_scalar(
                    scr.ap(), ores.ap(), st[:, 8:9], st[:, 11:12],
                    op0=ALU.subtract, op1=ALU.mult),
                 reads=[ores, st], writes=[scr])
            p.op("dve", lambda e: e.tensor_add(ores.ap(), xb.ap(), scr.ap()),
                 reads=[xb, scr], writes=[ores])
            dbuf, dap = xdst(t)
            p.dma("dsp", dbuf, dap, ores, ores.ap(),
                  unordered_dst=True, sem_buf=ores)
    _ctx.__exit__(None, None, None)
    if standalone:
        return p.finish()
    return None


def _attn_hostprep_kv(inputs, xflat, c, wq, wk, wv, bq, bk, bv):
    return {"xin": slice_tokens(xflat, c),
            "wqin": np.ascontiguousarray(wq.T.astype(np.float32)),
            "wkin": np.ascontiguousarray(wk.T.astype(np.float32)),
            "wvin": np.ascontiguousarray(wv.T.astype(np.float32)),
            "bqin": _rep(bq), "bkin": _rep(bk), "bvin": _rep(bv),
            "identin": np.eye(128, dtype=np.float32)}


def _gather_global(per_core, axis):
    out = {}
    for b in range(2):
        chunks = []
        for g in range(8):
            j = g if g < 4 else 7 - g
            arr = per_core[4 * b + j]
            sl = [slice(None)] * arr.ndim
            sl[axis] = slice(0, 1024) if g < 4 else slice(1024, 2048)
            chunks.append(arr[tuple(sl)])
        out[b] = np.concatenate(chunks, axis=axis)
    return out


def run_attn(inputs, xflat, which):
    if "attn_kv" not in _nc_cache:
        _nc_cache["attn_kv"] = build_attn_kv()
    if which == "l1":
        key = "attn_core_causal"
        if key not in _nc_cache:
            _nc_cache[key] = build_attn_core(True)
        wq, wk, wv, wo = (inputs["L1_attn_wq"], inputs["L1_attn_wk"],
                          inputs["L1_attn_wv"], inputs["L1_attn_wo"])
        zb = np.zeros(D, np.float32)
        bq = bk = bv = bo = zb
    else:
        key = "attn_core_full"
        if key not in _nc_cache:
            _nc_cache[key] = build_attn_core(False)
        wq, wk, wv, wo = (inputs["L2_attn_Wq"], inputs["L2_attn_Wk"],
                          inputs["L2_attn_Wv"], inputs["L2_attn_Wo"])
        bq, bk, bv, bo = (inputs["L2_attn_bq"], inputs["L2_attn_bk"],
                          inputs["L2_attn_bv"], inputs["L2_attn_bo"])
    in1 = [_attn_hostprep_kv(inputs, xflat, c, wq, wk, wv, bq, bk, bv)
           for c in range(NCORES)]
    r1 = run_bass_kernel_spmd(_nc_cache["attn_kv"], in1,
                              core_ids=list(range(NCORES)))
    kT_g = _gather_global([r1.results[c]["kTout"] for c in range(NCORES)], 1)
    v_g = _gather_global([r1.results[c]["vout"] for c in range(NCORES)], 0)

    def mk_augt(aug):
        return np.ascontiguousarray(
            np.tile(aug.reshape(64, 128), (1, 4)).astype(np.float32))
    kn_g = _gather_global([r1.results[c]["knout"] for c in range(NCORES)], 0)
    qTs = [r1.results[c]["qTout"] for c in range(NCORES)]
    qns = [r1.results[c]["qnout"] for c in range(NCORES)]
    ident = np.eye(128, dtype=np.float32)
    ones64 = np.ones((1, 64), np.float32)
    in2 = []
    if which == "l2":
        for c in range(NCORES):
            b = c // 4
            in2.append({"xin": slice_tokens(xflat, c),
                        "qTin": qTs[c],
                        "kTin": np.ascontiguousarray(kT_g[b]),
                        "vin": np.ascontiguousarray(v_g[b]),
                        "qnTin": np.ascontiguousarray(qns[c].T),
                        "knTin": np.ascontiguousarray(kn_g[b].T.astype(np.float32)),
                        "augin": np.ones((64, 512), np.float32),
                        "woin": np.ascontiguousarray(wo.T.astype(np.float32)),
                        "boin": _rep(bo)})
        res = run_bass_kernel_spmd(_nc_cache[key], in2,
                                   core_ids=list(range(NCORES)))
        return unslice_tokens([res.results[c]["xout"] for c in range(NCORES)])
    xg = _gather_global([slice_tokens(xflat, c) for c in range(NCORES)], 0)
    qT_g = _gather_global(qTs, 1)
    qn_g = _gather_global(qns, 0)
    outs_g = {0: np.empty((8192, D), np.float32), 1: np.empty((8192, D), np.float32)}
    for c in range(NCORES):
        b, j = divmod(c, 4)
        qsl = slice(2048 * j, 2048 * (j + 1))
        npad = 6144 - 2048 * j
        kT_sh = np.concatenate([np.zeros((D, npad), ml_dtypes_bf16()),
                                kT_g[b][:, :2048 * (j + 1)]], axis=1)
        v_sh = np.concatenate([np.zeros((npad, D), ml_dtypes_bf16()),
                               v_g[b][:2048 * (j + 1), :]], axis=0)
        aug_np = np.concatenate([np.full(npad, 30.0, np.float32),
                                 np.ones(8192 - npad, np.float32)])
        kn_sh = np.concatenate([np.zeros((16, npad), np.float32),
                                kn_g[b].T[:, :2048 * (j + 1)].astype(np.float32)], axis=1)
        aug = np.concatenate([np.full((1, npad), 30.0, np.float32),
                              np.ones((1, 8192 - npad), np.float32)], axis=1)
        in2.append({"xin": np.ascontiguousarray(xg[b][qsl]),
                    "qTin": np.ascontiguousarray(qT_g[b][:, qsl]),
                    "kTin": np.ascontiguousarray(kT_sh),
                    "vin": np.ascontiguousarray(v_sh),
                    "qnTin": np.ascontiguousarray(qn_g[b][qsl].T),
                    "knTin": kn_sh,
                    "augin": mk_augt(aug_np),
                    "woin": np.ascontiguousarray(wo.T.astype(np.float32)),
                    "boin": _rep(bo)})
    res = run_bass_kernel_spmd(_nc_cache[key], in2, core_ids=list(range(NCORES)))
    for c in range(NCORES):
        b, j = divmod(c, 4)
        outs_g[b][2048 * j:2048 * (j + 1)] = res.results[c]["xout"]
    flat = np.empty((16384, D), np.float32)
    for b in range(2):
        flat[8192 * b:8192 * (b + 1)] = outs_g[b]
    return flat


def ml_dtypes_bf16():
    import ml_dtypes
    return ml_dtypes.bfloat16


def run_l1attn(inputs, xflat):
    return run_attn(inputs, xflat, "l1")


def run_l2attn(inputs, xflat):
    return run_attn(inputs, xflat, "l2")





def build_l1ffn_d1(p=None, xsrc=None, pref="", yout=None, lgTout=None):
    standalone = p is None
    if standalone:
        p = P()
        xin = p.dram_in("xin", (TCORE, D), F32)
        xsrc = lambda t: (xin, xin[t * 128:(t + 1) * 128, :])
        yout = p.dram_out("yout", (TCORE, D), BF16)
        lgout = p.dram_out("lgout", (TCORE, 16), F32)
    _ctx = p.stage()
    _ctx.__enter__()
    rw1in = p.dram_in(pref + "rw1in", (D, 128), F32)
    rw2in = p.dram_in(pref + "rw2in", (128, 16), F32)
    identin = p.dram_in(pref + "identin", (128, 128), F32)

    ident32 = p.sb("ident32", (128, 128), F32)
    p.dma("dsp", ident32, ident32.ap(), identin, identin.ap())
    rw1 = p.sb("rw1", (128, 8 * 128), F32)
    p.dma("dsp", rw1, rw1.ap().rearrange("p (dc n) -> p dc n", dc=8),
          rw1in, rw1in.ap().rearrange("(dc p) n -> p dc n", p=128))
    rw2 = p.sb("rw2", (128, 16), F32)
    p.dma("dsp", rw2, rw2.ap(), rw2in, rw2in.ap())

    xt = [p.sb("xt0", (128, D), F32)] * 2
    scr = p.sb("scr", (128, D), F32)
    st = p.sb("st", (128, 16), F32)
    yf = p.sb("yf", (128, D), F32)
    yb = p.sb("yb", (128, D), BF16)
    yT32 = p.sb("yT32", (128, 128), F32)
    sr = p.sb("sr", (128, 128), F32)
    srT = p.sb("srT", (128, 128), F32)
    lg = p.sb("lg", (128, 16), F32)
    lgT = p.sb("lgT", (16, TCORE), F32) if lgTout is not None else None
    tr32_ps = p.psum("tr32_ps", (128, 128), F32)
    r_ps = p.psum("r_ps", (128, 128), F32)
    l_ps = p.psum("l_ps", (128, 16), F32)
    rw1v = rw1.ap().rearrange("p (dc n) -> p dc n", dc=8)

    for t in range(NTT):
        xb = xt[t % 2]
        p.dma("dsp", xb, xb.ap(), *xsrc(t))
        mean, rstd = emit_ln_stats(p, xb, xb.ap(), scr, st, 0)
        emit_ln_apply(p, yf, yf.ap(), xb, xb.ap(), st, mean, rstd)
        p.op("dve", lambda e: e.tensor_copy(yb.ap(), yf.ap()),
             reads=[yf], writes=[yb])
        p.dma("dsp", yout, yout[t * 128:(t + 1) * 128, :], yb, yb.ap(),
              unordered_dst=True, sem_buf=yb)
        for dc in range(8):
            sl = slice(dc * 128, (dc + 1) * 128)
            p.op("pe", lambda e, sl=sl: e.transpose(tr32_ps.ap(), yf[:, sl],
                                                    ident32.ap()),
                 reads=[yf, ident32], writes=[tr32_ps])
            p.op("dve", lambda e: e.tensor_copy(yT32.ap(), tr32_ps.ap()),
                 reads=[tr32_ps], writes=[yT32])
            p.op("pe", lambda e, dc=dc: e.matmul(
                    r_ps.ap(), yT32.ap(), rw1v[:, dc, :],
                    start=(dc == 0), stop=(dc == 7)),
                 reads=[yT32, rw1], writes=[r_ps])
        p.op("act", lambda e: e.activation(sr.ap(), r_ps.ap(), AF.Silu),
             reads=[r_ps], writes=[sr])
        p.op("pe", lambda e: e.transpose(tr32_ps.ap(), sr.ap(), ident32.ap()),
             reads=[sr, ident32], writes=[tr32_ps])
        p.op("dve", lambda e: e.tensor_copy(srT.ap(), tr32_ps.ap()),
             reads=[tr32_ps], writes=[srT])
        p.op("pe", lambda e: e.matmul(l_ps.ap(), srT.ap(), rw2.ap(),
                                      start=True, stop=True),
             reads=[srT, rw2], writes=[l_ps])
        p.op("dve", lambda e: e.tensor_copy(lg.ap(), l_ps.ap()),
             reads=[l_ps], writes=[lg])
        if standalone:
            p.dma("dsp", lgout, lgout[t * 128:(t + 1) * 128, :], lg, lg.ap(),
                  unordered_dst=True, sem_buf=lg)
        else:
            p.op("pe", lambda e: e.transpose(tr32_ps.ap_[0:16, :], lg.ap(),
                                             ident32.ap()),
                 reads=[lg, ident32], writes=[tr32_ps])
            p.op("dve", lambda e, t=t: e.tensor_copy(
                    lgT.ap_[:, t * 128:(t + 1) * 128], tr32_ps.ap_[0:16, :]),
                 reads=[tr32_ps], writes=[lgT])
    if not standalone:
        p.dma("dsp", lgTout, lgTout.ap(), lgT, lgT.ap())
    _ctx.__exit__(None, None, None)
    if standalone:
        return p.finish()
    return None


def build_l1ffn_d2(p=None, pref="", lg_fn=None, yin=None, woutout=None,
                   idxout=None):
    CAP = 256
    standalone = p is None
    if standalone:
        p = P()
        lgin = p.dram_in("lgin", (4, 8192), F32)
        lg_fn = lambda pp, b: pp.dma("dsp", b, b.ap(), lgin, lgin.ap())
        yin = p.dram_in("yin", (8192, D), BF16)
        woutout = p.dram_out("woutout", (4, CAP, D), F32)
        idxout = p.dram_out("idxout", (4, CAP), U32)
    _ctx = p.stage()
    _ctx.__enter__()
    w1in = p.dram_in(pref + "w1in", (4, D, 2048), F32)
    w2in = p.dram_in(pref + "w2in", (4, D, 2048), F32)
    w3in = p.dram_in(pref + "w3in", (4, 2048, D), F32)
    identin = p.dram_in(pref + "identin2", (128, 128), F32)

    ident = p.sb("ident", (128, 128), BF16)
    p.dma("dpool", ident, ident.ap(), identin, identin.ap())
    lgs = p.sb("lgs", (4, 8192), F32)
    lg_fn(p, lgs)
    topv = p.sb("topv", (4, CAP), F32)
    topi = p.sb("topi", (4, CAP), U32)
    m8 = p.sb("m8", (4, 8), F32)
    i8 = p.sb("i8", (4, 8), U32)
    rst = p.sb("rst", (4, 8), F32)
    ew = p.sb("ew", (4, CAP), F32)
    idxi = topi
    scrE = p.sb("scrE", (4, 8192), F32)

    for it in range(32):
        p.op("dve", lambda e: e.max(out=m8.ap(), in_=lgs.ap()),
             reads=[lgs], writes=[m8])
        p.op("dve", lambda e: e.max_index(out=i8.ap(), in_max=m8.ap(),
                                          in_values=lgs.ap()),
             reads=[lgs, m8], writes=[i8])
        p.op("dve", lambda e, it=it: e.tensor_copy(topv[:, it * 8:(it + 1) * 8],
                                                   m8.ap()),
             reads=[m8, topv], writes=[topv])
        p.op("dve", lambda e, it=it: e.tensor_copy(topi[:, it * 8:(it + 1) * 8],
                                                   i8.ap()),
             reads=[i8, topi], writes=[topi])
        p.op("dve", lambda e: e.match_replace(out=lgs.ap(), in_to_replace=m8.ap(),
                                              in_values=lgs.ap(), imm_value=-1e30),
             reads=[m8, lgs], writes=[lgs])
    p.op("dve", lambda e: e.tensor_scalar_mul(rst[:, 0:1], topv[:, 0:1], -1.0),
         reads=[topv, rst], writes=[rst])
    p.op("act", lambda e: e.activation(scrE.ap(), lgs.ap(), AF.Exp,
                                       bias=rst[:, 0:1], accum_out=rst[:, 1:2]),
         reads=[lgs, rst], writes=[scrE, rst])
    p.op("act", lambda e: e.activation(ew.ap(), topv.ap(), AF.Exp,
                                       bias=rst[:, 0:1], accum_out=rst[:, 2:3]),
         reads=[topv, rst], writes=[ew, rst])
    p.op("dve", lambda e: e.tensor_add(rst[:, 3:4], rst[:, 1:2], rst[:, 2:3]),
         reads=[rst], writes=[rst])
    p.op("dve", lambda e: e.scalar_tensor_tensor(rst[:, 4:5], rst[:, 3:4], 1e-9,
                                                 rst[:, 2:3], op0=ALU.mult,
                                                 op1=ALU.add),
         reads=[rst], writes=[rst])
    p.op("dve", lambda e: e.reciprocal(rst[:, 4:5], rst[:, 4:5]),
         reads=[rst], writes=[rst])
    p.op("dve", lambda e: e.tensor_scalar_mul(ew.ap(), ew.ap(), rst[:, 4:5]),
         reads=[rst, ew], writes=[ew])
    p.dma("dsp", idxout, idxout.ap(), idxi, idxi.ap())
    ewd = p.dram_tmp("ewd", (4, CAP), F32)
    p.dma("dsp", ewd, ewd.ap(), ew, ew.ap())

    ein = p.sb("ein", (128, 2 * D), BF16)
    einT = p.sb("einT", (128, 8 * 256), BF16)
    gT = p.sb("gT", (128, 8 * 256), BF16)
    w1b = p.sb("w1b", (128, 64 * 128), BF16)
    w2b = p.sb("w2b", (128, 64 * 128), BF16)
    w3b = p.sb("w3b", (128, 8 * D), BF16)
    wout = p.sb("wout", (128, 2 * D), F32)
    wcol = p.sb("wcol", (128, 2), F32)
    idx128 = p.sb("idx128", (128, 2), U32)
    tr_ps = p.psum("tr_ps", (128, 1024), BF16)
    h_ps = p.psum("h_ps", (128, 256), F32)
    h2_ps = p.psum("h2_ps", (128, 256), F32)
    o_ps = [p.psum(f"o_ps{i}", (128, D), F32) for i in range(2)]

    einv = ein.ap().rearrange("p (rc d) -> p rc d", rc=2)
    einTv = einT.ap().rearrange("p (dc r) -> p dc r", dc=8)
    gTv = gT.ap().rearrange("p (fc r) -> p fc r", fc=8)
    w1v = w1b.ap().rearrange("p (dc fc j) -> p dc fc j", dc=8, fc=8)
    w2v = w2b.ap().rearrange("p (dc fc j) -> p dc fc j", dc=8, fc=8)
    w3v = w3b.ap().rearrange("p (fc d) -> p fc d", fc=8)

    for pr in range(4):
        p.dma("dsp", idx128, idx128.ap(),
              idxout, idxout.ap()[pr].rearrange("(c p) -> p c", p=128),
              allow_slow_non_contiguous=True)
        p.dma("dsp", wcol, wcol.ap(),
              ewd, ewd.ap()[pr].rearrange("(c p) -> p c", p=128),
              allow_slow_non_contiguous=True)
        for rc in range(2):
            p.op("pool", lambda e, rc=rc: e.indirect_dma_start(
                    einv[:, rc, :], None, yin.ap(),
                    bass.IndirectOffsetOnAxis(idx128[:, rc:rc + 1], 0),
                    bounds_check=8191, oob_is_err=False),
                 reads=[yin, idx128], writes=[ein], dma16=True)
        for rc in range(2):
            for dc in range(8):
                p.op("pe", lambda e, rc=rc, dc=dc: e.transpose(
                        tr_ps.ap_[:, dc * 128:(dc + 1) * 128],
                        einv[:, rc, dc * 128:(dc + 1) * 128], ident.ap()),
                     reads=[ein, ident], writes=[tr_ps])
            p.op("act", lambda e, rc=rc: e.copy(
                    einTv[:, :, rc * 128:(rc + 1) * 128],
                    tr_ps.ap().rearrange("p (dc j) -> p dc j", dc=8)),
                 reads=[tr_ps], writes=[einT])
        for half in range(2):
            fsl = slice(half * 1024, (half + 1) * 1024)
            p.dma("dpool", w1b, w1v,
                  w1in, w1in[pr, :, fsl].rearrange("(dc p) (fc j) -> p dc fc j",
                                                   p=128, j=128))
            p.dma("dpool", w2b, w2v,
                  w2in, w2in[pr, :, fsl].rearrange("(dc p) (fc j) -> p dc fc j",
                                                   p=128, j=128))
            p.dma("dpool", w3b, w3v,
                  w3in, w3in[pr, fsl, :].rearrange("(fc p) d -> p fc d", p=128))
            for fc in range(8):
                for dc in range(8):
                    p.op("pe", lambda e, fc=fc, dc=dc: e.matmul(
                            h_ps.ap(), w1v[:, dc, fc, :], einTv[:, dc, :],
                            start=(dc == 0), stop=(dc == 7)),
                         reads=[w1b, einT], writes=[h_ps])
                p.op("act", lambda e, fc=fc: e.activation(
                        gTv[:, fc, :], h_ps.ap(), AF.Silu),
                     reads=[h_ps], writes=[gT])
                for dc in range(8):
                    p.op("pe", lambda e, fc=fc, dc=dc: e.matmul(
                            h2_ps.ap(), w2v[:, dc, fc, :], einTv[:, dc, :],
                            start=(dc == 0), stop=(dc == 7)),
                         reads=[w2b, einT], writes=[h2_ps])
                p.op("dve", lambda e, fc=fc: e.tensor_mul(
                        gTv[:, fc, :], gTv[:, fc, :], h2_ps.ap()),
                     reads=[h2_ps, gT], writes=[gT])
            for rc in range(2):
                for nh in range(2):
                    ns = slice(nh * 512, (nh + 1) * 512)
                    for fc in range(8):
                        p.op("pe", lambda e, rc=rc, nh=nh, ns=ns, fc=fc, half=half: e.matmul(
                                o_ps[rc][:, ns], gTv[:, fc, rc * 128:(rc + 1) * 128],
                                w3v[:, fc, ns],
                                start=(half == 0 and fc == 0),
                                stop=(half == 1 and fc == 7)),
                             reads=[gT, w3b], writes=[o_ps[rc]])
        for rc in range(2):
            p.op("dve", lambda e, rc=rc: e.tensor_scalar_mul(
                    wout.ap()[:, rc * D:(rc + 1) * D], o_ps[rc].ap(),
                    wcol[:, rc:rc + 1]),
                 reads=[o_ps[rc], wcol], writes=[wout])
        p.dma("dsp", woutout, woutout.ap()[pr].rearrange("(rc p) d -> p rc d", p=128),
              wout, wout.ap().rearrange("p (rc d) -> p rc d", rc=2),
              unordered_dst=True, sem_buf=wout)
    _ctx.__exit__(None, None, None)
    if standalone:
        return p.finish()
    return None


def build_l1ffn_d3(p=None, xsrc=None, xdst=None, pref="", winsin=None,
                   idxsin=None):
    CAP = 256
    standalone = p is None
    if standalone:
        p = P()
        xin = p.dram_in("xin", (TCORE, D), F32)
        xsrc = lambda t: (xin, xin[t * 128:(t + 1) * 128, :])
        xout_t = p.dram_out("xout", (TCORE, D), F32)
        xdst = lambda t: (xout_t, xout_t[t * 128:(t + 1) * 128, :])
        winsin = p.dram_in("winsin", (16, CAP, D), F32)
        idxsin = p.dram_in("idxsin", (16, CAP), U32)
    _ctx = p.stage()
    _ctx.__enter__()
    basesin = p.dram_in(pref + "basesin", (16, 2), U32)
    accs = [p.dram_tmp(pref + "accA", (1024, D), F32),
            p.dram_tmp(pref + "accB", (1024, D), F32)]

    idxs = p.sb("idxs", (16, CAP), U32)
    p.dma("dsp", idxs, idxs.ap(), idxsin, idxsin.ap())
    bases = p.sb("bases", (16, 2), U32)
    p.dma("dsp", bases, bases.ap(), basesin, basesin.ap())
    sh = p.sb("sh", (16, CAP), U32)
    shd = p.dram_tmp("shd", (16, CAP), U32)
    idx128 = p.sb("idx128", (128, 2), U32)
    wsb = p.sb("wsb", (128, D), F32)
    zt = p.sb("zt", (128, D), F32)

    bndreg = p.eng["pool"].alloc_register("bnd")
    p.eng["pool"].reg_mov(bndreg, 1023)
    p.op("dve", lambda e: e.memset(zt.ap(), 0.0), reads=[], writes=[zt])
    zev = None
    for t in range(NTT):
        acc = accs[t // 8]
        zev = p.dma("dsp", acc, acc[(t % 8) * 128:(t % 8 + 1) * 128, :],
                    zt, zt.ap(), unordered_dst=True, sem_buf=zt)
    for acc in accs:
        acc.w_ev = zev
    for rng in range(2):
        p.op("dve", lambda e, rng=rng: e.tensor_tensor(
                sh.ap(), idxs.ap(),
                bases[:, rng:rng + 1].to_broadcast((16, CAP)),
                op=ALU.subtract),
             reads=[idxs, bases], writes=[sh])
        p.dma("dsp", shd, shd.ap(), sh, sh.ap())
        for pair in range(16):
            for rc in range(2):
                p.dma("dsp", idx128, idx128.ap()[:, rc:rc + 1],
                      shd, shd.ap()[pair, rc * 128:(rc + 1) * 128]
                      .rearrange("(p o) -> p o", o=1))
                p.dma("dsp", wsb, wsb.ap(),
                      winsin, winsin[pair, rc * 128:(rc + 1) * 128, :])
                p.op("pool", lambda e, rng=rng: e.indirect_dma_start(
                        accs[rng].ap(),
                        bass.IndirectOffsetOnAxis(idx128[:, rc:rc + 1], 0),
                        wsb.ap(), None,
                        bounds_check=bndreg, oob_is_err=False,
                        compute_op=ALU.add),
                     reads=[wsb, idx128, accs[rng]], writes=[accs[rng]], dma16=True)
    xt = [p.sb("xt0", (128, D), F32)] * 2
    at = [p.sb("at0", (128, D), F32)] * 2
    scr = p.sb("scr", (128, D), F32)
    st = p.sb("st", (128, 16), F32)
    ob = p.sb("ob", (128, D), F32)
    for t in range(NTT):
        xb, ab = xt[t % 2], at[t % 2]
        p.dma("dsp", xb, xb.ap(), *xsrc(t))
        p.dma("dsp", ab, ab.ap(), accs[t // 8],
              accs[t // 8][(t % 8) * 128:(t % 8 + 1) * 128, :])
        mean2, rstd2 = emit_ln_stats(p, ab, ab.ap(), scr, st, 8)
        p.op("dve", lambda e: e.tensor_scalar(
                scr.ap(), ab.ap(), st[:, 8:9], st[:, 11:12],
                op0=ALU.subtract, op1=ALU.mult),
             reads=[ab, st], writes=[scr])
        p.op("dve", lambda e: e.tensor_add(ob.ap(), xb.ap(), scr.ap()),
             reads=[xb, scr], writes=[ob])
        dbuf, dap = xdst(t)
        p.dma("dsp", dbuf, dap, ob, ob.ap(), unordered_dst=True, sem_buf=ob)
    _ctx.__exit__(None, None, None)
    if standalone:
        return p.finish()
    return None


def build_l1ffn_f(p=None, xsrc=None, xdst=None, pref=""):
    CAP = 256
    y_c = p.dram_tmp(pref + "y_c", (TCORE, D), BF16)
    y_ag = p.dram_tmp(pref + "y_ag", (4 * TCORE, D), BF16)
    lgT_c = p.dram_tmp(pref + "lgT_c", (16, TCORE), F32)
    lgT_ag = p.dram_tmp(pref + "lgT_ag", (4 * 16, TCORE), F32)
    wout_c = p.dram_tmp(pref + "wout_c", (4, CAP, D), F32)
    idx_c = p.dram_tmp(pref + "idx_c", (4, CAP), U32)
    wout_ag = p.dram_tmp(pref + "wout_ag", (16 * CAP, D), F32)
    idx_ag = p.dram_tmp(pref + "idx_ag", (16, CAP), U32)
    groups = [[0, 1, 2, 3], [4, 5, 6, 7]]

    build_l1ffn_d1(p=p, xsrc=xsrc, pref=pref, yout=y_c, lgTout=lgT_c)
    p.cc("AllGather", ALU.bypass, groups, y_c, y_c.ap(), y_ag, y_ag.ap())
    p.cc("AllGather", ALU.bypass, groups, lgT_c, lgT_c.ap(), lgT_ag, lgT_ag.ap())

    ebin = p.dram_in(pref + "ebase", (1, 1), I32)
    ebs = p.sb("ebs", (1, 1), I32)
    p.dma("dsp", ebs, ebs.ap(), ebin, ebin.ap())
    ereg = p.sval("sp", ebs, ebs[0:1, 0:1])

    def lg_fn(pp, lgs_buf):
        v = lgT_ag.ap().rearrange("(r e) t -> e r t", r=4)
        pp.dma("dsp", lgs_buf,
               lgs_buf.ap().rearrange("e (r t) -> e r t", r=4),
               lgT_ag, v[bass.ts(ereg, 4), :, :])

    build_l1ffn_d2(p=p, pref=pref, lg_fn=lg_fn, yin=y_ag, woutout=wout_c,
                   idxout=idx_c)
    p.cc("AllGather", ALU.bypass, groups, wout_c,
         wout_c.ap().rearrange("q c d -> (q c) d"), wout_ag, wout_ag.ap())
    p.cc("AllGather", ALU.bypass, groups, idx_c, idx_c.ap(), idx_ag, idx_ag.ap())
    wins_view = Buf(wout_ag.name,
                    wout_ag.ap().rearrange("(q c) d -> q c d", c=CAP))
    wins_view.w_ev = wout_ag.w_ev
    build_l1ffn_d3(p=p, xsrc=xsrc, xdst=xdst, pref=pref,
                   winsin=wins_view, idxsin=idx_ag)
    return None


def prep_l1ffn_f(inputs, xflat, c, pref=""):
    b, j = divmod(c, 4)
    es = slice(4 * j, 4 * j + 4)
    rbase = j * TCORE
    bases = np.broadcast_to(
        np.array([rbase, rbase + 1024], np.uint32), (16, 2)).copy()
    d = {"rw1in": inputs["L1_ffn_router_w1"].astype(np.float32),
         "rw2in": inputs["L1_ffn_router_w2"].astype(np.float32),
         "identin": np.eye(128, dtype=np.float32),
         "identin2": np.eye(128, dtype=np.float32),
         "w1in": inputs["L1_ffn_w1"][es].astype(np.float32),
         "w2in": inputs["L1_ffn_w2"][es].astype(np.float32),
         "w3in": inputs["L1_ffn_w3"][es].astype(np.float32),
         "ebase": np.array([[j]], np.int32),
         "basesin": bases}
    return {pref + k: v for k, v in d.items()}


def run_l1ffn(inputs, xflat):
    for k, b in (("l1ffn_d1", build_l1ffn_d1), ("l1ffn_d2", build_l1ffn_d2),
                 ("l1ffn_d3", build_l1ffn_d3)):
        if k not in _nc_cache:
            _nc_cache[k] = b()
    ident = np.eye(128, dtype=np.float32)
    in1 = [{"xin": slice_tokens(xflat, c),
            "rw1in": inputs["L1_ffn_router_w1"].astype(np.float32),
            "rw2in": inputs["L1_ffn_router_w2"].astype(np.float32),
            "identin": ident} for c in range(NCORES)]
    r1 = run_bass_kernel_spmd(_nc_cache["l1ffn_d1"], in1,
                              core_ids=list(range(NCORES)))
    y_g = _gather_global([r1.results[c]["yout"] for c in range(NCORES)], 0)
    lg_g = _gather_global([r1.results[c]["lgout"] for c in range(NCORES)], 0)
    in2 = []
    for c in range(NCORES):
        b, j = divmod(c, 4)
        es = slice(4 * j, 4 * j + 4)
        in2.append({"lgin": np.ascontiguousarray(lg_g[b].T[es].astype(np.float32)),
                    "yin": y_g[b],
                    "w1in": inputs["L1_ffn_w1"][es].astype(np.float32),
                    "w2in": inputs["L1_ffn_w2"][es].astype(np.float32),
                    "w3in": inputs["L1_ffn_w3"][es].astype(np.float32),
                    "identin": ident})
    r2 = run_bass_kernel_spmd(_nc_cache["l1ffn_d2"], in2,
                              core_ids=list(range(NCORES)))
    in3 = []
    for c in range(NCORES):
        b, j = divmod(c, 4)
        wins = np.concatenate([r2.results[4 * b + jj]["woutout"]
                               for jj in range(4)], axis=0)
        idxs = np.concatenate([r2.results[4 * b + jj]["idxout"]
                               for jj in range(4)], axis=0)
        (r1_, n1), (r2_, n2) = core_row_slices(c)
        bases = np.broadcast_to(
            np.array([r1_ - 8192 * b, r2_ - 8192 * b], np.uint32), (16, 2)).copy()
        in3.append({"xin": slice_tokens(xflat, c),
                    "winsin": wins, "idxsin": idxs, "basesin": bases})
    r3 = run_bass_kernel_spmd(_nc_cache["l1ffn_d3"], in3,
                              core_ids=list(range(NCORES)))
    return unslice_tokens([r3.results[c]["xout"] for c in range(NCORES)])





def build_l3attn_g1(p=None, xsrc=None, pref="", nxout=None, plout=None,
                    lamout=None):
    standalone = p is None
    if standalone:
        p = P()
        xin = p.dram_in("xin", (TCORE, D), F32)
        xsrc = lambda t: (xin, xin[t * 128:(t + 1) * 128, :])
        nxout = p.dram_out("nxout", (128, 8 * TCORE), F32)
        plout = p.dram_out("plout", (128, 8 * TCORE), F32)
        lamout = p.dram_out("lamout", (128, 8 * 2), F32)
    _ctx = p.stage()
    _ctx.__enter__()
    igwin = p.dram_in(pref + "igwin", (8, 128, 128), F32)
    igbin = p.dram_in(pref + "igbin", (128, 8), F32)
    agwin = p.dram_in(pref + "agwin", (8, 128, 128), F32)
    agbin = p.dram_in(pref + "agbin", (128, 8), F32)
    apin = p.dram_in(pref + "apin", (128, 8), F32)
    identin = p.dram_in(pref + "identin", (128, 128), F32)

    ident = p.sb("ident", (128, 128), BF16)
    p.dma("dpool", ident, ident.ap(), identin, identin.ap())
    igw = p.sb("igw", (128, 8 * 128), BF16)
    p.dma("dpool", igw, igw.ap().rearrange("p (h n) -> p h n", h=8),
          igwin, igwin.ap().rearrange("h p n -> p h n"))
    agw = p.sb("agw", (128, 8 * 128), BF16)
    p.dma("dpool", agw, agw.ap().rearrange("p (h n) -> p h n", h=8),
          agwin, agwin.ap().rearrange("h p n -> p h n"))
    igb = p.sb("igb", (128, 8), F32)
    p.dma("dsp", igb, igb.ap(), igbin, igbin.ap())
    agb = p.sb("agb", (128, 8), F32)
    p.dma("dsp", agb, agb.ap(), agbin, agbin.ap())
    ap_ = p.sb("ap_", (128, 8), F32)
    p.dma("dsp", ap_, ap_.ap(), apin, apin.ap())

    sp = p.sb("sp", (128, 8), F32)
    spt = p.sb("spt", (128, 8), F32)
    p.op("act", lambda e: e.activation(spt.ap(), ap_.ap(), AF.Abs),
         reads=[ap_], writes=[spt])
    p.op("act", lambda e: e.activation(spt.ap(), spt.ap(), AF.Exp, scale=-1.0),
         reads=[spt], writes=[spt])
    p.op("act", lambda e: e.activation(spt.ap(), spt.ap(), AF.Ln, bias=1.0),
         reads=[spt], writes=[spt])
    p.op("dve", lambda e: e.tensor_scalar_max(sp.ap(), ap_.ap(), 0.0),
         reads=[ap_], writes=[sp])
    p.op("dve", lambda e: e.tensor_add(sp.ap(), sp.ap(), spt.ap()),
         reads=[spt, sp], writes=[sp])
    p.op("dve", lambda e: e.tensor_scalar_mul(sp.ap(), sp.ap(), -8.0),
         reads=[sp], writes=[sp])

    xt = [p.sb("xt0", (128, D), F32)] * 2
    scr = p.sb("scr", (128, D), F32)
    st = p.sb("st", (128, 16), F32)
    yf = p.sb("yf", (128, D), F32)
    yb = p.sb("yb", (128, D), BF16)
    yT = p.sb("yT", (128, 8 * 512), BF16)
    ga = p.sb("ga", (128, 512), F32)
    gx = p.sb("gx", (128, 512), F32)
    la = p.sb("la", (128, 512), F32)
    mlt = p.sb("mlt", (128, 512), F32)
    nx = p.sb("nx", (128, 512), F32)
    dtl = p.sb("dtl", (128, 512), F32)
    pl = p.sb("pl", (128, 512), F32)
    zz = p.sb("zz", (128, 512), F32)
    carA = p.sb("carA", (128, 8), F32)
    carP = p.sb("carP", (128, 8), F32)
    tr_ps = p.psum("tr_ps", (128, 1024), BF16)
    g_ps = p.psum("g_ps", (128, 512), F32)
    g2_ps = p.psum("g2_ps", (128, 512), F32)

    p.op("dve", lambda e: e.memset(zz.ap(), 0.0), reads=[], writes=[zz])
    yTv = yT.ap().rearrange("p (dc t) -> p dc t", dc=8)
    nxo = nxout.ap().rearrange("p (dc t) -> p dc t", dc=8)
    plo = plout.ap().rearrange("p (dc t) -> p dc t", dc=8)
    igwv = igw.ap().rearrange("p (h n) -> p h n", h=8)
    agwv = agw.ap().rearrange("p (h n) -> p h n", h=8)

    for blk in range(4):
        chunk_start = (blk % 2 == 0)
        for tc in range(4):
            t = blk * 4 + tc
            xb = xt[t % 2]
            p.dma("dsp", xb, xb.ap(), *xsrc(t))
            mean, rstd = emit_ln_stats(p, xb, xb.ap(), scr, st, 0)
            emit_ln_apply(p, yf, yf.ap(), xb, xb.ap(), st, mean, rstd)
            p.op("dve", lambda e: e.tensor_copy(yb.ap(), yf.ap()),
                 reads=[yf], writes=[yb])
            for dc in range(8):
                sl = slice(dc * 128, (dc + 1) * 128)
                p.op("pe", lambda e, sl=sl: e.transpose(tr_ps[:, sl], yb[:, sl],
                                                        ident.ap()),
                     reads=[yb, ident], writes=[tr_ps])
            p.op("act", lambda e, tc=tc: e.copy(
                    yTv[:, :, tc * 128:(tc + 1) * 128],
                    tr_ps.ap().rearrange("p (dc j) -> p dc j", dc=8)),
                 reads=[tr_ps], writes=[yT])
        for dc in range(8):
            p.op("pe", lambda e, dc=dc: e.matmul(
                    g_ps.ap(), igwv[:, dc, :], yTv[:, dc, :],
                    start=True, stop=True),
                 reads=[igw, yT], writes=[g_ps])
            p.op("act", lambda e, dc=dc: e.activation(
                    gx.ap(), g_ps.ap(), AF.Sigmoid, bias=igb[:, dc:dc + 1]),
                 reads=[g_ps, igb], writes=[gx])
            p.op("pe", lambda e, dc=dc: e.matmul(
                    g2_ps.ap(), agwv[:, dc, :], yTv[:, dc, :],
                    start=True, stop=True),
                 reads=[agw, yT], writes=[g2_ps])
            p.op("act", lambda e, dc=dc: e.activation(
                    ga.ap(), g2_ps.ap(), AF.Sigmoid, bias=agb[:, dc:dc + 1]),
                 reads=[g2_ps, agb], writes=[ga])
            p.op("dve", lambda e, dc=dc: e.tensor_scalar_mul(
                    la.ap(), ga.ap(), sp[:, dc:dc + 1]),
                 reads=[ga, sp], writes=[la])
            p.op("act", lambda e: e.activation(mlt.ap(), la.ap(), AF.Exp,
                                               scale=2.0),
                 reads=[la], writes=[mlt])
            p.op("act", lambda e: e.activation(la.ap(), la.ap(), AF.Exp),
                 reads=[la], writes=[la])
            p.op("dve", lambda e: e.tensor_scalar(mlt.ap(), mlt.ap(), -1.0, 1.0,
                                                  op0=ALU.mult, op1=ALU.add),
                 reads=[mlt], writes=[mlt])
            p.op("dve", lambda e: e.tensor_scalar_max(mlt.ap(), mlt.ap(), 1e-7),
                 reads=[mlt], writes=[mlt])
            p.op("act", lambda e: e.activation(mlt.ap(), mlt.ap(), AF.Sqrt),
                 reads=[mlt], writes=[mlt])
            p.op("dve", lambda e, dc=dc: e.tensor_tensor(
                    nx.ap(), yTv[:, dc, :], gx.ap(), op=ALU.mult),
                 reads=[yT, gx], writes=[nx])
            p.op("dve", lambda e: e.tensor_mul(nx.ap(), nx.ap(), mlt.ap()),
                 reads=[mlt, nx], writes=[nx])
            p.dma("dsp", nxout, nxo[:, dc, blk * 512:(blk + 1) * 512],
                  nx, nx.ap(), unordered_dst=True, sem_buf=nx)
            p.op("dve", lambda e: e.tensor_copy(dtl[:, 1:512], la[:, 0:511]),
                 reads=[la], writes=[dtl])
            if chunk_start:
                p.op("dve", lambda e: e.memset(dtl[:, 0:1], 1.0),
                     reads=[], writes=[dtl])
            else:
                p.op("dve", lambda e, dc=dc: e.tensor_copy(dtl[:, 0:1],
                                                           carA[:, dc:dc + 1]),
                     reads=[carA], writes=[dtl])
            if chunk_start:
                p.op("dve", lambda e: e.tensor_tensor_scan(
                        pl.ap(), dtl.ap(), zz.ap(), 1.0,
                        op0=ALU.mult, op1=ALU.add),
                     reads=[dtl, zz], writes=[pl])
            else:
                p.op("dve", lambda e, dc=dc: e.tensor_tensor_scan(
                        pl.ap(), dtl.ap(), zz.ap(), carP[:, dc:dc + 1],
                        op0=ALU.mult, op1=ALU.add),
                     reads=[dtl, zz, carP], writes=[pl])
            p.dma("dsp", plout, plo[:, dc, blk * 512:(blk + 1) * 512],
                  pl, pl.ap(), unordered_dst=True, sem_buf=pl)
            p.op("dve", lambda e, dc=dc: e.tensor_tensor(
                    carP[:, dc:dc + 1], pl[:, 511:512], la[:, 511:512],
                    op=ALU.mult),
                 reads=[pl, la, carP], writes=[carP])
            p.op("dve", lambda e, dc=dc: e.tensor_copy(carA[:, dc:dc + 1],
                                                       la[:, 511:512]),
                 reads=[la, carA], writes=[carA])
        if blk % 2 == 1:
            ch = blk // 2
            lamv = lamout.ap().rearrange("p (dc c) -> p dc c", dc=8)
            p.dma("dsp", lamout, lamv[:, :, ch:ch + 1],
                  carP, carP.ap().rearrange("p (dc o) -> p dc o", o=1),
                  unordered_dst=True, sem_buf=carP,
                  allow_slow_non_contiguous=True)
    _ctx.__exit__(None, None, None)
    if standalone:
        return p.finish()
    return None


def build_l3attn_g2(p=None, pref="", nxin=None, plin=None, lam_fn=None,
                    a1out=None, c2out=None, totout=None):
    standalone = p is None
    if standalone:
        p = P()
        nxin = p.dram_in("nxin", (128, 8 * TCORE), F32)
        plin = p.dram_in("plin", (128, 8 * TCORE), F32)
        lamin = p.dram_in("lamin", (128, 8 * 8), F32)
        lam_fn = lambda pp, lam_buf: pp.dma("dsp", lam_buf, lam_buf.ap(),
                                            lamin, lamin.ap())
        a1out = p.dram_out("a1out", (128, 8 * TCORE), F32)
        c2out = p.dram_out("c2out", (128, 8 * TCORE), F32)
        totout = p.dram_out("totout", (128, 8 * 2), F32)
    _ctx = p.stage()
    _ctx.__enter__()
    mskin = p.dram_in(pref + "mskin", (128, 2 * 8), F32)

    lam = p.sb("lam", (128, 64), F32)
    lam_fn(p, lam)
    msk = p.sb("msk", (128, 16), F32)
    p.dma("dsp", msk, msk.ap(), mskin, mskin.ap())
    pin = p.sb("pin", (128, 8 * 2), F32)
    invp = p.sb("invp", (128, 8 * 2), F32)
    tht = p.sb("tht", (128, 8 * 2), F32)
    tmp8 = p.sb("tmp8", (128, 8), F32)
    tot = p.sb("tot", (128, 8 * 2), F32)
    lamv = lam.ap().rearrange("p (dc g) -> p dc g", dc=8)
    mskv = msk.ap().rearrange("p (lc g) -> p lc g", lc=2)
    pinv = pin.ap().rearrange("p (dc lc) -> p dc lc", dc=8)
    invpv = invp.ap().rearrange("p (dc lc) -> p dc lc", dc=8)
    thtv = tht.ap().rearrange("p (dc lc) -> p dc lc", dc=8)
    totv = tot.ap().rearrange("p (dc lc) -> p dc lc", dc=8)

    for lc in range(2):
        p.op("dve", lambda e, lc=lc: e.memset(pinv[:, :, lc], 1.0),
             reads=[], writes=[pin])
        for g in range(8):
            p.op("dve", lambda e, lc=lc, g=g: e.scalar_tensor_tensor(
                    tmp8.ap(), lamv[:, :, g], msk[:, lc * 8 + g:lc * 8 + g + 1],
                    tmp8.ap(), op0=ALU.mult, op1=ALU.bypass),
                 reads=[lam, msk], writes=[tmp8])
            p.op("dve", lambda e: e.tensor_scalar_add(tmp8.ap(), tmp8.ap(), 1.0),
                 reads=[tmp8], writes=[tmp8])
            p.op("dve", lambda e, lc=lc, g=g: e.tensor_scalar(
                    tmp8.ap(), tmp8.ap(), msk[:, lc * 8 + g:lc * 8 + g + 1], None,
                    op0=ALU.subtract),
                 reads=[tmp8, msk], writes=[tmp8])
            p.op("dve", lambda e, lc=lc: e.tensor_mul(pinv[:, :, lc],
                                                      pinv[:, :, lc], tmp8.ap()),
                 reads=[tmp8, pin], writes=[pin])
    p.op("dve", lambda e: e.tensor_scalar_max(invp.ap(), pin.ap(), 1e-30),
         reads=[pin], writes=[invp])
    p.op("dve", lambda e: e.reciprocal(invp.ap(), invp.ap()),
         reads=[invp], writes=[invp])
    p.op("dve", lambda e: e.tensor_scalar_mul(tht.ap(), invp.ap(), 1e-7),
         reads=[invp], writes=[tht])

    nxv = nxin.ap().rearrange("p (dc t) -> p dc t", dc=8)
    plv = plin.ap().rearrange("p (dc t) -> p dc t", dc=8)
    a1o = a1out.ap().rearrange("p (dc t) -> p dc t", dc=8)
    c2o = c2out.ap().rearrange("p (dc t) -> p dc t", dc=8)
    nxb = p.sb("nxb", (128, 512), F32)
    plb = p.sb("plb", (128, 512), F32)
    mb = p.sb("mb", (128, 512), F32)
    q1 = p.sb("q1", (128, 512), F32)
    a1 = p.sb("a1", (128, 512), F32)
    c2 = p.sb("c2", (128, 512), F32)
    zz = p.sb("zz", (128, 512), F32)
    carA1 = p.sb("carA1", (128, 8), F32)
    carC2 = p.sb("carC2", (128, 8), F32)
    p.op("dve", lambda e: e.memset(zz.ap(), 0.0), reads=[], writes=[zz])

    for blk in range(4):
        lc = blk // 2
        chunk_start = (blk % 2 == 0)
        for dc in range(8):
            p.dma("dsp", nxb, nxb.ap(), nxin, nxv[:, dc, blk * 512:(blk + 1) * 512])
            p.dma("dsp", plb, plb.ap(), plin, plv[:, dc, blk * 512:(blk + 1) * 512])
            col = dc * 2 + lc
            p.op("dve", lambda e, col=col: e.tensor_single_scalar(
                    mb.ap(), plb.ap(), tht[:, col:col + 1], op=ALU.is_ge),
                 reads=[plb, tht], writes=[mb])
            p.op("dve", lambda e: e.tensor_scalar_max(q1.ap(), plb.ap(), 1e-30),
                 reads=[plb], writes=[q1])
            p.op("dve", lambda e: e.reciprocal(q1.ap(), q1.ap()),
                 reads=[q1], writes=[q1])
            p.op("dve", lambda e: e.tensor_mul(q1.ap(), q1.ap(), nxb.ap()),
                 reads=[nxb, q1], writes=[q1])
            p.op("dve", lambda e: e.tensor_mul(q1.ap(), q1.ap(), mb.ap()),
                 reads=[mb, q1], writes=[q1])
            if chunk_start:
                p.op("dve", lambda e: e.tensor_tensor_scan(
                        a1.ap(), q1.ap(), zz.ap(), 0.0, op0=ALU.add, op1=ALU.add),
                     reads=[q1, zz], writes=[a1])
            else:
                p.op("dve", lambda e, dc=dc: e.tensor_tensor_scan(
                        a1.ap(), q1.ap(), zz.ap(), carA1[:, dc:dc + 1],
                        op0=ALU.add, op1=ALU.add),
                     reads=[q1, zz, carA1], writes=[a1])
            p.op("dve", lambda e, dc=dc: e.tensor_copy(carA1[:, dc:dc + 1],
                                                       a1[:, 511:512]),
                 reads=[a1, carA1], writes=[carA1])
            p.dma("dsp", a1out, a1o[:, dc, blk * 512:(blk + 1) * 512],
                  a1, a1.ap(), unordered_dst=True, sem_buf=a1)
            p.op("dve", lambda e: e.tensor_mul(mb.ap(), mb.ap(), nxb.ap()),
                 reads=[nxb, mb], writes=[mb])
            p.op("dve", lambda e: e.tensor_sub(mb.ap(), nxb.ap(), mb.ap()),
                 reads=[nxb, mb], writes=[mb])
            if chunk_start:
                p.op("dve", lambda e: e.tensor_tensor_scan(
                        c2.ap(), mb.ap(), zz.ap(), 0.0, op0=ALU.add, op1=ALU.add),
                     reads=[mb, zz], writes=[c2])
            else:
                p.op("dve", lambda e, dc=dc: e.tensor_tensor_scan(
                        c2.ap(), mb.ap(), zz.ap(), carC2[:, dc:dc + 1],
                        op0=ALU.add, op1=ALU.add),
                     reads=[mb, zz, carC2], writes=[c2])
            p.op("dve", lambda e, dc=dc: e.tensor_copy(carC2[:, dc:dc + 1],
                                                       c2[:, 511:512]),
                 reads=[c2, carC2], writes=[carC2])
            p.dma("dsp", c2out, c2o[:, dc, blk * 512:(blk + 1) * 512],
                  c2, c2.ap(), unordered_dst=True, sem_buf=c2)
            if blk % 2 == 1:
                p.op("dve", lambda e, dc=dc, col=col: e.tensor_tensor(
                        tot[:, col:col + 1],
                        carA1[:, dc:dc + 1], invp[:, col:col + 1], op=ALU.mult),
                     reads=[carA1, invp, tot], writes=[tot])
                p.op("dve", lambda e, dc=dc, col=col: e.scalar_tensor_tensor(
                        tot[:, col:col + 1], carC2[:, dc:dc + 1], 1e7,
                        tot[:, col:col + 1], op0=ALU.mult, op1=ALU.add),
                     reads=[carC2, tot], writes=[tot])
    p.dma("dsp", totout, totout.ap(), tot, tot.ap())
    _ctx.__exit__(None, None, None)
    if standalone:
        return p.finish()
    return None


def build_l3attn_g3(p=None, xsrc=None, xdst=None, pref="", a1in=None,
                    c2in=None, plin=None, lam_fn=None, tot_fn=None):
    standalone = p is None
    if standalone:
        p = P()
        xin = p.dram_in("xin", (TCORE, D), F32)
        xsrc = lambda t: (xin, xin[t * 128:(t + 1) * 128, :])
        xout_t = p.dram_out("xout", (TCORE, D), F32)
        xdst = lambda t: (xout_t, xout_t[t * 128:(t + 1) * 128, :])
        a1in = p.dram_in("a1in", (128, 8 * TCORE), F32)
        c2in = p.dram_in("c2in", (128, 8 * TCORE), F32)
        plin = p.dram_in("plin", (128, 8 * TCORE), F32)
        lamin = p.dram_in("lamin", (128, 8 * 8), F32)
        totin = p.dram_in("totin", (128, 8 * 8), F32)
        lam_fn = lambda pp, b: pp.dma("dsp", b, b.ap(), lamin, lamin.ap())
        tot_fn = lambda pp, b: pp.dma("dsp", b, b.ap(), totin, totin.ap())
    _ctx = p.stage()
    _ctx.__enter__()
    mskin = p.dram_in(pref + "mskin3", (128, 2 * 8), F32)
    identin = p.dram_in(pref + "identin3", (128, 128), F32)

    ident32 = p.sb("ident32", (128, 128), F32)
    p.dma("dsp", ident32, ident32.ap(), identin, identin.ap())
    lam = p.sb("lam", (128, 64), F32)
    lam_fn(p, lam)
    tts = p.sb("tts", (128, 64), F32)
    tot_fn(p, tts)
    msk = p.sb("msk", (128, 16), F32)
    p.dma("dsp", msk, msk.ap(), mskin, mskin.ap())
    pin = p.sb("pin", (128, 16), F32)
    invp = p.sb("invp", (128, 16), F32)
    sin = p.sb("sin", (128, 16), F32)
    tmp8 = p.sb("tmp8", (128, 8), F32)
    lamv = lam.ap().rearrange("p (dc g) -> p dc g", dc=8)
    ttsv = tts.ap().rearrange("p (dc g) -> p dc g", dc=8)
    pinv = pin.ap().rearrange("p (dc lc) -> p dc lc", dc=8)

    for lc in range(2):
        p.op("dve", lambda e, lc=lc: e.memset(pinv[:, :, lc], 1.0),
             reads=[], writes=[pin])
        p.op("dve", lambda e, lc=lc: e.memset(
                sin.ap().rearrange("p (dc l) -> p dc l", dc=8)[:, :, lc], 0.0),
             reads=[], writes=[sin])
        for g in range(8):
            p.op("dve", lambda e, lc=lc, g=g: e.scalar_tensor_tensor(
                    tmp8.ap(), lamv[:, :, g], msk[:, lc * 8 + g:lc * 8 + g + 1],
                    tmp8.ap(), op0=ALU.mult, op1=ALU.bypass),
                 reads=[lam, msk], writes=[tmp8])
            p.op("dve", lambda e: e.tensor_scalar_add(tmp8.ap(), tmp8.ap(), 1.0),
                 reads=[tmp8], writes=[tmp8])
            p.op("dve", lambda e, lc=lc, g=g: e.tensor_scalar(
                    tmp8.ap(), tmp8.ap(), msk[:, lc * 8 + g:lc * 8 + g + 1], None,
                    op0=ALU.subtract),
                 reads=[tmp8, msk], writes=[tmp8])
            p.op("dve", lambda e, lc=lc: e.tensor_mul(pinv[:, :, lc],
                                                      pinv[:, :, lc], tmp8.ap()),
                 reads=[tmp8, pin], writes=[pin])
            p.op("dve", lambda e, lc=lc, g=g: e.scalar_tensor_tensor(
                    tmp8.ap(), ttsv[:, :, g], msk[:, lc * 8 + g:lc * 8 + g + 1],
                    tmp8.ap(), op0=ALU.mult, op1=ALU.bypass),
                 reads=[tts, msk], writes=[tmp8])
            p.op("dve", lambda e, lc=lc: e.tensor_add(
                    sin.ap().rearrange("p (dc l) -> p dc l", dc=8)[:, :, lc],
                    sin.ap().rearrange("p (dc l) -> p dc l", dc=8)[:, :, lc],
                    tmp8.ap()),
                 reads=[tmp8, sin], writes=[sin])
    p.op("dve", lambda e: e.tensor_scalar_max(invp.ap(), pin.ap(), 1e-30),
         reads=[pin], writes=[invp])
    p.op("dve", lambda e: e.reciprocal(invp.ap(), invp.ap()),
         reads=[invp], writes=[invp])

    a1v = a1in.ap().rearrange("p (dc t) -> p dc t", dc=8)
    c2v = c2in.ap().rearrange("p (dc t) -> p dc t", dc=8)
    plv = plin.ap().rearrange("p (dc t) -> p dc t", dc=8)
    a1b = p.sb("a1b", (128, 512), F32)
    c2b = p.sb("c2b", (128, 512), F32)
    plb = p.sb("plb", (128, 512), F32)
    hb = p.sb("hb", (128, 8 * 512), F32)
    hT = p.sb("hT", (128, D), F32)
    xt = [p.sb("xt0", (128, D), F32)] * 2
    scr = p.sb("scr", (128, D), F32)
    st = p.sb("st", (128, 16), F32)
    ob = p.sb("ob", (128, D), F32)
    tr32_ps = p.psum("tr32_ps", (128, 512), F32)
    hbv = hb.ap().rearrange("p (dc t) -> p dc t", dc=8)

    for blk in range(4):
        lc = blk // 2
        for dc in range(8):
            col = dc * 2 + lc
            p.dma("dsp", a1b, a1b.ap(), a1in, a1v[:, dc, blk * 512:(blk + 1) * 512])
            p.dma("dsp", c2b, c2b.ap(), c2in, c2v[:, dc, blk * 512:(blk + 1) * 512])
            p.dma("dsp", plb, plb.ap(), plin, plv[:, dc, blk * 512:(blk + 1) * 512])
            p.op("dve", lambda e, col=col: e.tensor_scalar(
                    a1b.ap(), a1b.ap(), invp[:, col:col + 1], sin[:, col:col + 1],
                    op0=ALU.mult, op1=ALU.add),
                 reads=[a1b, invp, sin], writes=[a1b])
            p.op("dve", lambda e: e.scalar_tensor_tensor(
                    a1b.ap(), c2b.ap(), 1e7, a1b.ap(), op0=ALU.mult, op1=ALU.add),
                 reads=[c2b, a1b], writes=[a1b])
            p.op("dve", lambda e, col=col: e.tensor_scalar(
                    plb.ap(), plb.ap(), pin[:, col:col + 1], 1e-7,
                    op0=ALU.mult, op1=ALU.max),
                 reads=[plb, pin], writes=[plb])
            p.op("dve", lambda e, dc=dc: e.tensor_tensor(
                    hbv[:, dc, :], plb.ap(), a1b.ap(), op=ALU.mult),
                 reads=[plb, a1b, hb], writes=[hb])
        for tc in range(4):
            t = blk * 4 + tc
            for dc in range(8):
                p.op("pe", lambda e, dc=dc, tc=tc: e.transpose(
                        tr32_ps.ap_[:, (dc % 4) * 128:(dc % 4 + 1) * 128],
                        hbv[:, dc, tc * 128:(tc + 1) * 128], ident32.ap()),
                     reads=[hb, ident32], writes=[tr32_ps])
                if dc % 4 == 3:
                    half = dc // 4
                    p.op("dve", lambda e, half=half: e.tensor_copy(
                            hT.ap()[:, half * 512:(half + 1) * 512], tr32_ps.ap()),
                         reads=[tr32_ps], writes=[hT])
            xb = xt[t % 2]
            p.dma("dsp", xb, xb.ap(), *xsrc(t))
            mean2, rstd2 = emit_ln_stats(p, hT, hT.ap(), scr, st, 8)
            p.op("dve", lambda e: e.tensor_scalar(
                    scr.ap(), hT.ap(), st[:, 8:9], st[:, 11:12],
                    op0=ALU.subtract, op1=ALU.mult),
                 reads=[hT, st], writes=[scr])
            p.op("dve", lambda e: e.tensor_add(ob.ap(), xb.ap(), scr.ap()),
                 reads=[xb, scr], writes=[ob])
            dbuf, dap = xdst(t)
            p.dma("dsp", dbuf, dap, ob, ob.ap(),
                  unordered_dst=True, sem_buf=ob)
    _ctx.__exit__(None, None, None)
    if standalone:
        return p.finish()
    return None




def _g_of(g):
    return (g, 0) if g < 4 else (7 - g, 1)


def build_l3attn_f(p=None, xsrc=None, xdst=None, pref=""):
    nx_d = p.dram_tmp(pref + "nx_d", (128, 8 * TCORE), F32)
    pl_d = p.dram_tmp(pref + "pl_d", (128, 8 * TCORE), F32)
    lam_c = p.dram_tmp(pref + "lam_c", (128, 16), F32)
    lam_ag = p.dram_tmp(pref + "lam_ag", (4 * 128, 16), F32)
    a1_d = p.dram_tmp(pref + "a1_d", (128, 8 * TCORE), F32)
    c2_d = p.dram_tmp(pref + "c2_d", (128, 8 * TCORE), F32)
    tot_c = p.dram_tmp(pref + "tot_c", (128, 16), F32)
    tot_ag = p.dram_tmp(pref + "tot_ag", (4 * 128, 16), F32)
    groups = [[0, 1, 2, 3], [4, 5, 6, 7]]

    build_l3attn_g1(p=p, xsrc=xsrc, pref=pref, nxout=nx_d, plout=pl_d,
                    lamout=lam_c)
    p.cc("AllGather", ALU.bypass, groups, lam_c, lam_c.ap(), lam_ag, lam_ag.ap())

    def mk_loader(ag_buf):
        def load(pp, sbuf_buf):
            v = sbuf_buf.ap().rearrange("p (dc g) -> p dc g", dc=8)
            for g in range(8):
                r, lc = _g_of(g)
                pp.dma("dsp", sbuf_buf, v[:, :, g:g + 1],
                       ag_buf, ag_buf.ap()[r * 128:(r + 1) * 128, lc::2]
                       .rearrange("p (dc o) -> p dc o", o=1),
                       allow_slow_non_contiguous=True)
        return load

    build_l3attn_g2(p=p, pref=pref, nxin=nx_d, plin=pl_d,
                    lam_fn=mk_loader(lam_ag), a1out=a1_d, c2out=c2_d,
                    totout=tot_c)
    p.cc("AllGather", ALU.bypass, groups, tot_c, tot_c.ap(), tot_ag, tot_ag.ap())
    build_l3attn_g3(p=p, xsrc=xsrc, xdst=xdst, pref=pref, a1in=a1_d,
                    c2in=c2_d, plin=pl_d, lam_fn=mk_loader(lam_ag),
                    tot_fn=mk_loader(tot_ag))
    return None


def prep_l3attn_f(inputs, xflat, c, pref=""):
    ident = np.eye(128, dtype=np.float32)
    ap_fm = np.ascontiguousarray(
        inputs["L3_attn_a_param"].astype(np.float32).reshape(8, 128).T)
    b, j = divmod(c, 4)
    m = np.zeros((2, 8), np.float32)
    m[0, :j] = 1.0
    m[1, :7 - j] = 1.0
    mm = np.broadcast_to(m.reshape(1, 16), (128, 16)).copy()
    d = {"igwin": inputs["L3_attn_ig_w"].astype(np.float32),
         "igbin": np.ascontiguousarray(inputs["L3_attn_ig_b"].astype(np.float32).T),
         "agwin": inputs["L3_attn_ag_w"].astype(np.float32),
         "agbin": np.ascontiguousarray(inputs["L3_attn_ag_b"].astype(np.float32).T),
         "apin": ap_fm, "identin": ident, "identin3": ident,
         "mskin": mm, "mskin3": mm}
    return {pref + k: v for k, v in d.items()}


def run_l3attn(inputs, xflat):
    for k, b in (("l3attn_g1", build_l3attn_g1), ("l3attn_g2", build_l3attn_g2),
                 ("l3attn_g3", build_l3attn_g3)):
        if k not in _nc_cache:
            _nc_cache[k] = b()
    ident = np.eye(128, dtype=np.float32)
    ap_fm = np.ascontiguousarray(
        inputs["L3_attn_a_param"].astype(np.float32).reshape(8, 128).T)
    in1 = [{"xin": slice_tokens(xflat, c),
            "igwin": inputs["L3_attn_ig_w"].astype(np.float32),
            "igbin": np.ascontiguousarray(inputs["L3_attn_ig_b"].astype(np.float32).T),
            "agwin": inputs["L3_attn_ag_w"].astype(np.float32),
            "agbin": np.ascontiguousarray(inputs["L3_attn_ag_b"].astype(np.float32).T),
            "apin": ap_fm, "identin": ident} for c in range(NCORES)]
    r1 = run_bass_kernel_spmd(_nc_cache["l3attn_g1"], in1,
                              core_ids=list(range(NCORES)))
    def gather_cols(res_key, ncol):
        out = {}
        for b in range(2):
            arr = np.empty((128, 8, 8), np.float32)
            for g in range(8):
                j = g if g < 4 else 7 - g
                col = 0 if g < 4 else 1
                arr[:, :, g] = res_key[4 * b + j].reshape(128, 8, ncol)[:, :, col]
            out[b] = arr.reshape(128, 64)
        return out
    lam_g = gather_cols([r1.results[c]["lamout"] for c in range(NCORES)], 2)
    def mk_msk(c):
        b, j = divmod(c, 4)
        m = np.zeros((2, 8), np.float32)
        m[0, :j] = 1.0
        m[1, :7 - j] = 1.0
        return np.broadcast_to(m.reshape(1, 16), (128, 16)).copy()
    in2 = []
    for c in range(NCORES):
        b = c // 4
        in2.append({"nxin": r1.results[c]["nxout"],
                    "plin": r1.results[c]["plout"],
                    "lamin": lam_g[b], "mskin": mk_msk(c)})
    r2 = run_bass_kernel_spmd(_nc_cache["l3attn_g2"], in2,
                              core_ids=list(range(NCORES)))
    tot_g = gather_cols([r2.results[c]["totout"] for c in range(NCORES)], 2)
    in3 = []
    for c in range(NCORES):
        b = c // 4
        in3.append({"xin": slice_tokens(xflat, c),
                    "a1in": r2.results[c]["a1out"],
                    "c2in": r2.results[c]["c2out"],
                    "plin": r1.results[c]["plout"],
                    "lamin": lam_g[b], "totin": tot_g[b],
                    "mskin": mk_msk(c), "identin": ident})
    r3 = run_bass_kernel_spmd(_nc_cache["l3attn_g3"], in3,
                              core_ids=list(range(NCORES)))
    return unslice_tokens([r3.results[c]["xout"] for c in range(NCORES)])





def _mk_xfn(b):
    return lambda t: (b, b[t * 128:(t + 1) * 128, :])


def build_chain(stage_fns):
    p = P()
    xin = p.dram_in("xin", (TCORE, D), F32)
    xout = p.dram_out("xout", (TCORE, D), F32)
    mids = [p.dram_tmp("xmid0", (TCORE, D), F32),
            p.dram_tmp("xmid1", (TCORE, D), F32)]
    cur = xin
    for i, (fn, pref) in enumerate(stage_fns):
        last = (i == len(stage_fns) - 1)
        nxt = xout if last else mids[i % 2]
        fn(p=p, xsrc=_mk_xfn(cur), xdst=_mk_xfn(nxt), pref=pref)
        cur = nxt
    return p.finish()


def run_chain(inputs, xflat, key, stage_fns, preps):
    if key not in _nc_cache:
        _nc_cache[key] = build_chain(stage_fns)
    in_maps = []
    for c in range(NCORES):
        d = {"xin": slice_tokens(xflat, c)}
        for prep, pref in preps:
            d.update(prep(inputs, xflat, c, pref=pref))
        in_maps.append(d)
    res = run_bass_kernel_spmd(_nc_cache[key], in_maps,
                               core_ids=list(range(NCORES)))
    return unslice_tokens([res.results[c]["xout"] for c in range(NCORES)])





def _attn_group_srcs(kT_ag, v_ag):
    def k_src(kt, hg):
        g = kt // 8
        r, half = _g_of(g)
        return (kT_ag, kT_ag.ap()[r * D + hg * 256: r * D + (hg + 1) * 256,
                                  half * 1024 + (kt % 8) * 128:
                                  half * 1024 + (kt % 8 + 1) * 128])

    def v_src(kt, hg):
        g = kt // 8
        r, half = _g_of(g)
        row = r * TCORE + half * 1024 + (kt % 8) * 128
        return (v_ag, v_ag.ap()[row:row + 128, hg * 256:(hg + 1) * 256])
    return k_src, v_src


def _mk_kmax_fn(p, kmax_ag):
    def kmax_fn(pp, kmax):
        pp.dma("dsp", kmax, kmax[:, 0:4],
               kmax_ag, kmax_ag.ap().rearrange("(r h) o -> h (r o)", r=4),
               allow_slow_non_contiguous=True)
        pp.op("dve", lambda e: e.reduce_max(kmax[:, 4:5], kmax[:, 0:4],
                                            axis=AX.X),
              reads=[kmax], writes=[kmax])
    return kmax_fn


GROUPS4 = [[0, 1, 2, 3], [4, 5, 6, 7]]


def build_l2attn_f(p=None, xsrc=None, xdst=None, pref=""):
    qT_c = p.dram_tmp(pref + "qT_c", (D, TCORE), BF16)
    kT_c = p.dram_tmp(pref + "kT_c", (D, TCORE), BF16)
    v_c = p.dram_tmp(pref + "v_c", (TCORE, D), BF16)
    qnT_c = p.dram_tmp(pref + "qnT_c", (16, TCORE), F32)
    kmax_c = p.dram_tmp(pref + "kmax_c", (16, 1), F32)
    kT_ag = p.dram_tmp(pref + "kT_ag", (4 * D, TCORE), BF16)
    v_ag = p.dram_tmp(pref + "v_ag", (4 * TCORE, D), BF16)
    kmax_ag = p.dram_tmp(pref + "kmax_ag", (64, 1), F32)
    build_attn_kv(p=p, xsrc=xsrc, pref=pref, qTout=qT_c, kTout=kT_c,
                  vout=v_c, qnTout=qnT_c, kmaxout=kmax_c)
    p.cc("AllGather", ALU.bypass, GROUPS4, kT_c, kT_c.ap(), kT_ag, kT_ag.ap())
    p.cc("AllGather", ALU.bypass, GROUPS4, v_c, v_c.ap(), v_ag, v_ag.ap())
    p.cc("AllGather", ALU.bypass, GROUPS4, kmax_c, kmax_c.ap(),
         kmax_ag, kmax_ag.ap())
    k_src, v_src = _attn_group_srcs(kT_ag, v_ag)
    build_attn_core(False, p=p, xsrc=xsrc, xdst=xdst, pref=pref,
                    qT_src=(qT_c, qT_c.ap()), k_src=k_src, v_src=v_src,
                    qnT_src=(qnT_c, qnT_c.ap()), kmax_fn=_mk_kmax_fn(p, kmax_ag))
    return None


def prep_l2attn_f(inputs, xflat, c, pref=""):
    d = {"wqin": np.ascontiguousarray(inputs["L2_attn_Wq"].T.astype(np.float32)),
         "wkin": np.ascontiguousarray(inputs["L2_attn_Wk"].T.astype(np.float32)),
         "wvin": np.ascontiguousarray(inputs["L2_attn_Wv"].T.astype(np.float32)),
         "bqin": _rep(inputs["L2_attn_bq"]), "bkin": _rep(inputs["L2_attn_bk"]),
         "bvin": _rep(inputs["L2_attn_bv"]),
         "identin": np.eye(128, dtype=np.float32),
         "augin": np.ones((64, 512), np.float32),
         "woin": np.ascontiguousarray(inputs["L2_attn_Wo"].T.astype(np.float32)),
         "boin": _rep(inputs["L2_attn_bo"])}
    return {pref + k: v for k, v in d.items()}


def build_l1attn_f(p=None, xsrc=None, xdst=None, pref=""):
    qT_c = p.dram_tmp(pref + "qT_c", (D, TCORE), BF16)
    kT_c = p.dram_tmp(pref + "kT_c", (D, TCORE), BF16)
    v_c = p.dram_tmp(pref + "v_c", (TCORE, D), BF16)
    qnT_c = p.dram_tmp(pref + "qnT_c", (16, TCORE), F32)
    kmax_c = p.dram_tmp(pref + "kmax_c", (16, 1), F32)
    x_c = p.dram_tmp(pref + "x_c", (TCORE, D), F32)
    qT_ag = p.dram_tmp(pref + "qT_ag", (4 * D, TCORE), BF16)
    kT_ag = p.dram_tmp(pref + "kT_ag", (4 * D, TCORE), BF16)
    v_ag = p.dram_tmp(pref + "v_ag", (4 * TCORE, D), BF16)
    qnT_ag = p.dram_tmp(pref + "qnT_ag", (64, TCORE), F32)
    kmax_ag = p.dram_tmp(pref + "kmax_ag", (64, 1), F32)
    x_ag = p.dram_tmp(pref + "x_ag", (4 * TCORE, D), F32)
    build_attn_kv(p=p, xsrc=xsrc, pref=pref, qTout=qT_c, kTout=kT_c,
                  vout=v_c, qnTout=qnT_c, kmaxout=kmax_c, xcopy=x_c)
    for cin, cout in ((qT_c, qT_ag), (kT_c, kT_ag), (v_c, v_ag),
                      (qnT_c, qnT_ag), (kmax_c, kmax_ag), (x_c, x_ag)):
        p.cc("AllGather", ALU.bypass, GROUPS4, cin, cin.ap(), cout, cout.ap())

    rhin = p.dram_in(pref + "rhin", (1, 8), I32)
    rhs_ = p.sb("rhs_", (1, 8), I32)
    p.dma("dsp", rhs_, rhs_.ap(), rhin, rhin.ap())
    regs_sp = [p.sval("sp", rhs_, rhs_[0:1, i:i + 1]) for i in range(4)]
    regs_pool = [p.sval("pool", rhs_, rhs_[0:1, i:i + 1]) for i in range(8)]
    regs_act = [p.sval("act", rhs_, rhs_[0:1, i:i + 1]) for i in range(4)]

    qT_my = p.dram_tmp(pref + "qT_my", (D, TCORE), BF16)
    kT_my = p.dram_tmp(pref + "kT_my", (D, TCORE), BF16)
    v_my = p.dram_tmp(pref + "v_my", (TCORE, D), BF16)
    x_my = p.dram_tmp(pref + "x_my", (TCORE, D), F32)
    qnT_my = p.dram_tmp(pref + "qnT_my", (16, TCORE), F32)

    for i in range(2):
        rk_sp, h_sp = regs_sp[2 * i], regs_sp[2 * i + 1]
        rk_a, h_a = regs_act[2 * i], regs_act[2 * i + 1]
        rk_p, h_p = regs_pool[2 * i], regs_pool[2 * i + 1]
        cs = slice(i * 1024, (i + 1) * 1024)
        for tag, tmy, tag_buf in (("q", qT_my, qT_ag), ("k", kT_my, kT_ag)):
            v4 = tag_buf.ap().rearrange("(rk ch) (h t) -> rk ch h t",
                                        rk=4, h=2)
            p.dma("dsp", tmy, tmy.ap()[:, cs].unsqueeze(0).unsqueeze(2),
                  tag_buf, v4[bass.ts(rk_sp, 1), :, bass.ts(h_sp, 1), :])
        for tmy, tag_buf in ((v_my, v_ag), (x_my, x_ag)):
            v4 = tag_buf.ap().rearrange("(rk h t) d -> rk h t d", rk=4, h=2)
            p.dma("dact", tmy, tmy.ap()[cs, :].unsqueeze(0).unsqueeze(0),
                  tag_buf, v4[bass.ts(rk_a, 1), bass.ts(h_a, 1), :, :])
        v4 = qnT_ag.ap().rearrange("(rk hd) (h t) -> rk hd h t", rk=4, h=2)
        p.dma("dpool", qnT_my, qnT_my.ap()[:, cs].unsqueeze(0).unsqueeze(2),
              qnT_ag, v4[bass.ts(rk_p, 1), :, bass.ts(h_p, 1), :])

    xp_c = p.dram_tmp(pref + "xp_c", (TCORE, D), F32)
    xp_ag = p.dram_tmp(pref + "xp_ag", (4 * TCORE, D), F32)
    k_src_g, v_src_g = _attn_group_srcs(kT_ag, v_ag)

    def k_src_my(ktl, hg):
        return (kT_my, kT_my.ap()[hg * 256:(hg + 1) * 256,
                                  ktl * 128:(ktl + 1) * 128])

    def v_src_my(ktl, hg):
        return (v_my, v_my.ap()[ktl * 128:(ktl + 1) * 128,
                                hg * 256:(hg + 1) * 256])

    def plan_fn(qb):
        plan = [(k_src_g, v_src_g, kt, kt, None) for kt in range(48)]
        for ktl in range(4 * qb + 4):
            mask_i = (ktl - 4 * qb) if ktl >= 4 * qb else None
            plan.append((k_src_my, v_src_my, ktl, 48 + ktl, mask_i))
        return plan

    build_attn_core(True, p=p,
                    xsrc=_mk_xfn(x_my), xdst=_mk_xfn(xp_c), pref=pref,
                    qT_src=(qT_my, qT_my.ap()), k_src=None, v_src=None,
                    qnT_src=(qnT_my, qnT_my.ap()),
                    kmax_fn=_mk_kmax_fn(p, kmax_ag), plan_fn=plan_fn)
    p.cc("AllGather", ALU.bypass, GROUPS4, xp_c, xp_c.ap(), xp_ag, xp_ag.ap())
    xnext = xdst(0)[0]
    for i in range(2):
        rk_p, h_p = regs_pool[4 + 2 * i], regs_pool[4 + 2 * i + 1]
        v4 = xp_ag.ap().rearrange("(rk h t) d -> rk h t d", rk=4, h=2)
        p.dma("dpool", xnext,
              xnext.ap_[i * 1024:(i + 1) * 1024, :].unsqueeze(0).unsqueeze(0),
              xp_ag, v4[bass.ts(rk_p, 1), bass.ts(h_p, 1), :, :])
    return None


def prep_l1attn_f(inputs, xflat, c, pref=""):
    b, j = divmod(c, 4)
    rh = []
    for g in (2 * j, 2 * j + 1):
        r, h = _g_of(g)
        rh += [r, h]
    for g in (j, 7 - j):
        rh += [g // 2, g % 2]
    aug = np.ones((64, 512), np.float32)
    npast = (2048 * j) // 128
    aug[npast:48] = 30.0
    d = {"wqin": np.ascontiguousarray(inputs["L1_attn_wq"].T.astype(np.float32)),
         "wkin": np.ascontiguousarray(inputs["L1_attn_wk"].T.astype(np.float32)),
         "wvin": np.ascontiguousarray(inputs["L1_attn_wv"].T.astype(np.float32)),
         "bqin": np.zeros((128, D), np.float32),
         "bkin": np.zeros((128, D), np.float32),
         "bvin": np.zeros((128, D), np.float32),
         "identin": np.eye(128, dtype=np.float32),
         "augin": aug,
         "woin": np.ascontiguousarray(inputs["L1_attn_wo"].T.astype(np.float32)),
         "boin": np.zeros((128, D), np.float32),
         "rhin": np.array([rh], np.int32)}
    return {pref + k: v for k, v in d.items()}



def kernel(**inputs):
    xflat = np.ascontiguousarray(
        np.asarray(inputs["L0_attn_y_de"], dtype=np.float32).reshape(16384, D))
    xflat = run_chain(
        inputs, xflat, "c_all",
        [(build_l0attn, "a_"), (build_l0ffn, "b_"), (build_l1attn_f, "g_"),
         (build_l1ffn_f, "c_"), (build_l2attn_f, "h_"), (build_l2ffn, "d_"),
         (build_l3attn_f, "e_"), (build_l3ffn_f, "f_")],
        [(prep_l0attn, "a_"), (prep_l0ffn, "b_"), (prep_l1attn_f, "g_"),
         (prep_l1ffn_f, "c_"), (prep_l2attn_f, "h_"), (prep_l2ffn, "d_"),
         (prep_l3attn_f, "e_"), (prep_l3ffn_f, "f_")])
    return xflat.reshape(2, 8192, D).astype(np.float32)
